# version 17
# baseline (speedup 1.0000x reference)
import contextlib
import numpy as np
import ml_dtypes

import concourse.bass as bass
import concourse.mybir as mybir
from concourse.bass_utils import run_bass_kernel_spmd

F32 = mybir.dt.float32
BF16 = mybir.dt.bfloat16
AF = mybir.ActivationFunctionType
ALU = mybir.AluOpType
BF = ml_dtypes.bfloat16

D = 2048
H = 16
NFC = 16
SC = 1024
CH = 256
OWN = [[0, 3, 4, 7], [1, 2, 5, 6]]
GLOB = OWN[0] + OWN[1]
LIVE = [sorted(g for g in range(8)
               if GLOB[g] <= OWN[0][s] or GLOB[g] <= OWN[1][s]) for s in range(4)]
EXP_SHIFT = 20.0
SCALE2 = float(1.0 / np.sqrt(np.float32(128.0)))
QSTD0, KSTD0, V0, LOW0 = 0, 1920, 3840, 5888
W16 = [(i * 128, 128) for i in range(16)]
WA_RANGES = ([(KSTD0 + 120 * i, 120) for i in range(16)] +
             [(LOW0, 256)] +
             [(V0 + 128 * i, 128) for i in range(16)] +
             [(QSTD0 + 120 * i, 120) for i in range(16)])
WA_OFF = np.cumsum([0] + [16 * w for _, w in WA_RANGES]).tolist()
GROUPS = [[0, 1], [2, 3], [4, 5], [6, 7]]


def _own_idx(parity):
    return np.concatenate([np.arange(CH) + c * CH for c in OWN[parity]])


def _img_panels(M, ranges):
    pieces = []
    for stt, w in ranges:
        blk = np.ascontiguousarray(M[:, stt:stt + w])
        pieces.append(blk.reshape(16, 128, w).transpose(1, 0, 2).reshape(128, 16 * w))
    return np.ascontiguousarray(np.concatenate(pieces, axis=1))


def _masks_for(parity):
    m = np.zeros((4, 2, 2, 128, 256), dtype=BF)
    for s in range(4):
        for j, g in enumerate((s, s + 4)):
            kg = GLOB[g] * CH + np.arange(CH)
            qg = OWN[parity][s] * CH + np.arange(CH)
            full = (kg[:, None] <= qg[None, :]).astype(np.float32)
            m[s, j, 0] = full[:128]
            m[s, j, 1] = full[128:]
    return m


class Views:

    def __init__(self, views):
        self.views = views

    def __getitem__(self, key):
        if isinstance(key, tuple):
            v = self.views[key[0]]
            rest = key[1:]
            if not rest:
                return v
            return v[rest if len(rest) > 1 else rest[0]]
        return self.views[key]


class Plan:
    def __init__(self):
        self.ops = []
        self.cnt = {}
        self.pending = {}
        self.last = {}

    def sem(self, n):
        self.cnt.setdefault(n, 0)

    def em(self, eng, fn, waits=(), inc=None, amt=1):
        w = list(self.pending.pop(eng, ())) + [x for x in waits if x is not None]
        self.ops.append((eng, fn, w, inc, amt))
        if inc is not None:
            self.cnt[inc] = self.cnt.get(inc, 0) + (amt if amt else 1)
            return (inc, self.cnt[inc])
        return None

    def now(self, s):
        return (s, self.cnt.get(s, 0))

    def barrier(self, exclude=("cc1", "cc2", "cc3")):
        snap = [(s, c) for s, c in self.cnt.items() if c > 0 and s not in exclude]
        for e in ("pe", "dve", "act", "gp", "sync"):
            self.pending[e] = snap

    def use(self, key, *ms):
        cur = self.last.setdefault(key, {})
        for m in ms:
            if m is not None:
                s, v = m
                cur[s] = max(cur.get(s, 0), v)

    def takewaits(self, key):
        d = self.last.pop(key, {})
        return list(d.items())


PH1_LIMIT = 16
PH0_PATTERNS = True
PH1_BIAS = True
PH1_EVAC = True
PH7_SERIAL = False

PHASE_MARKS = []


def build_program(stop_phase=99, debug_taps=()):
    PHASE_MARKS.clear()
    nc = bass.Bass("TRN2", target_bir_lowering=False, debug=False)
    p = Plan()
    SEM_NAMES = ("pe", "dve", "act", "dg", "ds", "cc1", "cc2", "cc3",
                 "dgw0", "dgw1", "dgw2", "dgh0", "dgh1", "dgh2", "dgb", "dkv",
                 "dgr0", "dgr1", "dgr2", "dgr3",
                 "dsx0", "dsx1", "dsx2", "dsx3", "dsz0", "dsz1", "dsz2", "dsr", "dsra",
                 "dst0", "dst1", "dsk0", "dsk1", "dckv", "dkf", "dvv",
                 "dgl", "dskt", "dsm", "dsc", "gpc")
    for n in SEM_NAMES:
        p.sem(n)

    x_img = nc.dram_tensor("x_img", [128, NFC * SC], F32, kind="ExternalInput").ap()
    wq_i = nc.dram_tensor("wq_i", [128, 8 * 4096], F32, kind="ExternalInput").ap()
    wk_i = nc.dram_tensor("wk_i", [128, 16 * 2176], F32, kind="ExternalInput").ap()
    wv_i = nc.dram_tensor("wv_i", [128, 16 * 2176], F32, kind="ExternalInput").ap()
    wo_i = nc.dram_tensor("wo_i", [128, 8 * 4096], F32, kind="ExternalInput").ap()
    wp_i = nc.dram_tensor("wp_i", [128, 8 * 4096], F32, kind="ExternalInput").ap()
    wa_i = nc.dram_tensor("wa_i", [128, WA_OFF[-1]], F32, kind="ExternalInput").ap()
    bq_c = nc.dram_tensor("bq_c", [128, 16], F32, kind="ExternalInput").ap()
    bo_c = nc.dram_tensor("bo_c", [128, 16], F32, kind="ExternalInput").ap()
    bk_r = nc.dram_tensor("bk_r", [1, D], F32, kind="ExternalInput").ap()
    bv_r = nc.dram_tensor("bv_r", [1, D], F32, kind="ExternalInput").ap()
    wsr_i = nc.dram_tensor("wsr_i", [1, 32], F32, kind="ExternalInput").ap()
    r01_i = nc.dram_tensor("r01_i", [16, 128], F32, kind="ExternalInput").ap()
    ident_i = nc.dram_tensor("ident_i", [128, 128], BF16, kind="ExternalInput").ap()
    masks_i = nc.dram_tensor("masks_i", [128, 16 * 256], BF16, kind="ExternalInput").ap()
    out_t = nc.dram_tensor("out_fm", [D, SC], F32, kind="ExternalOutput").ap()

    kv_in = nc.dram_tensor("kv_in", [128, H * 129], F32)
    kv_out = nc.dram_tensor("kv_out", [128, H * 129], F32)
    kf_in = nc.dram_tensor("kf_in", [H * 128, SC], BF16)
    kf_out = nc.dram_tensor("kf_out", [2 * H * 128, SC], BF16)
    vv_in = nc.dram_tensor("vv_in", [8 * 128, H * 129], BF16)
    vv_out = nc.dram_tensor("vv_out", [16 * 128, H * 129], BF16)
    zbuf = nc.dram_tensor("zbuf", [D, SC], F32)
    x1buf = nc.dram_tensor("x1buf", [D, SC], F32)
    rowbuf = nc.dram_tensor("rowbuf", [48, 512], F32)

    dbg = {}
    for name in debug_taps:
        shp = {"kv": [128, H * 129], "x1": [D, SC], "z": [D, SC],
               "kf": [2 * H * 128, SC], "vv": [16 * 128, H * 129],
               "attn2": [128, NFC * SC], "qf": [128, SC]}[name]
        dt = BF16 if name in ("kf", "vv", "attn2", "qf") else F32
        dbg[name] = nc.dram_tensor("dbg_" + name, shp, dt, kind="ExternalOutput").ap()

    stck = contextlib.ExitStack()
    sem = {n: stck.enter_context(nc.semaphore(n)) for n in SEM_NAMES}

    A = nc.alloc_sbuf_tensor

    def vsplit(ap, n, w):
        return Views([ap[:, i * w:(i + 1) * w] for i in range(n)])

    arena1 = A("arena1", [128, NFC * SC], BF16)
    qbuf_t = A("qbuf", [128, 2 * 2048], BF16)
    qbuf = vsplit(qbuf_t[:, :], 2, 2048)
    cdk = A("cdk", [128, H * SC], BF16)
    cdv = A("cdv", [128, 8 * H * 129], BF16)
    wbuf_t = A("wbuf", [128, 2 * 2176], BF16)
    wbuf = vsplit(wbuf_t[:, :], 2, 2176)
    xkv_t = A("xkv", [128, 2 * 4112], BF16)
    xkv = vsplit(xkv_t[:, :], 2, 4112)
    wpan = [wbuf[0], wbuf[1], xkv[0]]

    def wkey(si):
        return ("wbuf", si) if si < 2 else ("xkv", 0)
    kvbf_u = A("kvbf_u", [128, 2064], BF16)
    kvbf = kvbf_u[:, :]
    selq_fm = kvbf_u[:, 0:1024]
    selk_fm = kvbf_u[:, 1024:2048]
    kvs = Views([kvbf_u[:, i * 512:(i + 1) * 512] for i in range(4)])
    u1 = A("u1", [128, 2048], F32)
    trow = vsplit(u1[:, :], 2, 512)
    trow8 = vsplit(u1[:, :], 8, 256)
    x0t = vsplit(u1[:, :], 4, 512)
    mask_sb = u1[:, :].bitcast(BF16)
    u2 = A("u2", [128, 1536], F32)
    zt = vsplit(u2[:, :], 3, 512)
    pbufP = vsplit(u2[:, :].bitcast(BF16), 4, 256)
    u3 = A("u3", [128, 2048], F32)
    rbc = vsplit(u3[:, :], 4, 512)
    abc = Views([vsplit(u3[:, 0:1024], 2, 512), vsplit(u3[:, 1024:2048], 2, 512)])
    pbufW = vsplit(u3[:, :].bitcast(BF16), 8, 512)
    u4 = A("u4", [128, 1024], F32)
    u4b = u4[:, :].bitcast(BF16)
    zbft = vsplit(u4b, 2, 512)
    zsqt = Views([u4b[:, 1024 + i * 512:1024 + (i + 1) * 512] for i in range(2)])
    selq_tm = Views([u4b[:, i * 128:(i + 1) * 128] for i in range(8)])
    selk_tm = Views([u4b[:, 1024 + i * 128:1024 + (i + 1) * 128] for i in range(8)])
    u5 = A("u5", [128, 392], F32)
    kvst = vsplit(u5[:, 0:258], 2, 129)
    seltmp = vsplit(u5[:, 0:256], 2, 128)
    att_tm = Views([u5[:, 256 + i * 64:256 + (i + 1) * 64].bitcast(BF16) for i in range(2)])
    rec_t = Views([u5[:, 384 + i:385 + i] for i in range(2)])
    atm4 = Views([u5[:, 64 * i:64 * (i + 1)].bitcast(BF16) for i in range(4)])
    rec4 = Views([u5[:, 384 + i:385 + i] for i in range(4)])
    rows = A("rows", [1, 1024], F32)
    rows2 = A("rows2", [1, 1024], F32)
    patrow = rows2[0:1, 0:512]
    rows_bf = A("rows_bf", [1, 128], BF16)
    onesb_r = A("onesb_r", [1, 128], BF16)
    onesb_c = A("onesb_c", [128, 1], BF16)
    onesf_r = A("onesf_r", [1, 128], F32)
    eps_t = A("eps_t", [1, 1], F32)
    eps5_t = A("eps5_t", [1, 1], F32)
    nshift = A("nshift", [128, 1], F32)
    wrow = A("wrow", [1, 32], F32)
    wclip = A("wclip", [1, 32], F32)
    urow = A("urow", [1, 64], F32)
    patcol = A("patcol", [16, 4], F32)
    pat_t = A("pat", [128, 512], BF16)
    pat = vsplit(pat_t[:, :], 4, 128)
    wstd_c = A("wstd_c", [128, 16], F32)
    bqc_sb = A("bqc_sb", [128, 16], F32)
    boc_sb = A("boc_sb", [128, 16], F32)
    r01_sb = A("r01_sb", [16, 128], F32)
    ident = A("ident", [128, 128], BF16)

    pb = [nc.alloc_psum_tensor(f"pb{i}", [128, 512], F32) for i in range(8)]
    pb_bf = [b[:, :].bitcast(BF16) for b in pb]

    def mm(out_ap, lhsT, rhs, start, stop, waits=(), inc=None):
        return p.em("pe", lambda t: t.matmul(out_ap, lhsT, rhs, start=start, stop=stop),
                    waits=waits, inc=inc)

    def trans(out_ap, in_ap, waits=(), inc="pe"):
        return p.em("pe", lambda t: t.transpose(out_ap, in_ap, ident[:, :]),
                    waits=waits, inc=inc)

    def dma(q, out_ap, in_ap, waits=(), inc=None):
        return p.em(q, lambda e: e.dma_start(out=out_ap, in_=in_ap),
                    waits=waits, inc=inc, amt=16)

    def coll(kind, op, ins, outs, waits, inc):
        def fn(e):
            return e.collective_compute(kind, op, replica_groups=GROUPS, ins=ins, outs=outs)
        return p.em("gp", fn, waits=waits, inc=inc, amt=1)

    def xbf(fc, c0, w):
        return arena1[:, SC * fc + c0: SC * fc + c0 + w]

    dma("sync", ident[:, :], ident_i, inc="dsc")
    dma("sync", bqc_sb[:, :], bq_c, inc="dsc")
    dma("sync", boc_sb[:, :], bo_c, inc="dsc")
    dma("sync", wrow[:, :], wsr_i, inc="dsc")
    d_r01 = dma("sync", r01_sb[:, :], r01_i, inc="dsc")
    d_ident = d_bq = d_bo = d_wsr = d_r01 = p.now("dsc")
    d_x = dma("gp", arena1[:, :], x_img, inc="dg")

    p.em("dve", lambda v: v.memset(onesb_r[:, :], 1.0), inc="dve")
    p.em("dve", lambda v: v.memset(onesb_c[:, :], 1.0), inc="dve")
    p.em("dve", lambda v: v.memset(onesf_r[:, :], 1.0), inc="dve")
    m_eps = p.em("dve", lambda v: v.memset(eps_t[:, :], 1e-8), inc="dve")
    m_eps5 = p.em("dve", lambda v: v.memset(eps5_t[:, :], 1e-5), inc="dve")
    m_nsh = p.em("dve", lambda v: v.memset(nshift[:, :], -EXP_SHIFT), inc="dve")
    for sc in range(8):
        view = cdv[:, sc * (H * 129):(sc + 1) * (H * 129)].rearrange(
            "p (h c) -> p h c", h=H)[:, :, 128:129]
        p.em("dve", lambda v, vw=view: v.memset(vw, 1.0), inc="dve")
    mones = p.now("dve")

    mc = p.em("dve", lambda v: v.tensor_scalar_max(wclip[:, :], wrow[:, :], 1e-8),
              waits=[d_wsr], inc="dve")
    psub = p.em("dve", lambda v: v.tensor_scalar_sub(urow[0:1, 0:16], wrow[0:1, 16:32], 0.1),
                waits=[d_wsr], inc="dve")
    asg = p.em("act", lambda s: s.activation(urow[0:1, 0:16], urow[0:1, 0:16], AF.Sign,
                                             bias=0.0, scale=1.0),
               waits=[psub], inc="act")
    m_u = p.em("dve", lambda v: v.tensor_scalar_max(urow[0:1, 0:16], urow[0:1, 0:16], 0.0),
               waits=[asg], inc="dve")
    m_a = p.em("dve", lambda v: v.tensor_mul(urow[0:1, 16:32], urow[0:1, 0:16],
                                             wclip[0:1, 16:32]),
               waits=[m_u, mc], inc="dve")
    m_b = p.em("dve", lambda v: v.tensor_sub(urow[0:1, 32:48], wclip[0:1, 16:32],
                                             urow[0:1, 16:32]),
               waits=[m_a], inc="dve")
    m_n = p.em("dve", lambda v: v.tensor_scalar(urow[0:1, 48:64], urow[0:1, 0:16],
                                                -1.0, 1.0, ALU.mult, ALU.add),
               waits=[m_u], inc="dve")
    mrows = p.now("dve")

    srcrow = {0: 16, 1: 32, 2: 0, 3: 48}
    for i in range(4 if PH0_PATTERNS else 0):
        r_ = srcrow[i]
        m1 = mm(pb[0][0:16, 0:1], urow[0:1, r_:r_ + 16], onesf_r[0:1, 0:1], True, True,
                waits=[mrows, mones] + p.takewaits(("bank", 0)), inc="pe")
        c1 = p.em("dve", lambda v, i=i: v.tensor_copy(patcol[:, i:i + 1], pb[0][0:16, 0:1]),
                  waits=[m1], inc="dve")
        p.use(("bank", 0), c1)
        m2 = mm(pb[1][0:1, 0:128], patcol[:, i:i + 1], r01_sb[:, :], True, True,
                waits=[c1, d_r01] + p.takewaits(("bank", 1)), inc="pe")
        c2 = p.em("dve", lambda v, i=i: v.tensor_copy(patrow[0:1, i * 128:(i + 1) * 128],
                                                      pb[1][0:1, 0:128]),
                  waits=[m2], inc="dve")
        p.use(("bank", 1), c2)
        m3 = mm(pb[2][:, 0:128], onesf_r[0:1, :], patrow[0:1, i * 128:(i + 1) * 128],
                True, True, waits=[c2] + p.takewaits(("bank", 2)), inc="pe")
        c3 = p.em("dve", lambda v, i=i: v.tensor_copy(pat[i], pb[2][:, 0:128]),
                  waits=[m3], inc="dve")
        p.use(("bank", 2), c3)
    if PH0_PATTERNS:
        m4 = mm(pb[3][:, 0:16], onesf_r[0:1, :], wclip[0:1, 0:16], True, True,
                waits=[mc, mones] + p.takewaits(("bank", 3)), inc="pe")
        c4 = p.em("dve", lambda v: v.tensor_copy(wstd_c[:, :], pb[3][:, 0:16]),
                  waits=[m4], inc="dve")
        p.use(("bank", 3), c4)
    mpat = p.now("dve")

    p.barrier()

    if stop_phase <= 0:
        return _finish(nc, p, sem, stck)

    PHASE_MARKS.append(("PH1", sum(1 for o in p.ops if o[0] == "pe")))
    def tm_proj_b1(img_ap, bias_row, is_k):
        dws = [dma("gp", wbuf[0, :, :], img_ap[:, 0:2176],
                   waits=p.takewaits(("wbuf", 0)), inc="dgw0"),
               dma("gp", wbuf[1, :, :], img_ap[:, 2176:4352],
                   waits=p.takewaits(("wbuf", 1)), inc="dgw1")]
        for hidx in range(PH1_LIMIT):
            par = hidx % 2
            for wave in range(2):
                mss = []
                for i in range(4):
                    sc = wave * 4 + i
                    bidx = 4 * wave + i
                    bank = pb[bidx]
                    bw = [dws[hidx], d_x] + p.takewaits(("bank", bidx))
                    for fc in range(NFC):
                        mm(bank[:, :128], xbf(fc, sc * 128, 128),
                           wbuf[par, :, 128 * fc:128 * (fc + 1)],
                           start=(fc == 0), stop=False,
                           waits=bw if fc == 0 else [])
                        bw = []
                    ms = mm(bank[:, :128], onesb_r[0:1, :],
                            wbuf[par, 0:1, 2048:2176],
                            start=False, stop=True,
                            waits=[mones], inc="pe")
                    mss.append(ms)
                for i in range(4 if PH1_EVAC else 0):
                    sc = wave * 4 + i
                    bidx = 4 * wave + i
                    bank = pb[bidx]
                    if is_k:
                        pr = i % 4
                        t_t = trow8[2 * pr][:, :128]
                        t_r = trow8[2 * pr + 1][:, :128]
                        a1 = p.em("act", lambda s, b=bank, t=t_t: s.activation(
                            t, b[:, :128], AF.Exp, bias=0.0, scale=1.0),
                            waits=[mss[i]] + p.takewaits(("trow8", pr)), inc="act")
                        a2 = p.em("dve", lambda v, b=bank, t=t_r: v.tensor_scalar_max(
                            t, b[:, :128], 0.0),
                            waits=[mss[i], a1], inc="dve")
                        dst = cdk[:, 2048 * sc + 128 * hidx: 2048 * sc + 128 * (hidx + 1)]
                        a3 = p.em("dve", lambda v, d=dst, t1=t_t, t2=t_r:
                                  v.scalar_tensor_tensor(d, t1, 1.0, t2, ALU.min, ALU.add),
                            waits=[a1, a2], inc="dve")
                        p.use(("trow8", pr), a3)
                        p.use(("bank", bidx), a1, a2)
                        p.use(("cdk_w",), a3)
                    else:
                        dstv = cdv[:, sc * (H * 129) + 129 * hidx:
                                   sc * (H * 129) + 129 * hidx + 128]
                        a1 = p.em("act", lambda s, d=dstv, b=bank: s.copy(d, b[:, :128]),
                                  waits=[mss[i]], inc="act")
                        p.use(("bank", bidx), a1)
                        p.use(("cdv_w",), a1)
            p.use(("wbuf", par), p.now("pe"))
            if hidx + 2 < 16:
                par2 = (hidx + 2) % 2
                dws.append(dma("gp", wbuf[par2, :, :],
                               img_ap[:, (hidx + 2) * 2176:(hidx + 3) * 2176],
                               waits=p.takewaits(("wbuf", par2)),
                               inc="dgw%d" % par2))

    tm_proj_b1(wk_i, bk_r, True)
    if stop_phase <= 1:
        return _finish(nc, p, sem, stck)
    PHASE_MARKS.append(("PH2", sum(1 for o in p.ops if o[0] == "pe")))
    tm_proj_b1(wv_i, bv_r, False)
    if stop_phase <= 2:
        return _finish(nc, p, sem, stck)

    PHASE_MARKS.append(("PH3", sum(1 for o in p.ops if o[0] == "pe")))
    kq = p.takewaits(("cdk_w",)) + p.takewaits(("cdv_w",))
    kv_ev = []
    for h in range(H):
        bidx = h % 2
        bank = pb[bidx]
        bw = kq + p.takewaits(("bank", bidx))
        kq = []
        ms = None
        for sc in range(8):
            ms = mm(bank[:, :129],
                    cdk[:, 2048 * sc + 128 * h: 2048 * sc + 128 * (h + 1)],
                    cdv[:, sc * H * 129 + 129 * h: sc * H * 129 + 129 * (h + 1)],
                    start=(sc == 0), stop=(sc == 7),
                    waits=bw if sc == 0 else [], inc="pe" if sc == 7 else None)
            bw = []
        ev = p.em("act", lambda s, b=bank, h=h: s.copy(kvst[h % 2], b[:, :129]),
                  waits=[ms] + p.takewaits(("kvst", h % 2)), inc="act")
        p.use(("bank", bidx), ev)
        d_h = dma("sync", kv_in.ap()[:, 129 * h:129 * (h + 1)], kvst[h % 2],
                  waits=[ev], inc="dckv")
        p.use(("kvst", h % 2), d_h)
        kv_ev.append(d_h)
    coll("AllReduce", ALU.add, [kv_in.ap()], [kv_out.ap()], waits=kv_ev, inc="cc1")

    if "kv" in dbg:
        dma("sync", dbg["kv"], kv_out.ap(), waits=[("cc1", 1)], inc="ds")

    if stop_phase <= 3:
        return _finish(nc, p, sem, stck)

    PHASE_MARKS.append(("PH4", sum(1 for o in p.ops if o[0] == "pe")))
    nrow = 8

    def emit_qproj(h):
        par = h % 2
        wsi = h % 3
        dw = dwq[h]
        ev_q = {}
        mss = {}
        for slot in range(2):
            bidx = (2 * h + slot) % 4
            bank = pb[bidx]
            bw = [dw[0], d_x] + p.takewaits(("bank", bidx))
            ms = None
            for fc in range(NFC):
                if fc == 8:
                    bw = bw + [dw[1]]
                ms = mm(bank[:, :512],
                        wpan[wsi][:, 128 * fc:128 * (fc + 1)],
                        xbf(fc, slot * 512, 512),
                        start=(fc == 0), stop=(fc == NFC - 1),
                        waits=bw,
                        inc="pe" if fc == NFC - 1 else None)
                bw = []
            mss[slot] = ms
        p.use(wkey(wsi), p.now("pe"))
        for slot in range(2):
            bidx = (2 * h + slot) % 4
            bank = pb[bidx]
            a1 = p.em("act", lambda s, b=bank, h=h: s.activation(
                trow[0][:, :512], b[:, :512], AF.Exp,
                bias=bqc_sb[:, h:h + 1], scale=1.0),
                waits=[mss[slot], d_bq] + p.takewaits(("trow", 0)), inc="act")
            a2 = p.em("dve", lambda v, b=bank, h=h: v.tensor_scalar(
                trow[1][:, :512], b[:, :512], bqc_sb[:, h:h + 1], 0.0,
                ALU.add, ALU.max),
                waits=[mss[slot], d_bq, a1], inc="dve")
            dstq = qbuf[par, :, 512 * slot:512 * (slot + 1)]
            a3 = p.em("dve", lambda v, d=dstq: v.scalar_tensor_tensor(
                d, trow[0][:, :512], 1.0, trow[1][:, :512], ALU.min, ALU.add),
                waits=[a1, a2] + p.takewaits(("qbuf", par, slot)), inc="dve")
            p.use(("trow", 0), a3)
            p.use(("trow", 1), a3)
            p.use(("bank", bidx), a1, a2)
            ev_q[slot] = a3
        return ev_q

    R4 = [rows[0:1, 0:512], rows[0:1, 512:1024],
          rows2[0:1, 0:512], rows2[0:1, 512:1024]]
    pend_n = []

    def attn1_stage1(h, ev_q):
        par = h % 2
        for slot in range(2):
            i2 = 2 * h + slot
            ridx = i2 % 4
            qsl = qbuf[par, :, 512 * slot:512 * (slot + 1)]
            b_u = pb[4 + slot]
            msu = mm(b_u[:, :512], kvbf[:, 129 * h:129 * h + 128], qsl, True, True,
                     waits=[ev_q[slot], d_kvbf] + p.takewaits(("bank", 4 + slot)),
                     inc="pe")
            msd = mm(pb[6][0:1, :512], kvbf[:, 129 * h + 128:129 * h + 129], qsl,
                     True, True, waits=p.takewaits(("bank", 6)), inc="pe")
            p.use(("qbuf", par, slot), msu, msd)
            a_d = p.em("act", lambda s, r=R4[ridx]: s.activation(
                r, pb[6][0:1, :512], AF.Identity,
                bias=eps_t[0:1, 0:1], scale=1.0),
                waits=[msd, m_eps] + p.takewaits(("rowsA", ridx)), inc="act")
            p.use(("bank", 6), a_d)
            r_d = p.em("dve", lambda v, r=R4[ridx]: v.reciprocal(r, r),
                       waits=[a_d], inc="dve")
            dsta = cdk[:, 1024 * h + 512 * slot: 1024 * h + 512 * (slot + 1)]
            a_u = p.em("act", lambda s, d=dsta, b=b_u: s.copy(d, b[:, :512]),
                       waits=[msu], inc="act")
            p.use(("bank", 4 + slot), a_u)
            pend_n.append((ridx, r_d, a_u, dsta))

    def attn1_stage2():
        while pend_n:
            ridx, r_d, a_u, dsta = pend_n.pop(0)
            bb = 7 - (ridx % 2)
            msb = mm(pb[bb][:, :512], onesf_r[0:1, :], R4[ridx], True, True,
                     waits=[r_d] + p.takewaits(("bank", bb)), inc="pe")
            t_m = p.em("dve", lambda v, d=dsta, bb=bb: v.tensor_mul(d, d, pb[bb][:, :512]),
                       waits=[a_u, msb], inc="dve")
            p.use(("bank", bb), t_m)
            p.use(("rowsA", ridx), msb)
            p.use(("attn1",), t_m)

    dwq = {}

    def qproj_dma(h):
        wsi = h % 3
        mA = dma("gp", wpan[wsi][:, 0:1024], wq_i[:, h * 2048:h * 2048 + 1024],
                 waits=p.takewaits(wkey(wsi)), inc="dgw%d" % wsi)
        mB = dma("gp", wpan[wsi][:, 1024:2048],
                 wq_i[:, h * 2048 + 1024:(h + 1) * 2048], inc="dgh%d" % wsi)
        dwq[h] = (mA, mB)

    qproj_dma(0)
    qproj_dma(1)
    qproj_dma(2)
    d_kvbf = dma("gp", kvbf[:, :], kv_out.ap(), waits=[("cc1", 1)], inc="dkv")
    ev_prev = emit_qproj(0)
    for h in range(16):
        if h + 3 < 16:
            qproj_dma(h + 3)
        attn1_stage2()
        attn1_stage1(h, ev_prev)
        if h + 1 < 16:
            ev_prev = emit_qproj(h + 1)
    attn1_stage2()

    p.barrier()

    PHASE_MARKS.append(("PH5", sum(1 for o in p.ops if o[0] == "pe")))
    def proj_resid_stats(img_ap, rhs_fn, bias_col, xres_fn, zdst_ap,
                         bf_fn=None, bft3=None, sqt3=None, xfix=None):
        pend = []
        stat_started = [False, False]
        stat_sl = [[], []]

        def flush_stats(keep=0):
            while len(pend) > keep:
                jc, slot, zb_ms, zq_ms, zrot, s1rhs, trk = pend.pop(0)
                b_s, b_q = pb[4 + slot], pb[6 + slot]
                first = not stat_started[slot]
                w1 = [zb_ms] + (p.takewaits(("bank", 4 + slot)) if first else [])
                w2 = [zq_ms] + (p.takewaits(("bank", 6 + slot)) if first else [])
                s1 = mm(b_s[0:1, :512], onesb_c[:, 0:1], s1rhs,
                        start=first, stop=(jc == NFC - 1), waits=w1, inc="pe")
                s2 = mm(b_q[0:1, :512], onesb_c[:, 0:1], sqt3[zrot],
                        start=first, stop=(jc == NFC - 1), waits=w2, inc="pe")
                stat_started[slot] = True
                if trk:
                    p.use(("zbft8", zrot), s1)
                p.use(("zsqt8", zrot), s2)
                if jc == NFC - 1:
                    stat_sl[slot] = [s1, s2]

        def w_dma(jc):
            si2 = jc % 3
            mA = dma("gp", wpan[si2][:, 0:1024],
                     img_ap[:, jc * 2048:jc * 2048 + 1024],
                     waits=p.takewaits(wkey(si2)), inc="dgw%d" % si2)
            mB = dma("gp", wpan[si2][:, 1024:2048],
                     img_ap[:, jc * 2048 + 1024:(jc + 1) * 2048],
                     inc="dgh%d" % si2)
            return (mA, mB)

        dws = [w_dma(0), w_dma(1), w_dma(2)]
        for jc in range(16):
            par = jc % 3
            for slot in range(2):
                idx = 2 * jc + slot
                bidx = slot
                bank = pb[bidx]
                xrot = idx % 4
                d_x0 = dma("sync", x0t[xrot, :, :], xres_fn(jc, slot),
                           waits=p.takewaits(("x0t", xrot)), inc="dsx%d" % xrot)
                if xfix is not None:
                    axc, bcm8 = xfix
                    c1 = p.em("dve", lambda v, xr=xrot, s=slot: v.tensor_mul(
                        x0t[xr, :, :], x0t[xr, :, :], axc[s, 0, :, :]),
                        waits=[d_x0, bcm8[slot][0]], inc="dve")
                    c2 = p.em("dve", lambda v, xr=xrot, s=slot: v.tensor_add(
                        x0t[xr, :, :], x0t[xr, :, :], axc[s, 1, :, :]),
                        waits=[c1, bcm8[slot][1]], inc="dve")
                    p.use(("abcA", slot), c1)
                    p.use(("abcB", slot), c2)
                    xr_ms = c2
                else:
                    xr_ms = d_x0
                bw = [dws[jc][0]] + p.takewaits(("bank", bidx))
                ms = None
                for ec in range(NFC):
                    if ec == 8:
                        bw = bw + [dws[jc][1]]
                    ms = mm(bank[:, :512],
                            wpan[par][:, 128 * ec:128 * (ec + 1)],
                            rhs_fn(ec, slot),
                            start=(ec == 0), stop=(ec == NFC - 1),
                            waits=bw,
                            inc="pe" if ec == NFC - 1 else None)
                    bw = []
                zr = idx % 3
                if bias_col is not None:
                    ev = p.em("dve", lambda v, b=bank, jc=jc, zr=zr, xr=xrot:
                              v.scalar_tensor_tensor(zt[zr][:, :512], b[:, :512],
                                                     bias_col[:, jc:jc + 1],
                                                     x0t[xr, :, :], ALU.add, ALU.add),
                              waits=[ms, xr_ms, d_bo] + p.takewaits(("zt", zr)),
                              inc="dve")
                else:
                    ev = p.em("dve", lambda v, b=bank, zr=zr, xr=xrot:
                              v.scalar_tensor_tensor(zt[zr][:, :512], b[:, :512],
                                                     1.0, x0t[xr, :, :],
                                                     ALU.mult, ALU.add),
                              waits=[ms, xr_ms] + p.takewaits(("zt", zr)), inc="dve")
                p.use(("x0t", xrot), ev)
                p.use(("bank", bidx), ev)
                d_z = dma("sync", zdst_ap[128 * jc:128 * (jc + 1),
                                          512 * slot:512 * (slot + 1)],
                          zt[zr][:, :512], waits=[ev], inc="dsz%d" % zr)
                brot = idx % 3
                if bf_fn is not None:
                    bfdst = bf_fn(jc, slot)
                    zb = p.em("dve", lambda v, zr=zr, d=bfdst:
                              v.tensor_copy(d, zt[zr][:, :512]),
                              waits=[ev], inc="dve")
                    s1rhs, trk = bfdst, False
                else:
                    zb = p.em("dve", lambda v, zr=zr, br=brot:
                              v.tensor_copy(bft3[br], zt[zr][:, :512]),
                              waits=[ev] + p.takewaits(("zbft8", brot)), inc="dve")
                    s1rhs, trk = bft3[brot], True
                zq = p.em("act", lambda s, zr=zr, br=brot:
                          s.activation(sqt3[br], zt[zr][:, :512],
                                       AF.Square, bias=0.0, scale=1.0),
                          waits=[zb] + p.takewaits(("zsqt8", brot)), inc="act")
                p.use(("zt", zr), d_z, zq)
                pend.append((jc, slot, zb, zq, brot, s1rhs, trk))
                flush_stats(keep=2)
            p.use(wkey(par), p.now("pe"))
            if jc + 3 < 16:
                dws.append(w_dma(jc + 3))
        flush_stats()
        return stat_sl

    def ln_rows(base, stat_sl, mkbf=False):
        outm = []
        en_bf = []
        ea_bf = []
        for slot in range(2):
            b_s, b_q = pb[4 + slot], pb[6 + slot]
            rw = p.takewaits(("rowsA",)) + p.takewaits(("rowsB",))
            a1 = p.em("act", lambda s, b=b_s: s.activation(
                rows[0:1, 0:512], b[0:1, :512], AF.Identity, bias=0.0,
                scale=-1.0 / 2048.0), waits=stat_sl[slot] + rw, inc="act")
            a2 = p.em("act", lambda s, b=b_q: s.activation(
                rows[0:1, 512:1024], b[0:1, :512], AF.Identity, bias=0.0,
                scale=1.0 / 2048.0), inc="act")
            p.use(("bank", 4 + slot), a1)
            p.use(("bank", 6 + slot), a2)
            if mkbf:
                mn = mm(pb[4 + slot][:, :512], onesf_r[0:1, :], rows[0:1, 0:512],
                        True, True, waits=[a1] + p.takewaits(("bank", 4 + slot)),
                        inc="pe")
                p.use(("rowsA",), mn)
                en = p.em("act", lambda s, sl=slot: s.copy(
                    zbft[sl][:, :512], pb[4 + sl][:, :512]),
                    waits=[mn] + p.takewaits(("zbft8", slot)), inc="act")
                p.use(("bank", 4 + slot), en)
                en_bf.append(en)
            d1 = p.em("dve", lambda v: v.tensor_mul(rows2[0:1, 0:512],
                                                    rows[0:1, 0:512], rows[0:1, 0:512]),
                      waits=[a1] + p.takewaits(("rows2",)), inc="dve")
            d2 = p.em("dve", lambda v: v.tensor_sub(rows[0:1, 512:1024],
                                                    rows[0:1, 512:1024], rows2[0:1, 0:512]),
                      waits=[a2, d1], inc="dve")
            a3 = p.em("act", lambda s: s.activation(rows[0:1, 512:1024],
                                                    rows[0:1, 512:1024], AF.Sqrt,
                                                    bias=eps5_t[0:1, 0:1], scale=1.0),
                      waits=[d2, m_eps5], inc="act")
            d3 = p.em("dve", lambda v: v.reciprocal(rows2[0:1, 0:512],
                                                    rows[0:1, 512:1024]),
                      waits=[a3], inc="dve")
            dA = dma("sync", rowbuf.ap()[base + 2 * slot:base + 2 * slot + 1, :],
                     rows2[0:1, 0:512], waits=[d3], inc="dsr")
            if mkbf:
                ma = mm(pb[6 + slot][:, :512], onesf_r[0:1, :], rows2[0:1, 0:512],
                        True, True, waits=[d3] + p.takewaits(("bank", 6 + slot)),
                        inc="pe")
                p.use(("rows2",), ma)
                ea = p.em("act", lambda s, sl=slot: s.copy(
                    zsqt[sl][:, :512], pb[6 + sl][:, :512]),
                    waits=[ma] + p.takewaits(("zsqt8", slot)), inc="act")
                p.use(("bank", 6 + slot), ea)
                ea_bf.append(ea)
            d4 = p.em("dve", lambda v: v.scalar_tensor_tensor(
                rows2[0:1, 512:1024], rows[0:1, 0:512], 1.0, rows2[0:1, 0:512],
                ALU.mult, ALU.mult), waits=[d3], inc="dve")
            dB = dma("sync", rowbuf.ap()[base + 2 * slot + 1:base + 2 * slot + 2, :],
                     rows2[0:1, 512:1024], waits=[d4], inc="dsra")
            p.use(("rowsA",), d1, d4)
            p.use(("rowsB",), a3)
            p.use(("rows2",), dA, dB, d4)
            outm.append((dA, dB))
        return outm, en_bf, ea_bf

    def ln_normalize(base, rowdma, xout_ap, zsrc_ap):
        bms = {}
        for slot in range(2):
            dA = dma("gp", abc[slot, 0, :, :],
                     rowbuf.ap()[base + 2 * slot:base + 2 * slot + 1, :]
                     .partition_broadcast(128),
                     waits=[rowdma[slot][0]] + p.takewaits(("abcA", slot)),
                     inc="dgr%d" % (2 * slot))
            dB = dma("gp", abc[slot, 1, :, :],
                     rowbuf.ap()[base + 2 * slot + 1:base + 2 * slot + 2, :]
                     .partition_broadcast(128),
                     waits=[rowdma[slot][1]] + p.takewaits(("abcB", slot)),
                     inc="dgr%d" % (2 * slot + 1))
            bms[slot] = (dA, dB)
        zts = [zt[0], zt[1], zt[2], x0t[0], x0t[1], x0t[2], x0t[3]]

        def zkey(zr):
            return ("zt", zr) if zr < 3 else ("x0t", zr - 3)

        def zsem(zr):
            return "dsz%d" % zr if zr < 3 else "dsx%d" % (zr - 3)

        lastms = []
        for jc in range(NFC):
            for slot in range(2):
                idx = 2 * jc + slot
                zr = idx % 7
                d_z = dma("sync", zts[zr][:, :512],
                          zsrc_ap[128 * jc:128 * (jc + 1), 512 * slot:512 * (slot + 1)],
                          waits=p.takewaits(zkey(zr)), inc=zsem(zr))
                eng = "dve" if (idx % 2 == 0) else "gp"
                esem = "dve" if eng == "dve" else "gpc"
                t1 = p.em(eng, lambda v, zr=zr, s=slot: v.tensor_mul(
                    zts[zr][:, :512], zts[zr][:, :512], abc[s, 0, :, :]),
                    waits=[d_z, bms[slot][0]], inc=esem)
                t2 = p.em(eng, lambda v, zr=zr, s=slot: v.tensor_add(
                    zts[zr][:, :512], zts[zr][:, :512], abc[s, 1, :, :]),
                    waits=[t1, bms[slot][1]], inc=esem)
                p.use(("abcA", slot), t1)
                p.use(("abcB", slot), t2)
                d_o = dma("act", xout_ap[128 * jc:128 * (jc + 1),
                                         512 * slot:512 * (slot + 1)],
                          zts[zr][:, :512], waits=[t2], inc="dsc")
                p.use(zkey(zr), d_o)
                lastms.append(d_o)
        return lastms

    def wa_dma(pi, si):
        width = WA_RANGES[pi][1]
        mA = dma("gp", wpan[si][:, 0:8 * width],
                 wa_i[:, WA_OFF[pi]:WA_OFF[pi] + 8 * width],
                 waits=p.takewaits(wkey(si)), inc="dgw%d" % si)
        mB = dma("gp", wpan[si][:, 8 * width:16 * width],
                 wa_i[:, WA_OFF[pi] + 8 * width:WA_OFF[pi + 1]],
                 inc="dgh%d" % si)
        return (mA, mB)

    sm1 = proj_resid_stats(
        wo_i, lambda ec, slot: cdk[:, 1024 * ec + 512 * slot: 1024 * ec + 512 * (slot + 1)],
        boc_sb,
        lambda jc, slot: x_img[:, 1024 * jc + 512 * slot: 1024 * jc + 512 * (slot + 1)],
        zbuf.ap(),
        bf_fn=lambda jc, slot: arena1[:, SC * jc + 512 * slot:
                                      SC * jc + 512 * (slot + 1)],
        sqt3=[zsqt[0], zsqt[1], kvs[0]])
    p.use(("cdk_r",), p.now("pe"))
    dws6 = {}

    def kstd_dma(h):
        dws6[h] = wa_dma(h, h % 3)

    kstd_dma(0)
    kstd_dma(1)
    kstd_dma(2)
    lr1, en_bf, ea_bf = ln_rows(0, sm1, mkbf=True)
    x1bf_ms = {}
    cc_slot = {0: [], 1: []}
    for jc in range(NFC):
        for slot in range(2):
            idx = 2 * jc + slot
            eng = "dve" if idx % 2 == 0 else "gp"
            esem = "dve" if eng == "dve" else "gpc"
            tile = arena1[:, SC * jc + 512 * slot: SC * jc + 512 * (slot + 1)]
            c1 = p.em(eng, lambda v, t=tile, s=slot: v.tensor_add(t, t, zbft[s]),
                      waits=[en_bf[slot]], inc=esem)
            c2 = p.em(eng, lambda v, t=tile, s=slot: v.tensor_mul(t, t, zsqt[s]),
                      waits=[c1, ea_bf[slot]], inc=esem)
            x1bf_ms.setdefault(jc, []).append(c2)
            cc_slot[slot].append(c2)
    for slot in range(2):
        p.use(("zbft8", slot), *cc_slot[slot])
        p.use(("zsqt8", slot), *cc_slot[slot])

    if stop_phase <= 5:
        return _finish(nc, p, sem, stck)

    PHASE_MARKS.append(("PH6k", sum(1 for o in p.ops if o[0] == "pe")))
    for h in range(16):
        par = h % 3
        mss = {}
        for slot in range(2):
            bank = pb[slot]
            bw = [dws6[h][0]] + p.takewaits(("bank", slot))
            ms = None
            for fc in range(NFC):
                if fc == 8:
                    bw = bw + [dws6[h][1]]
                if h == 0:
                    bw = bw + x1bf_ms.get(fc, [])
                ms = mm(bank[0:120, :512],
                        wpan[par][:, 120 * fc:120 * (fc + 1)],
                        xbf(fc, slot * 512, 512),
                        start=(fc == 0), stop=(fc == NFC - 1),
                        waits=bw,
                        inc="pe" if fc == NFC - 1 else None)
                bw = []
            mss[slot] = ms
        p.use(wkey(par), p.now("pe"))
        if h + 3 < 16:
            kstd_dma(h + 3)
        for slot in range(2):
            bank = pb[slot]
            dst = cdk[0:120, 1024 * h + 512 * slot: 1024 * h + 512 * (slot + 1)]
            ev = p.em("act", lambda s, d=dst, b=bank: s.copy(d, b[0:120, :512]),
                      waits=[mss[slot]] + p.takewaits(("cdk_r",)), inc="act")
            p.use(("bank", slot), ev)
            p.use(("kstd",), ev)

    PHASE_MARKS.append(("PH6l", sum(1 for o in p.ops if o[0] == "pe")))
    lowbuf = mask_sb
    dwl_w = (p.takewaits(("x0t", 0)) + p.takewaits(("x0t", 1)) +
             p.takewaits(("x0t", 2)) + p.takewaits(("x0t", 3)) +
             p.takewaits(("trow", 0)) + p.takewaits(("trow", 1)))
    dwl = dma("gp", lowbuf[:, 0:4096], wa_i[:, WA_OFF[16]:WA_OFF[17]],
              waits=dwl_w, inc="dgl")
    sel_ev = []
    for wave in range(2):
        mss = []
        for i in range(4):
            sc = 4 * wave + i
            bidx = 4 + i
            bank = pb[bidx]
            bw = [dwl] + p.takewaits(("bank", bidx))
            ms = None
            for fc in range(NFC):
                ms = mm(bank[:, :256], xbf(fc, sc * 128, 128),
                        lowbuf[:, 256 * fc:256 * (fc + 1)],
                        start=(fc == 0), stop=(fc == NFC - 1),
                        waits=bw if fc == 0 else [],
                        inc="pe" if fc == NFC - 1 else None)
                bw = []
            mss.append(ms)
        for i in range(4):
            sc = 4 * wave + i
            bidx = 4 + i
            bank = pb[bidx]
            e1 = p.em("dve", lambda v, b=bank: v.tensor_mul(seltmp[0], b[:, 128:256], pat[0]),
                      waits=[mss[i], mpat] + p.takewaits(("seltmp",)) +
                      p.takewaits(("zbft8", 0)) + p.takewaits(("zbft8", 1)) +
                      p.takewaits(("zsqt8", 0)) + p.takewaits(("zsqt8", 1)), inc="dve")
            e2 = p.em("dve", lambda v, b=bank: v.tensor_mul(seltmp[1], b[:, 0:128], pat[1]),
                      waits=[mss[i]], inc="dve")
            e3 = p.em("dve", lambda v, sc=sc: v.tensor_add(selq_tm[sc], seltmp[0], seltmp[1]),
                      waits=[e1, e2] + p.takewaits(("u4q", sc)), inc="dve")
            e4 = p.em("dve", lambda v, b=bank: v.tensor_mul(seltmp[0], b[:, 0:128], pat[2]),
                      waits=[e3], inc="dve")
            e5 = p.em("dve", lambda v, b=bank: v.tensor_mul(seltmp[1], b[:, 128:256], pat[3]),
                      waits=[e3], inc="dve")
            e6 = p.em("dve", lambda v, sc=sc: v.tensor_add(selk_tm[sc], seltmp[0], seltmp[1]),
                      waits=[e4, e5] + p.takewaits(("u4k", sc)), inc="dve")
            p.use(("seltmp",), e6)
            p.use(("bank", bidx), e4, e5)
            sel_ev.append((e3, e6))
    for sc in range(8):
        t1 = trans(pb_bf[7][:, 0:128], selq_tm[sc],
                   waits=[sel_ev[sc][0], d_ident] + p.takewaits(("bank", 7)))
        c1 = p.em("act", lambda s, sc=sc: s.copy(selq_fm[:, 128 * sc:128 * (sc + 1)],
                                                 pb_bf[7][:, 0:128]),
                  waits=[t1], inc="act")
        p.use(("bank", 7), c1)
        t2 = trans(pb_bf[7][:, 0:128], selk_tm[sc],
                   waits=[sel_ev[sc][1]] + p.takewaits(("bank", 7)))
        c2 = p.em("act", lambda s, sc=sc: s.copy(selk_fm[:, 128 * sc:128 * (sc + 1)],
                                                 pb_bf[7][:, 0:128]),
                  waits=[t2], inc="act")
        p.use(("bank", 7), c2)
        p.use(("selq_fm",), c1)
        p.use(("selk_fm",), c2)

    low_last_pe = p.now("pe")

    wkt = p.takewaits(("selk_fm",)) + p.takewaits(("kstd",))
    ktail = []
    for h in range(H):
        d_t = dma("sync", cdk[120:128, 1024 * h:1024 * (h + 1)],
                  selk_fm[8 * h:8 * (h + 1), :],
                  waits=wkt if h == 0 else [], inc="dskt")
        wkt = []
        ktail.append(d_t)
    d_kf = dma("sync", kf_in.ap().rearrange("(h p) s -> p h s", p=128),
               cdk[:, :].rearrange("p (h s) -> p h s", h=H),
               waits=ktail, inc="dkf")
    coll("AllGather", ALU.bypass, [kf_in.ap()], [kf_out.ap()], waits=[d_kf], inc="cc2")

    PHASE_MARKS.append(("PH6v", sum(1 for o in p.ops if o[0] == "pe")))
    dws7 = {}

    def v_dma(h):
        dws7[h] = wa_dma(17 + h, h % 3)

    v_dma(0)
    v_dma(1)
    v_dma(2)
    for h in range(16):
        par = h % 3
        for wave in range(2):
            mss = []
            for i in range(4):
                sc = 4 * wave + i
                bank = pb[i]
                bw = [dws7[h][0]] + p.takewaits(("bank", i))
                ms = None
                for fc in range(NFC):
                    if fc == 8:
                        bw = bw + [dws7[h][1]]
                    ms = mm(bank[:, :128], xbf(fc, sc * 128, 128),
                            wpan[par][:, 128 * fc:128 * (fc + 1)],
                            start=(fc == 0), stop=(fc == NFC - 1),
                            waits=bw,
                            inc="pe" if fc == NFC - 1 else None)
                    bw = []
                mss.append(ms)
            for i in range(4):
                sc = 4 * wave + i
                bank = pb[i]
                dstv = cdv[:, sc * (H * 129) + 129 * h: sc * (H * 129) + 129 * h + 128]
                ev = p.em("act", lambda s, d=dstv, b=bank: s.copy(d, b[:, :128]),
                          waits=[mss[i]], inc="act")
                p.use(("bank", i), ev)
                p.use(("cdv2",), ev)
        p.use(wkey(par), p.now("pe"))
        if h + 3 < 16:
            v_dma(h + 3)
    d_vv = dma("sync", vv_in.ap().rearrange("(c p) s -> p c s", p=128),
               cdv[:, :].rearrange("p (c s) -> p c s", c=8),
               waits=p.takewaits(("cdv2",)), inc="dvv")
    coll("AllGather", ALU.bypass, [vv_in.ap()], [vv_out.ap()], waits=[d_vv], inc="cc3")

    d_mask = dma("sync", mask_sb[:, 0:4096], masks_i,
                 waits=[low_last_pe], inc="dsm")

    if "kf" in dbg:
        dma("sync", dbg["kf"], kf_out.ap(), waits=[("cc2", 1)], inc="ds")
    if "vv" in dbg:
        dma("sync", dbg["vv"], vv_out.ap(), waits=[("cc3", 1)], inc="ds")

    if stop_phase <= 6:
        return _finish(nc, p, sem, stck)

    PHASE_MARKS.append(("PH7", sum(1 for o in p.ops if o[0] == "pe")))
    dwq7 = {}

    def qstd_dma(h):
        dwq7[h] = wa_dma(33 + h, h % 2)

    class QstdFeed:

        def __init__(self, h):
            self.h = h
            self.par = h % 2
            self.dw = dwq7[h]
            self.seq = [(slot, fc) for slot in range(2) for fc in range(NFC)]
            self.i = 0
            self.evq = {}

        def feed(self, n):
            while n > 0 and self.i < len(self.seq):
                slot, fc = self.seq[self.i]
                self.i += 1
                n -= 1
                bidx = 5 + slot
                bank = pb[bidx]
                bw = []
                if fc == 0:
                    bw = [self.dw[0]] + p.takewaits(("bank", bidx))
                if fc == 8:
                    bw = bw + [self.dw[1]]
                ms = mm(bank[0:120, :512],
                        wbuf[self.par, :, 120 * fc:120 * (fc + 1)],
                        xbf(fc, slot * 512, 512),
                        start=(fc == 0), stop=(fc == NFC - 1),
                        waits=bw,
                        inc="pe" if fc == NFC - 1 else None)
                if fc == NFC - 1:
                    dstq = qbuf[self.par, 0:120, 512 * slot:512 * (slot + 1)]
                    ev = p.em("dve", lambda v, d=dstq, b=bank, hh=self.h:
                              v.tensor_scalar(d, b[0:120, :512],
                                              wstd_c[0:120, hh:hh + 1], 0.0,
                                              ALU.mult, ALU.add),
                              waits=[ms] + p.takewaits(("qbuf", self.par, slot)),
                              inc="dve")
                    p.use(("bank", bidx), ev)
                    self.evq[slot] = ev

        def finish(self):
            self.feed(len(self.seq))
            h, par = self.h, self.par
            p.use(("wbuf", par), p.now("pe"))
            d_t = dma("sync", qbuf[par, 120:128, 0:1024],
                      selq_fm[8 * h:8 * (h + 1), :],
                      waits=p.takewaits(("qbuf_t", par)), inc="dst%d" % par)
            wkv = p.takewaits(("xkv", par)) + ([("cc2", 1), ("cc3", 1)] if h < 2 else [])
            l1 = dma("sync", xkv[par, :, 0:1024],
                     kf_out.ap()[128 * h:128 * (h + 1), :], waits=wkv, inc="dsk%d" % par)
            l2 = dma("sync", xkv[par, :, 1024:2048],
                     kf_out.ap()[H * 128 + 128 * h: H * 128 + 128 * (h + 1), :],
                     inc="dsk%d" % par)
            l3 = dma("sync", xkv[par, :, 2048:4112].rearrange("p (g c) -> p g c", g=16),
                     vv_out.ap().rearrange("(g p) c -> p g c", p=128)[:, :, 129 * h:129 * (h + 1)],
                     inc="dsk%d" % par)
            return [self.evq[0], self.evq[1], d_t, l1, l2, l3]

    SMIN = [0, 1, 2, 3, 0, 1, 2, 3]
    PAIR_LIVE = [[g for g in range(8) if SMIN[g] <= 1],
                 list(range(8))]

    def emit_attn2(h, first_w, feeder=None):
        par = h % 2
        if "qf" in dbg and h == 13:
            dma("sync", dbg["qf"], qbuf[par, :, 0:1024], waits=list(first_w), inc="ds")
        kf = xkv[par, :, 0:2048]
        vv = xkv[par, :, 2048:4112]
        first_w = list(first_w)
        OB = [2, 3, 4, 7]
        prot = 0
        for pairi in range(2):
            base = 512 * pairi
            live = PAIR_LIVE[pairi]
            first_av = [True] * 4
            oww = [p.takewaits(("bank", OB[i])) for i in range(4)]
            seqlist = [(g, k) for g in live for k in range(2)]
            last_touch = {}
            for i in range(4):
                s_i = (4 * pairi + i) // 2
                last_touch[i] = [gk for gk in seqlist if SMIN[gk[0]] <= s_i][-1]
            LEAD = 6
            pmeta = {}
            nseq = len(seqlist)

            def emit_S(t):
                nonlocal first_w, prot
                g, k = seqlist[t]
                w0 = max(256 * SMIN[g], base)
                width = base + 512 - w0
                sb_i = t % 2
                sbank = pb[sb_i]
                msS = mm(sbank[:, :width],
                         kf[:, 256 * g + 128 * k: 256 * g + 128 * (k + 1)],
                         qbuf[par, :, w0:base + 512], True, True,
                         waits=first_w + p.takewaits(("bank", sb_i)),
                         inc="pe")
                first_w = []
                rot = prot
                prot = (prot + 1) % 8
                pdst = pbufW[rot][:, 0:width]
                aE = p.em("act", lambda s, d=pdst, b=sbank, w=width: s.activation(
                    d, b[:, :w], AF.Exp, bias=nshift[:, 0:1], scale=SCALE2),
                    waits=[msS, m_nsh] + p.takewaits(("pbufP", rot)), inc="act")
                p.use(("bank", sb_i), aE)
                lastP = aE
                if SMIN[g] >= base // 256:
                    jm = g // 4
                    moff = ((SMIN[g] * 2 + jm) * 2 + k) * 256
                    lastP = p.em("gp", lambda v, d=pdst, mo=moff: v.tensor_mul(
                        d[:, 0:256], d[:, 0:256], mask_sb[:, mo:mo + 256]),
                        waits=[aE, d_mask], inc="gpc")
                pmeta[t] = (rot, w0, lastP)

            def emit_AV(t):
                g, k = seqlist[t]
                skc = 2 * g + k
                rot, w0, lastP = pmeta.pop(t)
                n_sq = [i for i in range(4) if (4 * pairi + i) // 2 >= SMIN[g]]
                for ii, i in enumerate(n_sq):
                    col = 128 * i + base - w0
                    last_ch = (last_touch[i] == (g, k))
                    msA = mm(pb[OB[i]][:, :129],
                             pbufW[rot][:, col:col + 128],
                             vv[:, 129 * skc:129 * (skc + 1)],
                             start=first_av[i], stop=last_ch,
                             waits=[lastP] + oww[i],
                             inc="pe" if (ii == len(n_sq) - 1 or last_ch) else None)
                    oww[i] = []
                    first_av[i] = False
                    if ii == len(n_sq) - 1:
                        p.use(("pbufP", rot), msA)
                    if last_ch:
                        chunk_last[i] = msA

            chunk_last = {}
            for t in range(min(LEAD, nseq)):
                emit_S(t)
            for t in range(nseq):
                if feeder:
                    feeder.feed(1)
                if t + LEAD < nseq:
                    emit_S(t + LEAD)
                if feeder:
                    feeder.feed(1)
                emit_AV(t)
            for i in range(4):
                msA = chunk_last[i]
                sqg = 4 * pairi + i
                sw = p.takewaits(("seltmp",))
                r1 = p.em("dve", lambda v, b=pb[OB[i]], q=i: v.reciprocal(
                    rec4[q], b[:, 128:129]),
                    waits=[msA] + p.takewaits(("rec", i)), inc="dve")
                a1 = p.em("dve", lambda v, b=pb[OB[i]], q=i: v.tensor_scalar_mul(
                    atm4[q], b[:, 0:128], rec4[q]),
                    waits=[r1] + p.takewaits(("att_tm", i)) + sw, inc="dve")
                p.use(("rec", i), a1)
                tb = OB[i]
                t1 = trans(pb_bf[tb][:, 0:128], atm4[i],
                           waits=[a1, d_ident])
                p.use(("att_tm", i), t1)
                dsta = cdk[:, SC * h + 128 * sqg: SC * h + 128 * sqg + 128]
                cm = p.em("act", lambda s, d=dsta, tb=tb: s.copy(d, pb_bf[tb][:, 0:128]),
                          waits=[t1], inc="act")
                p.use(("bank", tb), cm)
                p.use(("attn2",), cm)
        endm = p.now("pe")
        p.use(("xkv", par), endm)
        for slot in range(2):
            p.use(("qbuf", par, slot), endm)
        p.use(("qbuf_t", par), endm)
        if PH7_SERIAL:
            p.barrier()

    qstd_dma(0)
    qstd_dma(1)
    fw_prev = QstdFeed(0).finish()
    for h in range(16):
        if h + 2 < 16:
            qstd_dma(h + 2)
        feeder = QstdFeed(h + 1) if h + 1 < 16 else None
        emit_attn2(h, fw_prev, feeder)
        fw_prev = feeder.finish() if feeder else None

    p.barrier()

    if "attn2" in dbg:
        dma("sync", dbg["attn2"], cdk[:, :], inc="ds")

    if stop_phase <= 7:
        return _finish(nc, p, sem, stck)

    PHASE_MARKS.append(("PH8", sum(1 for o in p.ops if o[0] == "pe")))
    bcm8 = []
    for slot in range(2):
        dA8 = dma("gp", abc[slot, 0, :, :],
                  rowbuf.ap()[2 * slot:2 * slot + 1, :].partition_broadcast(128),
                  waits=[lr1[slot][0]] + p.takewaits(("abcA", slot)),
                  inc="dgr%d" % (2 * slot))
        dB8 = dma("gp", abc[slot, 1, :, :],
                  rowbuf.ap()[2 * slot + 1:2 * slot + 2, :].partition_broadcast(128),
                  waits=[lr1[slot][1]] + p.takewaits(("abcB", slot)),
                  inc="dgr%d" % (2 * slot + 1))
        bcm8.append((dA8, dB8))
    sm2 = proj_resid_stats(
        wp_i, lambda ec, slot: cdk[:, SC * ec + 512 * slot: SC * ec + 512 * (slot + 1)],
        None,
        lambda jc, slot: zbuf.ap()[128 * jc:128 * (jc + 1),
                                   512 * slot:512 * (slot + 1)],
        x1buf.ap(),
        bft3=[zbft[0], zbft[1], kvs[1]],
        sqt3=[zsqt[0], zsqt[1], kvs[2]],
        xfix=(abc, bcm8))
    lr2, _, _ = ln_rows(4, sm2)
    out_ms = ln_normalize(4, lr2, out_t, x1buf.ap())

    return _finish(nc, p, sem, stck)


def _finish(nc, p, sem, stck):
    p.barrier(exclude=())
    p.em("sync", lambda e: e.nop())
    p.em("gp", lambda e: e.nop())
    p.em("pe", lambda e: e.nop())
    p.em("dve", lambda v: v.nop())
    p.em("act", lambda s: s.nop())

    with nc.Block() as block:
        def runner(engname):
            def _(eng):
                seen = {}
                for (e, fn, waits, inc, amt) in p.ops:
                    if e != engname:
                        continue
                    best = {}
                    for (s, v) in waits:
                        if v > 0 and v > seen.get(s, 0) and v > best.get(s, 0):
                            best[s] = v
                    for (s, v) in best.items():
                        eng.wait_ge(sem[s], v)
                        seen[s] = v
                    instr = fn(eng)
                    if inc is not None:
                        if inc.startswith("cc"):
                            instr.then_inc(sem[inc])
                        elif inc in ("pe", "dve", "act", "gpc"):
                            instr.then_inc(sem[inc], 1)
                        else:
                            instr.then_inc(sem[inc], 16)
            return _
        block.tensor(runner("pe"))
        block.vector(runner("dve"))
        block.scalar(runner("act"))
        block.gpsimd(runner("gp"))
        block.sync(runner("sync"))
    stck.close()
    return nc


def _host_inputs(inputs):
    f = np.float32
    wq = np.asarray(inputs["L0_attn_wq"], f)
    wk = np.asarray(inputs["L0_attn_wk"], f)
    wv = np.asarray(inputs["L0_attn_wv"], f)
    wo = np.asarray(inputs["L0_attn_wo"], f)
    Wa = np.asarray(inputs["L0_ffn_Wa"], f)
    Wp = np.asarray(inputs["L0_ffn_Wp"], f)
    def _img_bias(M, brow):
        pieces = []
        for stt, w in W16:
            blk = np.ascontiguousarray(M[:, stt:stt + w])
            im = blk.reshape(16, 128, w).transpose(1, 0, 2).reshape(128, 16 * w)
            bb = np.zeros((128, 128), np.float32)
            bb[0, :] = brow[stt:stt + w]
            pieces.append(np.concatenate([im, bb], axis=1))
        return np.ascontiguousarray(np.concatenate(pieces, axis=1))

    shared = dict(
        wq_i=_img_panels(np.ascontiguousarray(wq.T), W16),
        wk_i=_img_bias(np.ascontiguousarray(wk.T), np.asarray(inputs["L0_attn_bk"], f)),
        wv_i=_img_bias(np.ascontiguousarray(wv.T), np.asarray(inputs["L0_attn_bv"], f)),
        wo_i=_img_panels(np.ascontiguousarray(wo.T), W16),
        wp_i=_img_panels(Wp, W16),
        wa_i=_img_panels(Wa, WA_RANGES),
        bq_c=np.ascontiguousarray(np.asarray(inputs["L0_attn_bq"], f).reshape(16, 128).T),
        bo_c=np.ascontiguousarray(np.asarray(inputs["L0_attn_bo"], f).reshape(16, 128).T),
        bk_r=np.asarray(inputs["L0_attn_bk"], f).reshape(1, D),
        bv_r=np.asarray(inputs["L0_attn_bv"], f).reshape(1, D),
        wsr_i=np.concatenate([np.asarray(inputs["L0_ffn_w_std"], f),
                              np.asarray(inputs["L0_ffn_w_rec"], f)]).reshape(1, 32),
        r01_i=(np.arange(128)[None, :] // 8 == np.arange(16)[:, None]).astype(f),
        ident_i=np.eye(128, dtype=BF),
    )
    x = np.asarray(inputs["L0_attn_x"], f)
    per_core = []
    for c in range(8):
        b, parity = c // 2, c % 2
        xc = np.ascontiguousarray(x[b][_own_idx(parity)])
        xi = _img_panels(np.ascontiguousarray(xc.T), [(0, SC)])
        masks = _masks_for(parity).transpose(3, 0, 1, 2, 4).reshape(128, 16 * 256)
        per_core.append(dict(shared, x_img=xi, masks_i=np.ascontiguousarray(masks)))
    return per_core


def kernel(**inputs):
    per_core = _host_inputs(inputs)
    nc = build_program()
    res = run_bass_kernel_spmd(nc, per_core, core_ids=list(range(8)))
    x = np.asarray(inputs["L0_attn_x"])
    out = np.zeros((4, 2048, D), dtype=np.float32)
    for c in range(8):
        b, parity = c // 2, c % 2
        o = np.asarray(res.results[c]["out_fm"])
        out[b, _own_idx(parity)] = np.ascontiguousarray(o.T)
    return out


# version 19
# speedup vs baseline: 1.0074x; 1.0074x over previous
import contextlib
import numpy as np
import ml_dtypes

import concourse.bass as bass
import concourse.mybir as mybir
from concourse.bass_utils import run_bass_kernel_spmd

F32 = mybir.dt.float32
BF16 = mybir.dt.bfloat16
AF = mybir.ActivationFunctionType
ALU = mybir.AluOpType
BF = ml_dtypes.bfloat16

D = 2048
H = 16
NFC = 16
SC = 1024
CH = 256
OWN = [[0, 3, 4, 7], [1, 2, 5, 6]]
GLOB = OWN[0] + OWN[1]
LIVE = [sorted(g for g in range(8)
               if GLOB[g] <= OWN[0][s] or GLOB[g] <= OWN[1][s]) for s in range(4)]
EXP_SHIFT = 20.0
SCALE2 = float(1.0 / np.sqrt(np.float32(128.0)))
QSTD0, KSTD0, V0, LOW0 = 0, 1920, 3840, 5888
W16 = [(i * 128, 128) for i in range(16)]
WA_RANGES = ([(KSTD0 + 120 * i, 120) for i in range(16)] +
             [(LOW0, 256)] +
             [(V0 + 128 * i, 128) for i in range(16)] +
             [(QSTD0 + 120 * i, 120) for i in range(16)])
WA_OFF = np.cumsum([0] + [16 * w for _, w in WA_RANGES]).tolist()
GROUPS = [[0, 1], [2, 3], [4, 5], [6, 7]]


def _own_idx(parity):
    return np.concatenate([np.arange(CH) + c * CH for c in OWN[parity]])


def _img_panels(M, ranges):
    pieces = []
    for stt, w in ranges:
        blk = np.ascontiguousarray(M[:, stt:stt + w])
        pieces.append(blk.reshape(16, 128, w).transpose(1, 0, 2).reshape(128, 16 * w))
    return np.ascontiguousarray(np.concatenate(pieces, axis=1))


def _masks_for(parity):
    m = np.zeros((4, 2, 2, 128, 256), dtype=BF)
    for s in range(4):
        for j, g in enumerate((s, s + 4)):
            kg = GLOB[g] * CH + np.arange(CH)
            qg = OWN[parity][s] * CH + np.arange(CH)
            full = (kg[:, None] <= qg[None, :]).astype(np.float32)
            m[s, j, 0] = full[:128]
            m[s, j, 1] = full[128:]
    return m


class Views:

    def __init__(self, views):
        self.views = views

    def __getitem__(self, key):
        if isinstance(key, tuple):
            v = self.views[key[0]]
            rest = key[1:]
            if not rest:
                return v
            return v[rest if len(rest) > 1 else rest[0]]
        return self.views[key]


class Plan:
    def __init__(self):
        self.ops = []
        self.cnt = {}
        self.pending = {}
        self.last = {}

    def sem(self, n):
        self.cnt.setdefault(n, 0)

    def em(self, eng, fn, waits=(), inc=None, amt=1):
        w = list(self.pending.pop(eng, ())) + [x for x in waits if x is not None]
        self.ops.append((eng, fn, w, inc, amt))
        if inc is not None:
            self.cnt[inc] = self.cnt.get(inc, 0) + (amt if amt else 1)
            return (inc, self.cnt[inc])
        return None

    def now(self, s):
        return (s, self.cnt.get(s, 0))

    def barrier(self, exclude=("cc1", "cc2", "cc3")):
        snap = [(s, c) for s, c in self.cnt.items() if c > 0 and s not in exclude]
        for e in ("pe", "dve", "act", "gp", "sync"):
            self.pending[e] = snap

    def use(self, key, *ms):
        cur = self.last.setdefault(key, {})
        for m in ms:
            if m is not None:
                s, v = m
                cur[s] = max(cur.get(s, 0), v)

    def takewaits(self, key):
        d = self.last.pop(key, {})
        return list(d.items())


PH1_LIMIT = 16
PH0_PATTERNS = True
PH1_BIAS = True
PH1_EVAC = True
PH7_SERIAL = False

PHASE_MARKS = []


def build_program(stop_phase=99, debug_taps=()):
    PHASE_MARKS.clear()
    nc = bass.Bass("TRN2", target_bir_lowering=False, debug=False)
    p = Plan()
    SEM_NAMES = ("pe", "dve", "act", "dg", "ds", "cc1", "cc2", "cc3",
                 "dgw0", "dgw1", "dgw2", "dgh0", "dgh1", "dgh2",
                 "dgi0", "dgi1", "dgi2", "dgj0", "dgj1", "dgj2", "dgb", "dkv",
                 "dgr0", "dgr1", "dgr2", "dgr3",
                 "dsx0", "dsx1", "dsx2", "dsx3", "dsz0", "dsz1", "dsz2", "dsr", "dsra",
                 "dst0", "dst1", "dsk0", "dsk1", "dckv", "dkf", "dvv",
                 "dgl", "dskt", "dsm", "dsc", "gpc")
    for n in SEM_NAMES:
        p.sem(n)
    WQ_SEMS = ("dgw%d", "dgh%d", "dgi%d", "dgj%d")

    x_img = nc.dram_tensor("x_img", [128, NFC * SC], F32, kind="ExternalInput").ap()
    wq_i = nc.dram_tensor("wq_i", [128, 8 * 4096], F32, kind="ExternalInput").ap()
    wk_i = nc.dram_tensor("wk_i", [128, 16 * 2176], F32, kind="ExternalInput").ap()
    wv_i = nc.dram_tensor("wv_i", [128, 16 * 2176], F32, kind="ExternalInput").ap()
    wo_i = nc.dram_tensor("wo_i", [128, 8 * 4096], F32, kind="ExternalInput").ap()
    wp_i = nc.dram_tensor("wp_i", [128, 8 * 4096], F32, kind="ExternalInput").ap()
    wa_i = nc.dram_tensor("wa_i", [128, WA_OFF[-1]], F32, kind="ExternalInput").ap()
    bq_c = nc.dram_tensor("bq_c", [128, 16], F32, kind="ExternalInput").ap()
    bo_c = nc.dram_tensor("bo_c", [128, 16], F32, kind="ExternalInput").ap()
    bk_r = nc.dram_tensor("bk_r", [1, D], F32, kind="ExternalInput").ap()
    bv_r = nc.dram_tensor("bv_r", [1, D], F32, kind="ExternalInput").ap()
    wsr_i = nc.dram_tensor("wsr_i", [1, 32], F32, kind="ExternalInput").ap()
    r01_i = nc.dram_tensor("r01_i", [16, 128], F32, kind="ExternalInput").ap()
    ident_i = nc.dram_tensor("ident_i", [128, 128], BF16, kind="ExternalInput").ap()
    masks_i = nc.dram_tensor("masks_i", [128, 16 * 256], BF16, kind="ExternalInput").ap()
    out_t = nc.dram_tensor("out_fm", [D, SC], F32, kind="ExternalOutput").ap()

    kv_in = nc.dram_tensor("kv_in", [128, H * 129], F32)
    kv_out = nc.dram_tensor("kv_out", [128, H * 129], F32)
    kf_in = nc.dram_tensor("kf_in", [H * 128, SC], BF16)
    kf_out = nc.dram_tensor("kf_out", [2 * H * 128, SC], BF16)
    vv_in = nc.dram_tensor("vv_in", [8 * 128, H * 129], BF16)
    vv_out = nc.dram_tensor("vv_out", [16 * 128, H * 129], BF16)
    zbuf = nc.dram_tensor("zbuf", [D, SC], F32)
    x1buf = nc.dram_tensor("x1buf", [D, SC], F32)
    rowbuf = nc.dram_tensor("rowbuf", [48, 512], F32)

    dbg = {}
    for name in debug_taps:
        shp = {"kv": [128, H * 129], "x1": [D, SC], "z": [D, SC],
               "kf": [2 * H * 128, SC], "vv": [16 * 128, H * 129],
               "attn2": [128, NFC * SC], "qf": [128, SC]}[name]
        dt = BF16 if name in ("kf", "vv", "attn2", "qf") else F32
        dbg[name] = nc.dram_tensor("dbg_" + name, shp, dt, kind="ExternalOutput").ap()

    stck = contextlib.ExitStack()
    sem = {n: stck.enter_context(nc.semaphore(n)) for n in SEM_NAMES}

    A = nc.alloc_sbuf_tensor

    def vsplit(ap, n, w):
        return Views([ap[:, i * w:(i + 1) * w] for i in range(n)])

    arena1 = A("arena1", [128, NFC * SC], BF16)
    qbuf_t = A("qbuf", [128, 2 * 2048], BF16)
    qbuf = vsplit(qbuf_t[:, :], 2, 2048)
    cdk = A("cdk", [128, H * SC], BF16)
    cdv = A("cdv", [128, 8 * H * 129], BF16)
    wbuf_t = A("wbuf", [128, 2 * 2176], BF16)
    wbuf = vsplit(wbuf_t[:, :], 2, 2176)
    xkv_t = A("xkv", [128, 2 * 4112], BF16)
    xkv = vsplit(xkv_t[:, :], 2, 4112)
    wpan = [wbuf[0], wbuf[1], xkv[0]]

    def wkey(si):
        return ("wbuf", si) if si < 2 else ("xkv", 0)
    kvbf_u = A("kvbf_u", [128, 2064], BF16)
    kvbf = kvbf_u[:, :]
    selq_fm = kvbf_u[:, 0:1024]
    selk_fm = kvbf_u[:, 1024:2048]
    kvs = Views([kvbf_u[:, i * 512:(i + 1) * 512] for i in range(4)])
    u1 = A("u1", [128, 2048], F32)
    trow = vsplit(u1[:, :], 2, 512)
    trow8 = vsplit(u1[:, :], 8, 256)
    x0t = vsplit(u1[:, :], 4, 512)
    mask_sb = u1[:, :].bitcast(BF16)
    u2 = A("u2", [128, 1536], F32)
    zt = vsplit(u2[:, :], 3, 512)
    pbufP = vsplit(u2[:, :].bitcast(BF16), 4, 256)
    u3 = A("u3", [128, 2048], F32)
    rbc = vsplit(u3[:, :], 4, 512)
    abc = Views([vsplit(u3[:, 0:1024], 2, 512), vsplit(u3[:, 1024:2048], 2, 512)])
    pbufW = vsplit(u3[:, :].bitcast(BF16), 8, 512)
    u4 = A("u4", [128, 1024], F32)
    u4b = u4[:, :].bitcast(BF16)
    zbft = vsplit(u4b, 2, 512)
    zsqt = Views([u4b[:, 1024 + i * 512:1024 + (i + 1) * 512] for i in range(2)])
    selq_tm = Views([u4b[:, i * 128:(i + 1) * 128] for i in range(8)])
    selk_tm = Views([u4b[:, 1024 + i * 128:1024 + (i + 1) * 128] for i in range(8)])
    u5 = A("u5", [128, 392], F32)
    kvst = vsplit(u5[:, 0:258], 2, 129)
    seltmp = vsplit(u5[:, 0:256], 2, 128)
    att_tm = Views([u5[:, 256 + i * 64:256 + (i + 1) * 64].bitcast(BF16) for i in range(2)])
    rec_t = Views([u5[:, 384 + i:385 + i] for i in range(2)])
    atm4 = Views([u5[:, 64 * i:64 * (i + 1)].bitcast(BF16) for i in range(4)])
    rec4 = Views([u5[:, 384 + i:385 + i] for i in range(4)])
    rows = A("rows", [1, 1024], F32)
    rows2 = A("rows2", [1, 1024], F32)
    patrow = rows2[0:1, 0:512]
    rows_bf = A("rows_bf", [1, 128], BF16)
    onesb_r = A("onesb_r", [1, 128], BF16)
    onesb_c = A("onesb_c", [128, 1], BF16)
    onesf_r = A("onesf_r", [1, 128], F32)
    eps_t = A("eps_t", [1, 1], F32)
    eps5_t = A("eps5_t", [1, 1], F32)
    nshift = A("nshift", [128, 1], F32)
    wrow = A("wrow", [1, 32], F32)
    wclip = A("wclip", [1, 32], F32)
    urow = A("urow", [1, 64], F32)
    patcol = A("patcol", [16, 4], F32)
    pat_t = A("pat", [128, 512], BF16)
    pat = vsplit(pat_t[:, :], 4, 128)
    wstd_c = A("wstd_c", [128, 16], F32)
    bqc_sb = A("bqc_sb", [128, 16], F32)
    boc_sb = A("boc_sb", [128, 16], F32)
    r01_sb = A("r01_sb", [16, 128], F32)
    ident = A("ident", [128, 128], BF16)

    pb = [nc.alloc_psum_tensor(f"pb{i}", [128, 512], F32) for i in range(8)]
    pb_bf = [b[:, :].bitcast(BF16) for b in pb]

    def mm(out_ap, lhsT, rhs, start, stop, waits=(), inc=None):
        return p.em("pe", lambda t: t.matmul(out_ap, lhsT, rhs, start=start, stop=stop),
                    waits=waits, inc=inc)

    def trans(out_ap, in_ap, waits=(), inc="pe"):
        return p.em("pe", lambda t: t.transpose(out_ap, in_ap, ident[:, :]),
                    waits=waits, inc=inc)

    def dma(q, out_ap, in_ap, waits=(), inc=None):
        return p.em(q, lambda e: e.dma_start(out=out_ap, in_=in_ap),
                    waits=waits, inc=inc, amt=16)

    def coll(kind, op, ins, outs, waits, inc):
        def fn(e):
            return e.collective_compute(kind, op, replica_groups=GROUPS, ins=ins, outs=outs)
        return p.em("gp", fn, waits=waits, inc=inc, amt=1)

    def xbf(fc, c0, w):
        return arena1[:, SC * fc + c0: SC * fc + c0 + w]

    dma("sync", ident[:, :], ident_i, inc="dsc")
    dma("sync", bqc_sb[:, :], bq_c, inc="dsc")
    dma("sync", boc_sb[:, :], bo_c, inc="dsc")
    dma("sync", wrow[:, :], wsr_i, inc="dsc")
    d_r01 = dma("sync", r01_sb[:, :], r01_i, inc="dsc")
    d_ident = d_bq = d_bo = d_wsr = d_r01 = p.now("dsc")
    d_x = dma("gp", arena1[:, :], x_img, inc="dg")

    p.em("dve", lambda v: v.memset(onesb_r[:, :], 1.0), inc="dve")
    p.em("dve", lambda v: v.memset(onesb_c[:, :], 1.0), inc="dve")
    p.em("dve", lambda v: v.memset(onesf_r[:, :], 1.0), inc="dve")
    m_eps = p.em("dve", lambda v: v.memset(eps_t[:, :], 1e-8), inc="dve")
    m_eps5 = p.em("dve", lambda v: v.memset(eps5_t[:, :], 1e-5), inc="dve")
    m_nsh = p.em("dve", lambda v: v.memset(nshift[:, :], -EXP_SHIFT), inc="dve")
    for sc in range(8):
        view = cdv[:, sc * (H * 129):(sc + 1) * (H * 129)].rearrange(
            "p (h c) -> p h c", h=H)[:, :, 128:129]
        p.em("dve", lambda v, vw=view: v.memset(vw, 1.0), inc="dve")
    mones = p.now("dve")

    mc = p.em("dve", lambda v: v.tensor_scalar_max(wclip[:, :], wrow[:, :], 1e-8),
              waits=[d_wsr], inc="dve")
    psub = p.em("dve", lambda v: v.tensor_scalar_sub(urow[0:1, 0:16], wrow[0:1, 16:32], 0.1),
                waits=[d_wsr], inc="dve")
    asg = p.em("act", lambda s: s.activation(urow[0:1, 0:16], urow[0:1, 0:16], AF.Sign,
                                             bias=0.0, scale=1.0),
               waits=[psub], inc="act")
    m_u = p.em("dve", lambda v: v.tensor_scalar_max(urow[0:1, 0:16], urow[0:1, 0:16], 0.0),
               waits=[asg], inc="dve")
    m_a = p.em("dve", lambda v: v.tensor_mul(urow[0:1, 16:32], urow[0:1, 0:16],
                                             wclip[0:1, 16:32]),
               waits=[m_u, mc], inc="dve")
    m_b = p.em("dve", lambda v: v.tensor_sub(urow[0:1, 32:48], wclip[0:1, 16:32],
                                             urow[0:1, 16:32]),
               waits=[m_a], inc="dve")
    m_n = p.em("dve", lambda v: v.tensor_scalar(urow[0:1, 48:64], urow[0:1, 0:16],
                                                -1.0, 1.0, ALU.mult, ALU.add),
               waits=[m_u], inc="dve")
    mrows = p.now("dve")

    srcrow = {0: 16, 1: 32, 2: 0, 3: 48}
    for i in range(4 if PH0_PATTERNS else 0):
        r_ = srcrow[i]
        m1 = mm(pb[0][0:16, 0:1], urow[0:1, r_:r_ + 16], onesf_r[0:1, 0:1], True, True,
                waits=[mrows, mones] + p.takewaits(("bank", 0)), inc="pe")
        c1 = p.em("dve", lambda v, i=i: v.tensor_copy(patcol[:, i:i + 1], pb[0][0:16, 0:1]),
                  waits=[m1], inc="dve")
        p.use(("bank", 0), c1)
        m2 = mm(pb[1][0:1, 0:128], patcol[:, i:i + 1], r01_sb[:, :], True, True,
                waits=[c1, d_r01] + p.takewaits(("bank", 1)), inc="pe")
        c2 = p.em("dve", lambda v, i=i: v.tensor_copy(patrow[0:1, i * 128:(i + 1) * 128],
                                                      pb[1][0:1, 0:128]),
                  waits=[m2], inc="dve")
        p.use(("bank", 1), c2)
        m3 = mm(pb[2][:, 0:128], onesf_r[0:1, :], patrow[0:1, i * 128:(i + 1) * 128],
                True, True, waits=[c2] + p.takewaits(("bank", 2)), inc="pe")
        c3 = p.em("dve", lambda v, i=i: v.tensor_copy(pat[i], pb[2][:, 0:128]),
                  waits=[m3], inc="dve")
        p.use(("bank", 2), c3)
    if PH0_PATTERNS:
        m4 = mm(pb[3][:, 0:16], onesf_r[0:1, :], wclip[0:1, 0:16], True, True,
                waits=[mc, mones] + p.takewaits(("bank", 3)), inc="pe")
        c4 = p.em("dve", lambda v: v.tensor_copy(wstd_c[:, :], pb[3][:, 0:16]),
                  waits=[m4], inc="dve")
        p.use(("bank", 3), c4)
    mpat = p.now("dve")

    p.barrier()

    if stop_phase <= 0:
        return _finish(nc, p, sem, stck)

    PHASE_MARKS.append(("PH1", sum(1 for o in p.ops if o[0] == "pe")))
    def tm_proj_b1(img_ap, bias_row, is_k):
        dws = [dma("gp", wbuf[0, :, :], img_ap[:, 0:2176],
                   waits=p.takewaits(("wbuf", 0)), inc="dgw0"),
               dma("gp", wbuf[1, :, :], img_ap[:, 2176:4352],
                   waits=p.takewaits(("wbuf", 1)), inc="dgw1")]
        for hidx in range(PH1_LIMIT):
            par = hidx % 2
            for wave in range(2):
                mss = []
                for i in range(4):
                    sc = wave * 4 + i
                    bidx = 4 * wave + i
                    bank = pb[bidx]
                    bw = [dws[hidx], d_x] + p.takewaits(("bank", bidx))
                    for fc in range(NFC):
                        mm(bank[:, :128], xbf(fc, sc * 128, 128),
                           wbuf[par, :, 128 * fc:128 * (fc + 1)],
                           start=(fc == 0), stop=False,
                           waits=bw if fc == 0 else [])
                        bw = []
                    ms = mm(bank[:, :128], onesb_r[0:1, :],
                            wbuf[par, 0:1, 2048:2176],
                            start=False, stop=True,
                            waits=[mones], inc="pe")
                    mss.append(ms)
                for i in range(4 if PH1_EVAC else 0):
                    sc = wave * 4 + i
                    bidx = 4 * wave + i
                    bank = pb[bidx]
                    if is_k:
                        pr = i % 4
                        t_t = trow8[2 * pr][:, :128]
                        t_r = trow8[2 * pr + 1][:, :128]
                        a1 = p.em("act", lambda s, b=bank, t=t_t: s.activation(
                            t, b[:, :128], AF.Exp, bias=0.0, scale=1.0),
                            waits=[mss[i]] + p.takewaits(("trow8", pr)), inc="act")
                        a2 = p.em("dve", lambda v, b=bank, t=t_r: v.tensor_scalar_max(
                            t, b[:, :128], 0.0),
                            waits=[mss[i], a1], inc="dve")
                        dst = cdk[:, 2048 * sc + 128 * hidx: 2048 * sc + 128 * (hidx + 1)]
                        a3 = p.em("dve", lambda v, d=dst, t1=t_t, t2=t_r:
                                  v.scalar_tensor_tensor(d, t1, 1.0, t2, ALU.min, ALU.add),
                            waits=[a1, a2], inc="dve")
                        p.use(("trow8", pr), a3)
                        p.use(("bank", bidx), a1, a2)
                        p.use(("cdk_w",), a3)
                    else:
                        dstv = cdv[:, sc * (H * 129) + 129 * hidx:
                                   sc * (H * 129) + 129 * hidx + 128]
                        a1 = p.em("act", lambda s, d=dstv, b=bank: s.copy(d, b[:, :128]),
                                  waits=[mss[i]], inc="act")
                        p.use(("bank", bidx), a1)
                        p.use(("cdv_w",), a1)
            p.use(("wbuf", par), p.now("pe"))
            if hidx + 2 < 16:
                par2 = (hidx + 2) % 2
                dws.append(dma("gp", wbuf[par2, :, :],
                               img_ap[:, (hidx + 2) * 2176:(hidx + 3) * 2176],
                               waits=p.takewaits(("wbuf", par2)),
                               inc="dgw%d" % par2))

    tm_proj_b1(wk_i, bk_r, True)
    if stop_phase <= 1:
        return _finish(nc, p, sem, stck)
    PHASE_MARKS.append(("PH2", sum(1 for o in p.ops if o[0] == "pe")))
    tm_proj_b1(wv_i, bv_r, False)
    if stop_phase <= 2:
        return _finish(nc, p, sem, stck)

    PHASE_MARKS.append(("PH3", sum(1 for o in p.ops if o[0] == "pe")))
    kq = p.takewaits(("cdk_w",)) + p.takewaits(("cdv_w",))
    kv_ev = []
    for h in range(H):
        bidx = h % 2
        bank = pb[bidx]
        bw = kq + p.takewaits(("bank", bidx))
        kq = []
        ms = None
        for sc in range(8):
            ms = mm(bank[:, :129],
                    cdk[:, 2048 * sc + 128 * h: 2048 * sc + 128 * (h + 1)],
                    cdv[:, sc * H * 129 + 129 * h: sc * H * 129 + 129 * (h + 1)],
                    start=(sc == 0), stop=(sc == 7),
                    waits=bw if sc == 0 else [], inc="pe" if sc == 7 else None)
            bw = []
        ev = p.em("act", lambda s, b=bank, h=h: s.copy(kvst[h % 2], b[:, :129]),
                  waits=[ms] + p.takewaits(("kvst", h % 2)), inc="act")
        p.use(("bank", bidx), ev)
        d_h = dma("sync", kv_in.ap()[:, 129 * h:129 * (h + 1)], kvst[h % 2],
                  waits=[ev], inc="dckv")
        p.use(("kvst", h % 2), d_h)
        kv_ev.append(d_h)
    coll("AllReduce", ALU.add, [kv_in.ap()], [kv_out.ap()], waits=kv_ev, inc="cc1")

    if "kv" in dbg:
        dma("sync", dbg["kv"], kv_out.ap(), waits=[("cc1", 1)], inc="ds")

    if stop_phase <= 3:
        return _finish(nc, p, sem, stck)

    PHASE_MARKS.append(("PH4", sum(1 for o in p.ops if o[0] == "pe")))
    nrow = 8

    def emit_qproj(h):
        par = h % 2
        wsi = h % 3
        dw = dwq[h]
        ev_q = {}
        mss = {}
        for slot in range(2):
            bidx = (2 * h + slot) % 4
            bank = pb[bidx]
            bw = [dw[0], d_x] + p.takewaits(("bank", bidx))
            ms = None
            for fc in range(NFC):
                if fc in (4, 8, 12):
                    bw = bw + [dw[fc // 4]]
                ms = mm(bank[:, :512],
                        wpan[wsi][:, 128 * fc:128 * (fc + 1)],
                        xbf(fc, slot * 512, 512),
                        start=(fc == 0), stop=(fc == NFC - 1),
                        waits=bw,
                        inc="pe" if fc == NFC - 1 else None)
                bw = []
            mss[slot] = ms
        p.use(wkey(wsi), p.now("pe"))
        for slot in range(2):
            bidx = (2 * h + slot) % 4
            bank = pb[bidx]
            a1 = p.em("act", lambda s, b=bank, h=h: s.activation(
                trow[0][:, :512], b[:, :512], AF.Exp,
                bias=bqc_sb[:, h:h + 1], scale=1.0),
                waits=[mss[slot], d_bq] + p.takewaits(("trow", 0)), inc="act")
            a2 = p.em("dve", lambda v, b=bank, h=h: v.tensor_scalar(
                trow[1][:, :512], b[:, :512], bqc_sb[:, h:h + 1], 0.0,
                ALU.add, ALU.max),
                waits=[mss[slot], d_bq, a1], inc="dve")
            dstq = qbuf[par, :, 512 * slot:512 * (slot + 1)]
            a3 = p.em("dve", lambda v, d=dstq: v.scalar_tensor_tensor(
                d, trow[0][:, :512], 1.0, trow[1][:, :512], ALU.min, ALU.add),
                waits=[a1, a2] + p.takewaits(("qbuf", par, slot)), inc="dve")
            p.use(("trow", 0), a3)
            p.use(("trow", 1), a3)
            p.use(("bank", bidx), a1, a2)
            ev_q[slot] = a3
        return ev_q

    R4 = [rows[0:1, 0:512], rows[0:1, 512:1024],
          rows2[0:1, 0:512], rows2[0:1, 512:1024]]
    pend_n = []

    def attn1_stage1(h, ev_q):
        par = h % 2
        for slot in range(2):
            i2 = 2 * h + slot
            ridx = i2 % 4
            qsl = qbuf[par, :, 512 * slot:512 * (slot + 1)]
            b_u = pb[4 + slot]
            msu = mm(b_u[:, :512], kvbf[:, 129 * h:129 * h + 128], qsl, True, True,
                     waits=[ev_q[slot], d_kvbf] + p.takewaits(("bank", 4 + slot)),
                     inc="pe")
            msd = mm(pb[6][0:1, :512], kvbf[:, 129 * h + 128:129 * h + 129], qsl,
                     True, True, waits=p.takewaits(("bank", 6)), inc="pe")
            p.use(("qbuf", par, slot), msu, msd)
            a_d = p.em("act", lambda s, r=R4[ridx]: s.activation(
                r, pb[6][0:1, :512], AF.Identity,
                bias=eps_t[0:1, 0:1], scale=1.0),
                waits=[msd, m_eps] + p.takewaits(("rowsA", ridx)), inc="act")
            p.use(("bank", 6), a_d)
            r_d = p.em("dve", lambda v, r=R4[ridx]: v.reciprocal(r, r),
                       waits=[a_d], inc="dve")
            dsta = cdk[:, 1024 * h + 512 * slot: 1024 * h + 512 * (slot + 1)]
            a_u = p.em("act", lambda s, d=dsta, b=b_u: s.copy(d, b[:, :512]),
                       waits=[msu], inc="act")
            p.use(("bank", 4 + slot), a_u)
            pend_n.append((ridx, r_d, a_u, dsta))

    def attn1_stage2():
        while pend_n:
            ridx, r_d, a_u, dsta = pend_n.pop(0)
            bb = 7 - (ridx % 2)
            msb = mm(pb[bb][:, :512], onesf_r[0:1, :], R4[ridx], True, True,
                     waits=[r_d] + p.takewaits(("bank", bb)), inc="pe")
            t_m = p.em("dve", lambda v, d=dsta, bb=bb: v.tensor_mul(d, d, pb[bb][:, :512]),
                       waits=[a_u, msb], inc="dve")
            p.use(("bank", bb), t_m)
            p.use(("rowsA", ridx), msb)
            p.use(("attn1",), t_m)

    dwq = {}

    def qproj_dma(h):
        wsi = h % 3
        ww = p.takewaits(wkey(wsi))
        ms4 = []
        for q in range(4):
            ms4.append(dma("gp", wpan[wsi][:, 512 * q:512 * (q + 1)],
                           wq_i[:, h * 2048 + 512 * q:h * 2048 + 512 * (q + 1)],
                           waits=ww, inc=WQ_SEMS[q] % wsi))
            ww = []
        dwq[h] = ms4

    qproj_dma(0)
    qproj_dma(1)
    qproj_dma(2)
    d_kvbf = dma("gp", kvbf[:, :], kv_out.ap(), waits=[("cc1", 1)], inc="dkv")
    ev_prev = emit_qproj(0)
    for h in range(16):
        if h + 3 < 16:
            qproj_dma(h + 3)
        attn1_stage2()
        attn1_stage1(h, ev_prev)
        if h + 1 < 16:
            ev_prev = emit_qproj(h + 1)
    attn1_stage2()

    p.barrier()

    PHASE_MARKS.append(("PH5", sum(1 for o in p.ops if o[0] == "pe")))
    def proj_resid_stats(img_ap, rhs_fn, bias_col, xres_fn, zdst_ap,
                         bf_fn=None, bft3=None, sqt3=None, xfix=None):
        pend = []
        stat_started = [False, False]
        stat_sl = [[], []]

        def flush_stats(keep=0):
            while len(pend) > keep:
                jc, slot, zb_ms, zq_ms, zrot, s1rhs, trk = pend.pop(0)
                b_s, b_q = pb[4 + slot], pb[6 + slot]
                first = not stat_started[slot]
                w1 = [zb_ms] + (p.takewaits(("bank", 4 + slot)) if first else [])
                w2 = [zq_ms] + (p.takewaits(("bank", 6 + slot)) if first else [])
                s1 = mm(b_s[0:1, :512], onesb_c[:, 0:1], s1rhs,
                        start=first, stop=(jc == NFC - 1), waits=w1, inc="pe")
                s2 = mm(b_q[0:1, :512], onesb_c[:, 0:1], sqt3[zrot],
                        start=first, stop=(jc == NFC - 1), waits=w2, inc="pe")
                stat_started[slot] = True
                if trk:
                    p.use(("zbft8", zrot), s1)
                p.use(("zsqt8", zrot), s2)
                if jc == NFC - 1:
                    stat_sl[slot] = [s1, s2]

        def w_dma(jc):
            si2 = jc % 3
            ww = p.takewaits(wkey(si2))
            ms4 = []
            for q in range(4):
                ms4.append(dma("gp", wpan[si2][:, 512 * q:512 * (q + 1)],
                               img_ap[:, jc * 2048 + 512 * q:
                                      jc * 2048 + 512 * (q + 1)],
                               waits=ww, inc=WQ_SEMS[q] % si2))
                ww = []
            return ms4

        dws = [w_dma(0), w_dma(1), w_dma(2)]
        for jc in range(16):
            par = jc % 3
            for slot in range(2):
                idx = 2 * jc + slot
                bidx = slot
                bank = pb[bidx]
                xrot = idx % 4
                d_x0 = dma("sync", x0t[xrot, :, :], xres_fn(jc, slot),
                           waits=p.takewaits(("x0t", xrot)), inc="dsx%d" % xrot)
                if xfix is not None:
                    axc, bcm8 = xfix
                    c1 = p.em("dve", lambda v, xr=xrot, s=slot: v.tensor_mul(
                        x0t[xr, :, :], x0t[xr, :, :], axc[s, 0, :, :]),
                        waits=[d_x0, bcm8[slot][0]], inc="dve")
                    c2 = p.em("dve", lambda v, xr=xrot, s=slot: v.tensor_add(
                        x0t[xr, :, :], x0t[xr, :, :], axc[s, 1, :, :]),
                        waits=[c1, bcm8[slot][1]], inc="dve")
                    p.use(("abcA", slot), c1)
                    p.use(("abcB", slot), c2)
                    xr_ms = c2
                else:
                    xr_ms = d_x0
                bw = [dws[jc][0]] + p.takewaits(("bank", bidx))
                ms = None
                for ec in range(NFC):
                    if ec in (4, 8, 12):
                        bw = bw + [dws[jc][ec // 4]]
                    ms = mm(bank[:, :512],
                            wpan[par][:, 128 * ec:128 * (ec + 1)],
                            rhs_fn(ec, slot),
                            start=(ec == 0), stop=(ec == NFC - 1),
                            waits=bw,
                            inc="pe" if ec == NFC - 1 else None)
                    bw = []
                zr = idx % 3
                if bias_col is not None:
                    ev = p.em("dve", lambda v, b=bank, jc=jc, zr=zr, xr=xrot:
                              v.scalar_tensor_tensor(zt[zr][:, :512], b[:, :512],
                                                     bias_col[:, jc:jc + 1],
                                                     x0t[xr, :, :], ALU.add, ALU.add),
                              waits=[ms, xr_ms, d_bo] + p.takewaits(("zt", zr)),
                              inc="dve")
                else:
                    ev = p.em("dve", lambda v, b=bank, zr=zr, xr=xrot:
                              v.scalar_tensor_tensor(zt[zr][:, :512], b[:, :512],
                                                     1.0, x0t[xr, :, :],
                                                     ALU.mult, ALU.add),
                              waits=[ms, xr_ms] + p.takewaits(("zt", zr)), inc="dve")
                p.use(("x0t", xrot), ev)
                p.use(("bank", bidx), ev)
                d_z = dma("sync", zdst_ap[128 * jc:128 * (jc + 1),
                                          512 * slot:512 * (slot + 1)],
                          zt[zr][:, :512], waits=[ev], inc="dsz%d" % zr)
                brot = idx % 3
                if bf_fn is not None:
                    bfdst = bf_fn(jc, slot)
                    zb = p.em("dve", lambda v, zr=zr, d=bfdst:
                              v.tensor_copy(d, zt[zr][:, :512]),
                              waits=[ev], inc="dve")
                    s1rhs, trk = bfdst, False
                else:
                    zb = p.em("dve", lambda v, zr=zr, br=brot:
                              v.tensor_copy(bft3[br], zt[zr][:, :512]),
                              waits=[ev] + p.takewaits(("zbft8", brot)), inc="dve")
                    s1rhs, trk = bft3[brot], True
                zq = p.em("act", lambda s, zr=zr, br=brot:
                          s.activation(sqt3[br], zt[zr][:, :512],
                                       AF.Square, bias=0.0, scale=1.0),
                          waits=[zb] + p.takewaits(("zsqt8", brot)), inc="act")
                p.use(("zt", zr), d_z, zq)
                pend.append((jc, slot, zb, zq, brot, s1rhs, trk))
                flush_stats(keep=2)
            p.use(wkey(par), p.now("pe"))
            if jc + 3 < 16:
                dws.append(w_dma(jc + 3))
        flush_stats()
        return stat_sl

    def ln_rows(base, stat_sl, mkbf=False):
        outm = []
        en_bf = []
        ea_bf = []
        for slot in range(2):
            b_s, b_q = pb[4 + slot], pb[6 + slot]
            rw = p.takewaits(("rowsA",)) + p.takewaits(("rowsB",))
            a1 = p.em("act", lambda s, b=b_s: s.activation(
                rows[0:1, 0:512], b[0:1, :512], AF.Identity, bias=0.0,
                scale=-1.0 / 2048.0), waits=stat_sl[slot] + rw, inc="act")
            a2 = p.em("act", lambda s, b=b_q: s.activation(
                rows[0:1, 512:1024], b[0:1, :512], AF.Identity, bias=0.0,
                scale=1.0 / 2048.0), inc="act")
            p.use(("bank", 4 + slot), a1)
            p.use(("bank", 6 + slot), a2)
            if mkbf:
                mn = mm(pb[4 + slot][:, :512], onesf_r[0:1, :], rows[0:1, 0:512],
                        True, True, waits=[a1] + p.takewaits(("bank", 4 + slot)),
                        inc="pe")
                p.use(("rowsA",), mn)
                en = p.em("act", lambda s, sl=slot: s.copy(
                    zbft[sl][:, :512], pb[4 + sl][:, :512]),
                    waits=[mn] + p.takewaits(("zbft8", slot)), inc="act")
                p.use(("bank", 4 + slot), en)
                en_bf.append(en)
            d1 = p.em("dve", lambda v: v.tensor_mul(rows2[0:1, 0:512],
                                                    rows[0:1, 0:512], rows[0:1, 0:512]),
                      waits=[a1] + p.takewaits(("rows2",)), inc="dve")
            d2 = p.em("dve", lambda v: v.tensor_sub(rows[0:1, 512:1024],
                                                    rows[0:1, 512:1024], rows2[0:1, 0:512]),
                      waits=[a2, d1], inc="dve")
            a3 = p.em("act", lambda s: s.activation(rows[0:1, 512:1024],
                                                    rows[0:1, 512:1024], AF.Sqrt,
                                                    bias=eps5_t[0:1, 0:1], scale=1.0),
                      waits=[d2, m_eps5], inc="act")
            d3 = p.em("dve", lambda v: v.reciprocal(rows2[0:1, 0:512],
                                                    rows[0:1, 512:1024]),
                      waits=[a3], inc="dve")
            dA = dma("sync", rowbuf.ap()[base + 2 * slot:base + 2 * slot + 1, :],
                     rows2[0:1, 0:512], waits=[d3], inc="dsr")
            if mkbf:
                ma = mm(pb[6 + slot][:, :512], onesf_r[0:1, :], rows2[0:1, 0:512],
                        True, True, waits=[d3] + p.takewaits(("bank", 6 + slot)),
                        inc="pe")
                p.use(("rows2",), ma)
                ea = p.em("act", lambda s, sl=slot: s.copy(
                    zsqt[sl][:, :512], pb[6 + sl][:, :512]),
                    waits=[ma] + p.takewaits(("zsqt8", slot)), inc="act")
                p.use(("bank", 6 + slot), ea)
                ea_bf.append(ea)
            d4 = p.em("dve", lambda v: v.scalar_tensor_tensor(
                rows2[0:1, 512:1024], rows[0:1, 0:512], 1.0, rows2[0:1, 0:512],
                ALU.mult, ALU.mult), waits=[d3], inc="dve")
            dB = dma("sync", rowbuf.ap()[base + 2 * slot + 1:base + 2 * slot + 2, :],
                     rows2[0:1, 512:1024], waits=[d4], inc="dsra")
            p.use(("rowsA",), d1, d4)
            p.use(("rowsB",), a3)
            p.use(("rows2",), dA, dB, d4)
            outm.append((dA, dB))
        return outm, en_bf, ea_bf

    def ln_normalize(base, rowdma, xout_ap, zsrc_ap):
        bms = {}
        for slot in range(2):
            dA = dma("gp", abc[slot, 0, :, :],
                     rowbuf.ap()[base + 2 * slot:base + 2 * slot + 1, :]
                     .partition_broadcast(128),
                     waits=[rowdma[slot][0]] + p.takewaits(("abcA", slot)),
                     inc="dgr%d" % (2 * slot))
            dB = dma("gp", abc[slot, 1, :, :],
                     rowbuf.ap()[base + 2 * slot + 1:base + 2 * slot + 2, :]
                     .partition_broadcast(128),
                     waits=[rowdma[slot][1]] + p.takewaits(("abcB", slot)),
                     inc="dgr%d" % (2 * slot + 1))
            bms[slot] = (dA, dB)
        zts = [zt[0], zt[1], zt[2], x0t[0], x0t[1], x0t[2], x0t[3]]

        def zkey(zr):
            return ("zt", zr) if zr < 3 else ("x0t", zr - 3)

        def zsem(zr):
            return "dsz%d" % zr if zr < 3 else "dsx%d" % (zr - 3)

        lastms = []
        for jc in range(NFC):
            for slot in range(2):
                idx = 2 * jc + slot
                zr = idx % 7
                d_z = dma("sync", zts[zr][:, :512],
                          zsrc_ap[128 * jc:128 * (jc + 1), 512 * slot:512 * (slot + 1)],
                          waits=p.takewaits(zkey(zr)), inc=zsem(zr))
                eng = "dve" if (idx % 2 == 0) else "gp"
                esem = "dve" if eng == "dve" else "gpc"
                t1 = p.em(eng, lambda v, zr=zr, s=slot: v.tensor_mul(
                    zts[zr][:, :512], zts[zr][:, :512], abc[s, 0, :, :]),
                    waits=[d_z, bms[slot][0]], inc=esem)
                t2 = p.em(eng, lambda v, zr=zr, s=slot: v.tensor_add(
                    zts[zr][:, :512], zts[zr][:, :512], abc[s, 1, :, :]),
                    waits=[t1, bms[slot][1]], inc=esem)
                p.use(("abcA", slot), t1)
                p.use(("abcB", slot), t2)
                d_o = dma("act", xout_ap[128 * jc:128 * (jc + 1),
                                         512 * slot:512 * (slot + 1)],
                          zts[zr][:, :512], waits=[t2], inc="dsc")
                p.use(zkey(zr), d_o)
                lastms.append(d_o)
        return lastms

    def wa_dma(pi, si):
        width = WA_RANGES[pi][1]
        ww = p.takewaits(wkey(si))
        ms4 = []
        for q in range(4):
            ms4.append(dma("gp", wpan[si][:, 4 * q * width:4 * (q + 1) * width],
                           wa_i[:, WA_OFF[pi] + 4 * q * width:
                                WA_OFF[pi] + 4 * (q + 1) * width],
                           waits=ww, inc=WQ_SEMS[q] % si))
            ww = []
        return ms4

    sm1 = proj_resid_stats(
        wo_i, lambda ec, slot: cdk[:, 1024 * ec + 512 * slot: 1024 * ec + 512 * (slot + 1)],
        boc_sb,
        lambda jc, slot: x_img[:, 1024 * jc + 512 * slot: 1024 * jc + 512 * (slot + 1)],
        zbuf.ap(),
        bf_fn=lambda jc, slot: arena1[:, SC * jc + 512 * slot:
                                      SC * jc + 512 * (slot + 1)],
        sqt3=[zsqt[0], zsqt[1], kvs[0]])
    p.use(("cdk_r",), p.now("pe"))
    dws6 = {}

    def kstd_dma(h):
        dws6[h] = wa_dma(h, h % 3)

    kstd_dma(0)
    kstd_dma(1)
    kstd_dma(2)
    lr1, en_bf, ea_bf = ln_rows(0, sm1, mkbf=True)
    x1bf_ms = {}
    cc_slot = {0: [], 1: []}
    for jc in range(NFC):
        for slot in range(2):
            idx = 2 * jc + slot
            eng = "dve" if idx % 2 == 0 else "gp"
            esem = "dve" if eng == "dve" else "gpc"
            tile = arena1[:, SC * jc + 512 * slot: SC * jc + 512 * (slot + 1)]
            c1 = p.em(eng, lambda v, t=tile, s=slot: v.tensor_add(t, t, zbft[s]),
                      waits=[en_bf[slot]], inc=esem)
            c2 = p.em(eng, lambda v, t=tile, s=slot: v.tensor_mul(t, t, zsqt[s]),
                      waits=[c1, ea_bf[slot]], inc=esem)
            x1bf_ms.setdefault(jc, []).append(c2)
            cc_slot[slot].append(c2)
    for slot in range(2):
        p.use(("zbft8", slot), *cc_slot[slot])
        p.use(("zsqt8", slot), *cc_slot[slot])

    if stop_phase <= 5:
        return _finish(nc, p, sem, stck)

    PHASE_MARKS.append(("PH6k", sum(1 for o in p.ops if o[0] == "pe")))
    for h in range(16):
        par = h % 3
        mss = {}
        for slot in range(2):
            bank = pb[slot]
            bw = [dws6[h][0]] + p.takewaits(("bank", slot))
            ms = None
            for fc in range(NFC):
                if fc in (4, 8, 12):
                    bw = bw + [dws6[h][fc // 4]]
                if h == 0:
                    bw = bw + x1bf_ms.get(fc, [])
                ms = mm(bank[0:120, :512],
                        wpan[par][:, 120 * fc:120 * (fc + 1)],
                        xbf(fc, slot * 512, 512),
                        start=(fc == 0), stop=(fc == NFC - 1),
                        waits=bw,
                        inc="pe" if fc == NFC - 1 else None)
                bw = []
            mss[slot] = ms
        p.use(wkey(par), p.now("pe"))
        if h + 3 < 16:
            kstd_dma(h + 3)
        for slot in range(2):
            bank = pb[slot]
            dst = cdk[0:120, 1024 * h + 512 * slot: 1024 * h + 512 * (slot + 1)]
            ev = p.em("act", lambda s, d=dst, b=bank: s.copy(d, b[0:120, :512]),
                      waits=[mss[slot]] + p.takewaits(("cdk_r",)), inc="act")
            p.use(("bank", slot), ev)
            p.use(("kstd",), ev)

    PHASE_MARKS.append(("PH6l", sum(1 for o in p.ops if o[0] == "pe")))
    lowbuf = mask_sb
    dwl_w = (p.takewaits(("x0t", 0)) + p.takewaits(("x0t", 1)) +
             p.takewaits(("x0t", 2)) + p.takewaits(("x0t", 3)) +
             p.takewaits(("trow", 0)) + p.takewaits(("trow", 1)))
    dwl = dma("gp", lowbuf[:, 0:4096], wa_i[:, WA_OFF[16]:WA_OFF[17]],
              waits=dwl_w, inc="dgl")
    sel_ev = []
    for wave in range(2):
        mss = []
        for i in range(4):
            sc = 4 * wave + i
            bidx = 4 + i
            bank = pb[bidx]
            bw = [dwl] + p.takewaits(("bank", bidx))
            ms = None
            for fc in range(NFC):
                ms = mm(bank[:, :256], xbf(fc, sc * 128, 128),
                        lowbuf[:, 256 * fc:256 * (fc + 1)],
                        start=(fc == 0), stop=(fc == NFC - 1),
                        waits=bw if fc == 0 else [],
                        inc="pe" if fc == NFC - 1 else None)
                bw = []
            mss.append(ms)
        for i in range(4):
            sc = 4 * wave + i
            bidx = 4 + i
            bank = pb[bidx]
            e1 = p.em("dve", lambda v, b=bank: v.tensor_mul(seltmp[0], b[:, 128:256], pat[0]),
                      waits=[mss[i], mpat] + p.takewaits(("seltmp",)) +
                      p.takewaits(("zbft8", 0)) + p.takewaits(("zbft8", 1)) +
                      p.takewaits(("zsqt8", 0)) + p.takewaits(("zsqt8", 1)), inc="dve")
            e2 = p.em("dve", lambda v, b=bank: v.tensor_mul(seltmp[1], b[:, 0:128], pat[1]),
                      waits=[mss[i]], inc="dve")
            e3 = p.em("dve", lambda v, sc=sc: v.tensor_add(selq_tm[sc], seltmp[0], seltmp[1]),
                      waits=[e1, e2] + p.takewaits(("u4q", sc)), inc="dve")
            e4 = p.em("dve", lambda v, b=bank: v.tensor_mul(seltmp[0], b[:, 0:128], pat[2]),
                      waits=[e3], inc="dve")
            e5 = p.em("dve", lambda v, b=bank: v.tensor_mul(seltmp[1], b[:, 128:256], pat[3]),
                      waits=[e3], inc="dve")
            e6 = p.em("dve", lambda v, sc=sc: v.tensor_add(selk_tm[sc], seltmp[0], seltmp[1]),
                      waits=[e4, e5] + p.takewaits(("u4k", sc)), inc="dve")
            p.use(("seltmp",), e6)
            p.use(("bank", bidx), e4, e5)
            sel_ev.append((e3, e6))
    for sc in range(8):
        t1 = trans(pb_bf[7][:, 0:128], selq_tm[sc],
                   waits=[sel_ev[sc][0], d_ident] + p.takewaits(("bank", 7)))
        c1 = p.em("act", lambda s, sc=sc: s.copy(selq_fm[:, 128 * sc:128 * (sc + 1)],
                                                 pb_bf[7][:, 0:128]),
                  waits=[t1], inc="act")
        p.use(("bank", 7), c1)
        t2 = trans(pb_bf[7][:, 0:128], selk_tm[sc],
                   waits=[sel_ev[sc][1]] + p.takewaits(("bank", 7)))
        c2 = p.em("act", lambda s, sc=sc: s.copy(selk_fm[:, 128 * sc:128 * (sc + 1)],
                                                 pb_bf[7][:, 0:128]),
                  waits=[t2], inc="act")
        p.use(("bank", 7), c2)
        p.use(("selq_fm",), c1)
        p.use(("selk_fm",), c2)

    low_last_pe = p.now("pe")

    wkt = p.takewaits(("selk_fm",)) + p.takewaits(("kstd",))
    ktail = []
    for h in range(H):
        d_t = dma("sync", cdk[120:128, 1024 * h:1024 * (h + 1)],
                  selk_fm[8 * h:8 * (h + 1), :],
                  waits=wkt if h == 0 else [], inc="dskt")
        wkt = []
        ktail.append(d_t)
    d_kf = dma("sync", kf_in.ap().rearrange("(h p) s -> p h s", p=128),
               cdk[:, :].rearrange("p (h s) -> p h s", h=H),
               waits=ktail, inc="dkf")
    coll("AllGather", ALU.bypass, [kf_in.ap()], [kf_out.ap()], waits=[d_kf], inc="cc2")

    PHASE_MARKS.append(("PH6v", sum(1 for o in p.ops if o[0] == "pe")))
    dws7 = {}

    def v_dma(h):
        dws7[h] = wa_dma(17 + h, h % 3)

    v_dma(0)
    v_dma(1)
    v_dma(2)
    for h in range(16):
        par = h % 3
        for wave in range(2):
            mss = []
            for i in range(4):
                sc = 4 * wave + i
                bank = pb[i]
                bw = [dws7[h][0]] + p.takewaits(("bank", i))
                ms = None
                for fc in range(NFC):
                    if fc in (4, 8, 12):
                        bw = bw + [dws7[h][fc // 4]]
                    ms = mm(bank[:, :128], xbf(fc, sc * 128, 128),
                            wpan[par][:, 128 * fc:128 * (fc + 1)],
                            start=(fc == 0), stop=(fc == NFC - 1),
                            waits=bw,
                            inc="pe" if fc == NFC - 1 else None)
                    bw = []
                mss.append(ms)
            for i in range(4):
                sc = 4 * wave + i
                bank = pb[i]
                dstv = cdv[:, sc * (H * 129) + 129 * h: sc * (H * 129) + 129 * h + 128]
                ev = p.em("act", lambda s, d=dstv, b=bank: s.copy(d, b[:, :128]),
                          waits=[mss[i]], inc="act")
                p.use(("bank", i), ev)
                p.use(("cdv2",), ev)
        p.use(wkey(par), p.now("pe"))
        if h + 3 < 16:
            v_dma(h + 3)
    d_vv = dma("sync", vv_in.ap().rearrange("(c p) s -> p c s", p=128),
               cdv[:, :].rearrange("p (c s) -> p c s", c=8),
               waits=p.takewaits(("cdv2",)), inc="dvv")
    coll("AllGather", ALU.bypass, [vv_in.ap()], [vv_out.ap()], waits=[d_vv], inc="cc3")

    d_mask = dma("sync", mask_sb[:, 0:4096], masks_i,
                 waits=[low_last_pe], inc="dsm")

    if "kf" in dbg:
        dma("sync", dbg["kf"], kf_out.ap(), waits=[("cc2", 1)], inc="ds")
    if "vv" in dbg:
        dma("sync", dbg["vv"], vv_out.ap(), waits=[("cc3", 1)], inc="ds")

    if stop_phase <= 6:
        return _finish(nc, p, sem, stck)

    PHASE_MARKS.append(("PH7", sum(1 for o in p.ops if o[0] == "pe")))
    dwq7 = {}

    def qstd_dma(h):
        dwq7[h] = wa_dma(33 + h, h % 2)

    class QstdFeed:

        def __init__(self, h):
            self.h = h
            self.par = h % 2
            self.dw = dwq7[h]
            self.seq = [(slot, fc) for slot in range(2) for fc in range(NFC)]
            self.i = 0
            self.evq = {}

        def feed(self, n):
            while n > 0 and self.i < len(self.seq):
                slot, fc = self.seq[self.i]
                self.i += 1
                n -= 1
                bidx = 5 + slot
                bank = pb[bidx]
                bw = []
                if fc == 0:
                    bw = [self.dw[0]] + p.takewaits(("bank", bidx))
                if fc in (4, 8, 12):
                    bw = bw + [self.dw[fc // 4]]
                ms = mm(bank[0:120, :512],
                        wbuf[self.par, :, 120 * fc:120 * (fc + 1)],
                        xbf(fc, slot * 512, 512),
                        start=(fc == 0), stop=(fc == NFC - 1),
                        waits=bw,
                        inc="pe" if fc == NFC - 1 else None)
                if fc == NFC - 1:
                    dstq = qbuf[self.par, 0:120, 512 * slot:512 * (slot + 1)]
                    ev = p.em("act", lambda s, d=dstq, b=bank, hh=self.h:
                              s.activation(d, b[0:120, :512], AF.Identity,
                                           bias=0.0, scale=wstd_c[0:120, hh:hh + 1]),
                              waits=[ms] + p.takewaits(("qbuf", self.par, slot)),
                              inc="act")
                    p.use(("bank", bidx), ev)
                    self.evq[slot] = ev

        def finish(self):
            self.feed(len(self.seq))
            h, par = self.h, self.par
            p.use(("wbuf", par), p.now("pe"))
            d_t = dma("sync", qbuf[par, 120:128, 0:1024],
                      selq_fm[8 * h:8 * (h + 1), :],
                      waits=p.takewaits(("qbuf_t", par)), inc="dst%d" % par)
            wkv = p.takewaits(("xkv", par)) + ([("cc2", 1), ("cc3", 1)] if h < 2 else [])
            l1 = dma("sync", xkv[par, :, 0:1024],
                     kf_out.ap()[128 * h:128 * (h + 1), :], waits=wkv, inc="dsk%d" % par)
            l2 = dma("sync", xkv[par, :, 1024:2048],
                     kf_out.ap()[H * 128 + 128 * h: H * 128 + 128 * (h + 1), :],
                     inc="dsk%d" % par)
            l3 = dma("sync", xkv[par, :, 2048:4112].rearrange("p (g c) -> p g c", g=16),
                     vv_out.ap().rearrange("(g p) c -> p g c", p=128)[:, :, 129 * h:129 * (h + 1)],
                     inc="dsk%d" % par)
            return [self.evq[0], self.evq[1], d_t, l1, l2, l3]

    SMIN = [0, 1, 2, 3, 0, 1, 2, 3]
    PAIR_LIVE = [[g for g in range(8) if SMIN[g] <= 1],
                 list(range(8))]

    def emit_attn2(h, first_w, feeder=None):
        par = h % 2
        if "qf" in dbg and h == 13:
            dma("sync", dbg["qf"], qbuf[par, :, 0:1024], waits=list(first_w), inc="ds")
        kf = xkv[par, :, 0:2048]
        vv = xkv[par, :, 2048:4112]
        first_w = list(first_w)
        OB = [2, 3, 4, 7]
        prot = 0
        for pairi in range(2):
            base = 512 * pairi
            live = PAIR_LIVE[pairi]
            first_av = [True] * 4
            oww = [p.takewaits(("bank", OB[i])) for i in range(4)]
            seqlist = [(g, k) for g in live for k in range(2)]
            last_touch = {}
            for i in range(4):
                s_i = (4 * pairi + i) // 2
                last_touch[i] = [gk for gk in seqlist if SMIN[gk[0]] <= s_i][-1]
            LEAD = 6
            pmeta = {}
            nseq = len(seqlist)

            def emit_S(t):
                nonlocal first_w, prot
                g, k = seqlist[t]
                w0 = max(256 * SMIN[g], base)
                width = base + 512 - w0
                sb_i = t % 2
                sbank = pb[sb_i]
                msS = mm(sbank[:, :width],
                         kf[:, 256 * g + 128 * k: 256 * g + 128 * (k + 1)],
                         qbuf[par, :, w0:base + 512], True, True,
                         waits=first_w + p.takewaits(("bank", sb_i)),
                         inc="pe")
                first_w = []
                rot = prot
                prot = (prot + 1) % 8
                pdst = pbufW[rot][:, 0:width]
                aE = p.em("act", lambda s, d=pdst, b=sbank, w=width: s.activation(
                    d, b[:, :w], AF.Exp, bias=nshift[:, 0:1], scale=SCALE2),
                    waits=[msS, m_nsh] + p.takewaits(("pbufP", rot)), inc="act")
                p.use(("bank", sb_i), aE)
                lastP = aE
                if SMIN[g] >= base // 256:
                    jm = g // 4
                    moff = ((SMIN[g] * 2 + jm) * 2 + k) * 256
                    lastP = p.em("dve", lambda v, d=pdst, mo=moff: v.tensor_mul(
                        d[:, 0:256], d[:, 0:256], mask_sb[:, mo:mo + 256]),
                        waits=[aE, d_mask], inc="dve")
                pmeta[t] = (rot, w0, lastP)

            def emit_AV(t):
                g, k = seqlist[t]
                skc = 2 * g + k
                rot, w0, lastP = pmeta.pop(t)
                n_sq = [i for i in range(4) if (4 * pairi + i) // 2 >= SMIN[g]]
                for ii, i in enumerate(n_sq):
                    col = 128 * i + base - w0
                    last_ch = (last_touch[i] == (g, k))
                    msA = mm(pb[OB[i]][:, :129],
                             pbufW[rot][:, col:col + 128],
                             vv[:, 129 * skc:129 * (skc + 1)],
                             start=first_av[i], stop=last_ch,
                             waits=[lastP] + oww[i],
                             inc="pe" if (ii == len(n_sq) - 1 or last_ch) else None)
                    oww[i] = []
                    first_av[i] = False
                    if ii == len(n_sq) - 1:
                        p.use(("pbufP", rot), msA)
                    if last_ch:
                        chunk_last[i] = msA

            chunk_last = {}
            for t in range(min(LEAD, nseq)):
                emit_S(t)
            for t in range(nseq):
                if feeder:
                    feeder.feed(1)
                if t + LEAD < nseq:
                    emit_S(t + LEAD)
                if feeder:
                    feeder.feed(1)
                emit_AV(t)
            for i in range(4):
                msA = chunk_last[i]
                sqg = 4 * pairi + i
                sw = p.takewaits(("seltmp",))
                r1 = p.em("dve", lambda v, b=pb[OB[i]], q=i: v.reciprocal(
                    rec4[q], b[:, 128:129]),
                    waits=[msA] + p.takewaits(("rec", i)), inc="dve")
                a1 = p.em("dve", lambda v, b=pb[OB[i]], q=i: v.tensor_scalar_mul(
                    atm4[q], b[:, 0:128], rec4[q]),
                    waits=[r1] + p.takewaits(("att_tm", i)) + sw, inc="dve")
                p.use(("rec", i), a1)
                tb = OB[i]
                t1 = trans(pb_bf[tb][:, 0:128], atm4[i],
                           waits=[a1, d_ident])
                p.use(("att_tm", i), t1)
                dsta = cdk[:, SC * h + 128 * sqg: SC * h + 128 * sqg + 128]
                cm = p.em("act", lambda s, d=dsta, tb=tb: s.copy(d, pb_bf[tb][:, 0:128]),
                          waits=[t1], inc="act")
                p.use(("bank", tb), cm)
                p.use(("attn2",), cm)
        endm = p.now("pe")
        p.use(("xkv", par), endm)
        for slot in range(2):
            p.use(("qbuf", par, slot), endm)
        p.use(("qbuf_t", par), endm)
        if PH7_SERIAL:
            p.barrier()

    qstd_dma(0)
    qstd_dma(1)
    fw_prev = QstdFeed(0).finish()
    for h in range(16):
        if h + 2 < 16:
            qstd_dma(h + 2)
        feeder = QstdFeed(h + 1) if h + 1 < 16 else None
        emit_attn2(h, fw_prev, feeder)
        fw_prev = feeder.finish() if feeder else None

    p.barrier()

    if "attn2" in dbg:
        dma("sync", dbg["attn2"], cdk[:, :], inc="ds")

    if stop_phase <= 7:
        return _finish(nc, p, sem, stck)

    PHASE_MARKS.append(("PH8", sum(1 for o in p.ops if o[0] == "pe")))
    bcm8 = []
    for slot in range(2):
        dA8 = dma("gp", abc[slot, 0, :, :],
                  rowbuf.ap()[2 * slot:2 * slot + 1, :].partition_broadcast(128),
                  waits=[lr1[slot][0]] + p.takewaits(("abcA", slot)),
                  inc="dgr%d" % (2 * slot))
        dB8 = dma("gp", abc[slot, 1, :, :],
                  rowbuf.ap()[2 * slot + 1:2 * slot + 2, :].partition_broadcast(128),
                  waits=[lr1[slot][1]] + p.takewaits(("abcB", slot)),
                  inc="dgr%d" % (2 * slot + 1))
        bcm8.append((dA8, dB8))
    sm2 = proj_resid_stats(
        wp_i, lambda ec, slot: cdk[:, SC * ec + 512 * slot: SC * ec + 512 * (slot + 1)],
        None,
        lambda jc, slot: zbuf.ap()[128 * jc:128 * (jc + 1),
                                   512 * slot:512 * (slot + 1)],
        x1buf.ap(),
        bft3=[zbft[0], zbft[1], kvs[1]],
        sqt3=[zsqt[0], zsqt[1], kvs[2]],
        xfix=(abc, bcm8))
    lr2, _, _ = ln_rows(4, sm2)
    out_ms = ln_normalize(4, lr2, out_t, x1buf.ap())

    return _finish(nc, p, sem, stck)


def _finish(nc, p, sem, stck):
    p.barrier(exclude=())
    p.em("sync", lambda e: e.nop())
    p.em("gp", lambda e: e.nop())
    p.em("pe", lambda e: e.nop())
    p.em("dve", lambda v: v.nop())
    p.em("act", lambda s: s.nop())

    with nc.Block() as block:
        def runner(engname):
            def _(eng):
                seen = {}
                for (e, fn, waits, inc, amt) in p.ops:
                    if e != engname:
                        continue
                    best = {}
                    for (s, v) in waits:
                        if v > 0 and v > seen.get(s, 0) and v > best.get(s, 0):
                            best[s] = v
                    for (s, v) in best.items():
                        eng.wait_ge(sem[s], v)
                        seen[s] = v
                    instr = fn(eng)
                    if inc is not None:
                        if inc.startswith("cc"):
                            instr.then_inc(sem[inc])
                        elif inc in ("pe", "dve", "act", "gpc"):
                            instr.then_inc(sem[inc], 1)
                        else:
                            instr.then_inc(sem[inc], 16)
            return _
        block.tensor(runner("pe"))
        block.vector(runner("dve"))
        block.scalar(runner("act"))
        block.gpsimd(runner("gp"))
        block.sync(runner("sync"))
    stck.close()
    return nc


def _host_inputs(inputs):
    f = np.float32
    wq = np.asarray(inputs["L0_attn_wq"], f)
    wk = np.asarray(inputs["L0_attn_wk"], f)
    wv = np.asarray(inputs["L0_attn_wv"], f)
    wo = np.asarray(inputs["L0_attn_wo"], f)
    Wa = np.asarray(inputs["L0_ffn_Wa"], f)
    Wp = np.asarray(inputs["L0_ffn_Wp"], f)
    def _img_bias(M, brow):
        pieces = []
        for stt, w in W16:
            blk = np.ascontiguousarray(M[:, stt:stt + w])
            im = blk.reshape(16, 128, w).transpose(1, 0, 2).reshape(128, 16 * w)
            bb = np.zeros((128, 128), np.float32)
            bb[0, :] = brow[stt:stt + w]
            pieces.append(np.concatenate([im, bb], axis=1))
        return np.ascontiguousarray(np.concatenate(pieces, axis=1))

    shared = dict(
        wq_i=_img_panels(np.ascontiguousarray(wq.T), W16),
        wk_i=_img_bias(np.ascontiguousarray(wk.T), np.asarray(inputs["L0_attn_bk"], f)),
        wv_i=_img_bias(np.ascontiguousarray(wv.T), np.asarray(inputs["L0_attn_bv"], f)),
        wo_i=_img_panels(np.ascontiguousarray(wo.T), W16),
        wp_i=_img_panels(Wp, W16),
        wa_i=_img_panels(Wa, WA_RANGES),
        bq_c=np.ascontiguousarray(np.asarray(inputs["L0_attn_bq"], f).reshape(16, 128).T),
        bo_c=np.ascontiguousarray(np.asarray(inputs["L0_attn_bo"], f).reshape(16, 128).T),
        bk_r=np.asarray(inputs["L0_attn_bk"], f).reshape(1, D),
        bv_r=np.asarray(inputs["L0_attn_bv"], f).reshape(1, D),
        wsr_i=np.concatenate([np.asarray(inputs["L0_ffn_w_std"], f),
                              np.asarray(inputs["L0_ffn_w_rec"], f)]).reshape(1, 32),
        r01_i=(np.arange(128)[None, :] // 8 == np.arange(16)[:, None]).astype(f),
        ident_i=np.eye(128, dtype=BF),
    )
    x = np.asarray(inputs["L0_attn_x"], f)
    per_core = []
    for c in range(8):
        b, parity = c // 2, c % 2
        xc = np.ascontiguousarray(x[b][_own_idx(parity)])
        xi = _img_panels(np.ascontiguousarray(xc.T), [(0, SC)])
        masks = _masks_for(parity).transpose(3, 0, 1, 2, 4).reshape(128, 16 * 256)
        per_core.append(dict(shared, x_img=xi, masks_i=np.ascontiguousarray(masks)))
    return per_core


def kernel(**inputs):
    per_core = _host_inputs(inputs)
    nc = build_program()
    res = run_bass_kernel_spmd(nc, per_core, core_ids=list(range(8)))
    x = np.asarray(inputs["L0_attn_x"])
    out = np.zeros((4, 2048, D), dtype=np.float32)
    for c in range(8):
        b, parity = c // 2, c % 2
        o = np.asarray(res.results[c]["out_fm"])
        out[b, _own_idx(parity)] = np.ascontiguousarray(o.T)
    return out


# version 20
# speedup vs baseline: 1.1146x; 1.1064x over previous
import contextlib
import numpy as np
import ml_dtypes

import concourse.bass as bass
import concourse.mybir as mybir
from concourse.bass_utils import run_bass_kernel_spmd

F32 = mybir.dt.float32
BF16 = mybir.dt.bfloat16
AF = mybir.ActivationFunctionType
ALU = mybir.AluOpType
BF = ml_dtypes.bfloat16

D = 2048
H = 16
NFC = 16
SC = 1024
CH = 256
OWN = [[0, 3, 4, 7], [1, 2, 5, 6]]
GLOB = OWN[0] + OWN[1]
LIVE = [sorted(g for g in range(8)
               if GLOB[g] <= OWN[0][s] or GLOB[g] <= OWN[1][s]) for s in range(4)]
EXP_SHIFT = 20.0
SCALE2 = float(1.0 / np.sqrt(np.float32(128.0)))
QSTD0, KSTD0, V0, LOW0 = 0, 1920, 3840, 5888
W16 = [(i * 128, 128) for i in range(16)]
WA_RANGES = ([(KSTD0 + 120 * i, 120) for i in range(16)] +
             [(LOW0, 256)] +
             [(V0 + 128 * i, 128) for i in range(16)] +
             [(QSTD0 + 120 * i, 120) for i in range(16)])
WA_OFF = np.cumsum([0] + [16 * w for _, w in WA_RANGES]).tolist()
GROUPS = [[0, 1], [2, 3], [4, 5], [6, 7]]


def _own_idx(parity):
    return np.concatenate([np.arange(CH) + c * CH for c in OWN[parity]])


def _img_panels(M, ranges):
    pieces = []
    for stt, w in ranges:
        blk = np.ascontiguousarray(M[:, stt:stt + w])
        pieces.append(blk.reshape(16, 128, w).transpose(1, 0, 2).reshape(128, 16 * w))
    return np.ascontiguousarray(np.concatenate(pieces, axis=1))


def _masks_for(parity):
    m = np.zeros((4, 2, 2, 128, 256), dtype=BF)
    for s in range(4):
        for j, g in enumerate((s, s + 4)):
            kg = GLOB[g] * CH + np.arange(CH)
            qg = OWN[parity][s] * CH + np.arange(CH)
            full = (kg[:, None] <= qg[None, :]).astype(np.float32)
            m[s, j, 0] = full[:128]
            m[s, j, 1] = full[128:]
    return m


class Views:

    def __init__(self, views):
        self.views = views

    def __getitem__(self, key):
        if isinstance(key, tuple):
            v = self.views[key[0]]
            rest = key[1:]
            if not rest:
                return v
            return v[rest if len(rest) > 1 else rest[0]]
        return self.views[key]


class Plan:
    def __init__(self):
        self.ops = []
        self.cnt = {}
        self.pending = {}
        self.last = {}

    def sem(self, n):
        self.cnt.setdefault(n, 0)

    def em(self, eng, fn, waits=(), inc=None, amt=1):
        w = list(self.pending.pop(eng, ())) + [x for x in waits if x is not None]
        self.ops.append((eng, fn, w, inc, amt))
        if inc is not None:
            self.cnt[inc] = self.cnt.get(inc, 0) + (amt if amt else 1)
            return (inc, self.cnt[inc])
        return None

    def now(self, s):
        return (s, self.cnt.get(s, 0))

    def barrier(self, exclude=("cc1", "cc2", "cc3")):
        snap = [(s, c) for s, c in self.cnt.items() if c > 0 and s not in exclude]
        for e in ("pe", "dve", "act", "gp", "sync"):
            self.pending[e] = snap

    def use(self, key, *ms):
        cur = self.last.setdefault(key, {})
        for m in ms:
            if m is not None:
                s, v = m
                cur[s] = max(cur.get(s, 0), v)

    def takewaits(self, key):
        d = self.last.pop(key, {})
        return list(d.items())


PH1_LIMIT = 16
PH0_PATTERNS = True
PH1_BIAS = True
PH1_EVAC = True
PH7_SERIAL = False

PHASE_MARKS = []


def build_program(stop_phase=99, debug_taps=()):
    PHASE_MARKS.clear()
    nc = bass.Bass("TRN2", target_bir_lowering=False, debug=False)
    p = Plan()
    SEM_NAMES = ("pe", "dve", "act", "dg", "ds", "cc1", "cc2", "cc3",
                 "dgw0", "dgw1", "dgw2", "dgh0", "dgh1", "dgh2",
                 "dgi0", "dgi1", "dgi2", "dgj0", "dgj1", "dgj2", "dgb", "dkv",
                 "dgr0", "dgr1", "dgr2", "dgr3",
                 "dsx0", "dsx1", "dsx2", "dsx3", "dsz0", "dsz1", "dsz2", "dsr", "dsra",
                 "dst0", "dst1", "dsk0", "dsk1", "dckv", "dkf", "dvv",
                 "dgl", "dskt", "dsm", "dsc", "gpc")
    for n in SEM_NAMES:
        p.sem(n)
    WQ_SEMS = ("dgw%d", "dgh%d", "dgi%d", "dgj%d")

    x_img = nc.dram_tensor("x_img", [128, NFC * SC], F32, kind="ExternalInput").ap()
    wq_i = nc.dram_tensor("wq_i", [128, 8 * 4096], F32, kind="ExternalInput").ap()
    wk_i = nc.dram_tensor("wk_i", [128, 16 * 2176], F32, kind="ExternalInput").ap()
    wv_i = nc.dram_tensor("wv_i", [128, 16 * 2176], F32, kind="ExternalInput").ap()
    wo_i = nc.dram_tensor("wo_i", [128, 8 * 4096], F32, kind="ExternalInput").ap()
    wp_i = nc.dram_tensor("wp_i", [128, 8 * 4096], F32, kind="ExternalInput").ap()
    wa_i = nc.dram_tensor("wa_i", [128, WA_OFF[-1]], F32, kind="ExternalInput").ap()
    bq_c = nc.dram_tensor("bq_c", [128, 16], F32, kind="ExternalInput").ap()
    bo_c = nc.dram_tensor("bo_c", [128, 16], F32, kind="ExternalInput").ap()
    bk_r = nc.dram_tensor("bk_r", [1, D], F32, kind="ExternalInput").ap()
    bv_r = nc.dram_tensor("bv_r", [1, D], F32, kind="ExternalInput").ap()
    wsr_i = nc.dram_tensor("wsr_i", [1, 32], F32, kind="ExternalInput").ap()
    r01_i = nc.dram_tensor("r01_i", [16, 128], F32, kind="ExternalInput").ap()
    ident_i = nc.dram_tensor("ident_i", [128, 128], BF16, kind="ExternalInput").ap()
    masks_i = nc.dram_tensor("masks_i", [128, 16 * 256], BF16, kind="ExternalInput").ap()
    out_t = nc.dram_tensor("out_fm", [D, SC], F32, kind="ExternalOutput").ap()

    kv_in = nc.dram_tensor("kv_in", [128, H * 129], F32)
    kv_out = nc.dram_tensor("kv_out", [128, H * 129], F32)
    kf_in = nc.dram_tensor("kf_in", [H * 128, SC], BF16)
    kf_out = nc.dram_tensor("kf_out", [2 * H * 128, SC], BF16)
    vv_in = nc.dram_tensor("vv_in", [8 * 128, H * 129], BF16)
    vv_out = nc.dram_tensor("vv_out", [16 * 128, H * 129], BF16)
    zbuf = nc.dram_tensor("zbuf", [D, SC], F32)
    x1buf = nc.dram_tensor("x1buf", [D, SC], F32)
    rowbuf = nc.dram_tensor("rowbuf", [48, 512], F32)

    dbg = {}
    for name in debug_taps:
        shp = {"kv": [128, H * 129], "x1": [D, SC], "z": [D, SC],
               "kf": [2 * H * 128, SC], "vv": [16 * 128, H * 129],
               "attn2": [128, NFC * SC], "qf": [128, SC]}[name]
        dt = BF16 if name in ("kf", "vv", "attn2", "qf") else F32
        dbg[name] = nc.dram_tensor("dbg_" + name, shp, dt, kind="ExternalOutput").ap()

    stck = contextlib.ExitStack()
    sem = {n: stck.enter_context(nc.semaphore(n)) for n in SEM_NAMES}

    A = nc.alloc_sbuf_tensor

    def vsplit(ap, n, w):
        return Views([ap[:, i * w:(i + 1) * w] for i in range(n)])

    arena1 = A("arena1", [128, NFC * SC], BF16)
    qbuf_t = A("qbuf", [128, 2 * 2048], BF16)
    qbuf = vsplit(qbuf_t[:, :], 2, 2048)
    cdk = A("cdk", [128, H * SC], BF16)
    cdv = A("cdv", [128, 8 * H * 129], BF16)
    wbuf_t = A("wbuf", [128, 2 * 2176], BF16)
    wbuf = vsplit(wbuf_t[:, :], 2, 2176)
    xkv_t = A("xkv", [128, 2 * 4112], BF16)
    xkv = vsplit(xkv_t[:, :], 2, 4112)
    wpan = [wbuf[0], wbuf[1], xkv[0]]

    def wkey(si):
        return ("wbuf", si) if si < 2 else ("xkv", 0)
    kvbf_u = A("kvbf_u", [128, 2064], BF16)
    kvbf = kvbf_u[:, :]
    selq_fm = kvbf_u[:, 0:1024]
    selk_fm = kvbf_u[:, 1024:2048]
    kvs = Views([kvbf_u[:, i * 512:(i + 1) * 512] for i in range(4)])
    u1 = A("u1", [128, 2048], F32)
    trow = vsplit(u1[:, :], 2, 512)
    trow8 = vsplit(u1[:, :], 8, 256)
    x0t = vsplit(u1[:, :], 4, 512)
    mask_sb = u1[:, :].bitcast(BF16)
    u2 = A("u2", [128, 1536], F32)
    zt = vsplit(u2[:, :], 3, 512)
    pbufP = vsplit(u2[:, :].bitcast(BF16), 4, 256)
    u3 = A("u3", [128, 2048], F32)
    rbc = vsplit(u3[:, :], 4, 512)
    abc = Views([vsplit(u3[:, 0:1024], 2, 512), vsplit(u3[:, 1024:2048], 2, 512)])
    pbufW = vsplit(u3[:, :].bitcast(BF16), 8, 512)
    u4 = A("u4", [128, 1024], F32)
    u4b = u4[:, :].bitcast(BF16)
    zbft = vsplit(u4b, 2, 512)
    zsqt = Views([u4b[:, 1024 + i * 512:1024 + (i + 1) * 512] for i in range(2)])
    selq_tm = Views([u4b[:, i * 128:(i + 1) * 128] for i in range(8)])
    selk_tm = Views([u4b[:, 1024 + i * 128:1024 + (i + 1) * 128] for i in range(8)])
    u5 = A("u5", [128, 392], F32)
    kvst = vsplit(u5[:, 0:258], 2, 129)
    seltmp = vsplit(u5[:, 0:256], 2, 128)
    att_tm = Views([u5[:, 256 + i * 64:256 + (i + 1) * 64].bitcast(BF16) for i in range(2)])
    rec_t = Views([u5[:, 384 + i:385 + i] for i in range(2)])
    atm4 = Views([u5[:, 64 * i:64 * (i + 1)].bitcast(BF16) for i in range(4)])
    rec4 = Views([u5[:, 384 + i:385 + i] for i in range(4)])
    rows = A("rows", [1, 1024], F32)
    rows2 = A("rows2", [1, 1024], F32)
    patrow = rows2[0:1, 0:512]
    rows_bf = A("rows_bf", [1, 128], BF16)
    onesb_r = A("onesb_r", [1, 128], BF16)
    onesb_c = A("onesb_c", [128, 1], BF16)
    onesf_r = A("onesf_r", [1, 128], F32)
    eps_t = A("eps_t", [1, 1], F32)
    eps5_t = A("eps5_t", [1, 1], F32)
    nshift = A("nshift", [128, 1], F32)
    wrow = A("wrow", [1, 32], F32)
    wclip = A("wclip", [1, 32], F32)
    urow = A("urow", [1, 64], F32)
    patcol = A("patcol", [16, 4], F32)
    pat_t = A("pat", [128, 512], BF16)
    pat = vsplit(pat_t[:, :], 4, 128)
    wstd_c = A("wstd_c", [128, 16], F32)
    bqc_sb = A("bqc_sb", [128, 16], F32)
    boc_sb = A("boc_sb", [128, 16], F32)
    r01_sb = A("r01_sb", [16, 128], F32)
    ident = A("ident", [128, 128], BF16)

    pb = [nc.alloc_psum_tensor(f"pb{i}", [128, 512], F32) for i in range(8)]
    pb_bf = [b[:, :].bitcast(BF16) for b in pb]

    def mm(out_ap, lhsT, rhs, start, stop, waits=(), inc=None):
        return p.em("pe", lambda t: t.matmul(out_ap, lhsT, rhs, start=start, stop=stop),
                    waits=waits, inc=inc)

    def trans(out_ap, in_ap, waits=(), inc="pe"):
        return p.em("pe", lambda t: t.transpose(out_ap, in_ap, ident[:, :]),
                    waits=waits, inc=inc)

    def dma(q, out_ap, in_ap, waits=(), inc=None):
        return p.em(q, lambda e: e.dma_start(out=out_ap, in_=in_ap),
                    waits=waits, inc=inc, amt=16)

    def coll(kind, op, ins, outs, waits, inc):
        def fn(e):
            return e.collective_compute(kind, op, replica_groups=GROUPS, ins=ins, outs=outs)
        return p.em("gp", fn, waits=waits, inc=inc, amt=1)

    def xbf(fc, c0, w):
        return arena1[:, SC * fc + c0: SC * fc + c0 + w]

    dma("sync", ident[:, :], ident_i, inc="dsc")
    dma("sync", bqc_sb[:, :], bq_c, inc="dsc")
    dma("sync", boc_sb[:, :], bo_c, inc="dsc")
    dma("sync", wrow[:, :], wsr_i, inc="dsc")
    d_r01 = dma("sync", r01_sb[:, :], r01_i, inc="dsc")
    d_ident = d_bq = d_bo = d_wsr = d_r01 = p.now("dsc")
    d_x = dma("gp", arena1[:, :], x_img, inc="dg")

    p.em("dve", lambda v: v.memset(onesb_r[:, :], 1.0), inc="dve")
    p.em("dve", lambda v: v.memset(onesb_c[:, :], 1.0), inc="dve")
    p.em("dve", lambda v: v.memset(onesf_r[:, :], 1.0), inc="dve")
    m_eps = p.em("dve", lambda v: v.memset(eps_t[:, :], 1e-8), inc="dve")
    m_eps5 = p.em("dve", lambda v: v.memset(eps5_t[:, :], 1e-5), inc="dve")
    m_nsh = p.em("dve", lambda v: v.memset(nshift[:, :], -EXP_SHIFT), inc="dve")
    for sc in range(8):
        view = cdv[:, sc * (H * 129):(sc + 1) * (H * 129)].rearrange(
            "p (h c) -> p h c", h=H)[:, :, 128:129]
        p.em("dve", lambda v, vw=view: v.memset(vw, 1.0), inc="dve")
    mones = p.now("dve")

    mc = p.em("dve", lambda v: v.tensor_scalar_max(wclip[:, :], wrow[:, :], 1e-8),
              waits=[d_wsr], inc="dve")
    psub = p.em("dve", lambda v: v.tensor_scalar_sub(urow[0:1, 0:16], wrow[0:1, 16:32], 0.1),
                waits=[d_wsr], inc="dve")
    asg = p.em("act", lambda s: s.activation(urow[0:1, 0:16], urow[0:1, 0:16], AF.Sign,
                                             bias=0.0, scale=1.0),
               waits=[psub], inc="act")
    m_u = p.em("dve", lambda v: v.tensor_scalar_max(urow[0:1, 0:16], urow[0:1, 0:16], 0.0),
               waits=[asg], inc="dve")
    m_a = p.em("dve", lambda v: v.tensor_mul(urow[0:1, 16:32], urow[0:1, 0:16],
                                             wclip[0:1, 16:32]),
               waits=[m_u, mc], inc="dve")
    m_b = p.em("dve", lambda v: v.tensor_sub(urow[0:1, 32:48], wclip[0:1, 16:32],
                                             urow[0:1, 16:32]),
               waits=[m_a], inc="dve")
    m_n = p.em("dve", lambda v: v.tensor_scalar(urow[0:1, 48:64], urow[0:1, 0:16],
                                                -1.0, 1.0, ALU.mult, ALU.add),
               waits=[m_u], inc="dve")
    mrows = p.now("dve")

    srcrow = {0: 16, 1: 32, 2: 0, 3: 48}
    for i in range(4 if PH0_PATTERNS else 0):
        r_ = srcrow[i]
        m1 = mm(pb[0][0:16, 0:1], urow[0:1, r_:r_ + 16], onesf_r[0:1, 0:1], True, True,
                waits=[mrows, mones] + p.takewaits(("bank", 0)), inc="pe")
        c1 = p.em("dve", lambda v, i=i: v.tensor_copy(patcol[:, i:i + 1], pb[0][0:16, 0:1]),
                  waits=[m1], inc="dve")
        p.use(("bank", 0), c1)
        m2 = mm(pb[1][0:1, 0:128], patcol[:, i:i + 1], r01_sb[:, :], True, True,
                waits=[c1, d_r01] + p.takewaits(("bank", 1)), inc="pe")
        c2 = p.em("dve", lambda v, i=i: v.tensor_copy(patrow[0:1, i * 128:(i + 1) * 128],
                                                      pb[1][0:1, 0:128]),
                  waits=[m2], inc="dve")
        p.use(("bank", 1), c2)
        m3 = mm(pb[2][:, 0:128], onesf_r[0:1, :], patrow[0:1, i * 128:(i + 1) * 128],
                True, True, waits=[c2] + p.takewaits(("bank", 2)), inc="pe")
        c3 = p.em("dve", lambda v, i=i: v.tensor_copy(pat[i], pb[2][:, 0:128]),
                  waits=[m3], inc="dve")
        p.use(("bank", 2), c3)
    if PH0_PATTERNS:
        m4 = mm(pb[3][:, 0:16], onesf_r[0:1, :], wclip[0:1, 0:16], True, True,
                waits=[mc, mones] + p.takewaits(("bank", 3)), inc="pe")
        c4 = p.em("dve", lambda v: v.tensor_copy(wstd_c[:, :], pb[3][:, 0:16]),
                  waits=[m4], inc="dve")
        p.use(("bank", 3), c4)
    mpat = p.now("dve")

    p.barrier()

    if stop_phase <= 0:
        return _finish(nc, p, sem, stck)

    PHASE_MARKS.append(("PH1", sum(1 for o in p.ops if o[0] == "pe")))
    def tm_proj_b1(img_ap, bias_row, is_k):
        dws = [dma("gp", wbuf[0, :, :], img_ap[:, 0:2176],
                   waits=p.takewaits(("wbuf", 0)), inc="dgw0"),
               dma("gp", wbuf[1, :, :], img_ap[:, 2176:4352],
                   waits=p.takewaits(("wbuf", 1)), inc="dgw1")]
        for hidx in range(PH1_LIMIT):
            par = hidx % 2
            for wave in range(2):
                mss = []
                for i in range(4):
                    sc = wave * 4 + i
                    bidx = 4 * wave + i
                    bank = pb[bidx]
                    bw = [dws[hidx], d_x] + p.takewaits(("bank", bidx))
                    for fc in range(NFC):
                        mm(bank[:, :128], xbf(fc, sc * 128, 128),
                           wbuf[par, :, 128 * fc:128 * (fc + 1)],
                           start=(fc == 0), stop=False,
                           waits=bw if fc == 0 else [])
                        bw = []
                    ms = mm(bank[:, :128], onesb_r[0:1, :],
                            wbuf[par, 0:1, 2048:2176],
                            start=False, stop=True,
                            waits=[mones], inc="pe")
                    mss.append(ms)
                for i in range(4 if PH1_EVAC else 0):
                    sc = wave * 4 + i
                    bidx = 4 * wave + i
                    bank = pb[bidx]
                    if is_k:
                        pr = i % 4
                        t_t = trow8[2 * pr][:, :128]
                        t_r = trow8[2 * pr + 1][:, :128]
                        a1 = p.em("act", lambda s, b=bank, t=t_t: s.activation(
                            t, b[:, :128], AF.Exp, bias=0.0, scale=1.0),
                            waits=[mss[i]] + p.takewaits(("trow8", pr)), inc="act")
                        a2 = p.em("dve", lambda v, b=bank, t=t_r: v.tensor_scalar_max(
                            t, b[:, :128], 0.0),
                            waits=[mss[i], a1], inc="dve")
                        dst = cdk[:, 2048 * sc + 128 * hidx: 2048 * sc + 128 * (hidx + 1)]
                        a3 = p.em("dve", lambda v, d=dst, t1=t_t, t2=t_r:
                                  v.scalar_tensor_tensor(d, t1, 1.0, t2, ALU.min, ALU.add),
                            waits=[a1, a2], inc="dve")
                        p.use(("trow8", pr), a3)
                        p.use(("bank", bidx), a1, a2)
                        p.use(("cdk_w",), a3)
                    else:
                        dstv = cdv[:, sc * (H * 129) + 129 * hidx:
                                   sc * (H * 129) + 129 * hidx + 128]
                        a1 = p.em("act", lambda s, d=dstv, b=bank: s.copy(d, b[:, :128]),
                                  waits=[mss[i]], inc="act")
                        p.use(("bank", bidx), a1)
                        p.use(("cdv_w",), a1)
            p.use(("wbuf", par), p.now("pe"))
            if hidx + 2 < 16:
                par2 = (hidx + 2) % 2
                dws.append(dma("gp", wbuf[par2, :, :],
                               img_ap[:, (hidx + 2) * 2176:(hidx + 3) * 2176],
                               waits=p.takewaits(("wbuf", par2)),
                               inc="dgw%d" % par2))

    tm_proj_b1(wk_i, bk_r, True)
    if stop_phase <= 1:
        return _finish(nc, p, sem, stck)
    PHASE_MARKS.append(("PH2", sum(1 for o in p.ops if o[0] == "pe")))
    tm_proj_b1(wv_i, bv_r, False)
    if stop_phase <= 2:
        return _finish(nc, p, sem, stck)

    PHASE_MARKS.append(("PH3", sum(1 for o in p.ops if o[0] == "pe")))
    kq = p.takewaits(("cdk_w",)) + p.takewaits(("cdv_w",))
    kv_ev = []
    for h in range(H):
        bidx = h % 2
        bank = pb[bidx]
        bw = kq + p.takewaits(("bank", bidx))
        kq = []
        ms = None
        for sc in range(8):
            ms = mm(bank[:, :129],
                    cdk[:, 2048 * sc + 128 * h: 2048 * sc + 128 * (h + 1)],
                    cdv[:, sc * H * 129 + 129 * h: sc * H * 129 + 129 * (h + 1)],
                    start=(sc == 0), stop=(sc == 7),
                    waits=bw if sc == 0 else [], inc="pe" if sc == 7 else None)
            bw = []
        ev = p.em("act", lambda s, b=bank, h=h: s.copy(kvst[h % 2], b[:, :129]),
                  waits=[ms] + p.takewaits(("kvst", h % 2)), inc="act")
        p.use(("bank", bidx), ev)
        d_h = dma("sync", kv_in.ap()[:, 129 * h:129 * (h + 1)], kvst[h % 2],
                  waits=[ev], inc="dckv")
        p.use(("kvst", h % 2), d_h)
        kv_ev.append(d_h)
    coll("AllReduce", ALU.add, [kv_in.ap()], [kv_out.ap()], waits=kv_ev, inc="cc1")

    if "kv" in dbg:
        dma("sync", dbg["kv"], kv_out.ap(), waits=[("cc1", 1)], inc="ds")

    if stop_phase <= 3:
        return _finish(nc, p, sem, stck)

    PHASE_MARKS.append(("PH4", sum(1 for o in p.ops if o[0] == "pe")))
    nrow = 8

    def emit_qproj(h):
        par = h % 2
        wsi = h % 3
        dw = dwq[h]
        ev_q = {}
        mss = {}
        for slot in range(2):
            bidx = (2 * h + slot) % 4
            bank = pb[bidx]
            bw = [dw[0], d_x] + p.takewaits(("bank", bidx))
            ms = None
            for fc in range(NFC):
                if fc in (4, 8, 12):
                    bw = bw + [dw[fc // 4]]
                ms = mm(bank[:, :512],
                        wpan[wsi][:, 128 * fc:128 * (fc + 1)],
                        xbf(fc, slot * 512, 512),
                        start=(fc == 0), stop=(fc == NFC - 1),
                        waits=bw,
                        inc="pe" if fc == NFC - 1 else None)
                bw = []
            mss[slot] = ms
        p.use(wkey(wsi), p.now("pe"))
        for slot in range(2):
            bidx = (2 * h + slot) % 4
            bank = pb[bidx]
            a1 = p.em("act", lambda s, b=bank, h=h: s.activation(
                trow[0][:, :512], b[:, :512], AF.Exp,
                bias=bqc_sb[:, h:h + 1], scale=1.0),
                waits=[mss[slot], d_bq] + p.takewaits(("trow", 0)), inc="act")
            a2 = p.em("dve", lambda v, b=bank, h=h: v.tensor_scalar(
                trow[1][:, :512], b[:, :512], bqc_sb[:, h:h + 1], 0.0,
                ALU.add, ALU.max),
                waits=[mss[slot], d_bq, a1], inc="dve")
            dstq = qbuf[par, :, 512 * slot:512 * (slot + 1)]
            a3 = p.em("dve", lambda v, d=dstq: v.scalar_tensor_tensor(
                d, trow[0][:, :512], 1.0, trow[1][:, :512], ALU.min, ALU.add),
                waits=[a1, a2] + p.takewaits(("qbuf", par, slot)), inc="dve")
            p.use(("trow", 0), a3)
            p.use(("trow", 1), a3)
            p.use(("bank", bidx), a1, a2)
            ev_q[slot] = a3
        return ev_q

    R4 = [rows[0:1, 0:512], rows[0:1, 512:1024],
          rows2[0:1, 0:512], rows2[0:1, 512:1024]]
    pend_n = []

    def attn1_stage1(h, ev_q):
        par = h % 2
        for slot in range(2):
            i2 = 2 * h + slot
            ridx = i2 % 4
            qsl = qbuf[par, :, 512 * slot:512 * (slot + 1)]
            b_u = pb[4 + slot]
            msu = mm(b_u[:, :512], kvbf[:, 129 * h:129 * h + 128], qsl, True, True,
                     waits=[ev_q[slot], d_kvbf] + p.takewaits(("bank", 4 + slot)),
                     inc="pe")
            msd = mm(pb[6][0:1, :512], kvbf[:, 129 * h + 128:129 * h + 129], qsl,
                     True, True, waits=p.takewaits(("bank", 6)), inc="pe")
            p.use(("qbuf", par, slot), msu, msd)
            a_d = p.em("act", lambda s, r=R4[ridx]: s.activation(
                r, pb[6][0:1, :512], AF.Identity,
                bias=eps_t[0:1, 0:1], scale=1.0),
                waits=[msd, m_eps] + p.takewaits(("rowsA", ridx)), inc="act")
            p.use(("bank", 6), a_d)
            r_d = p.em("dve", lambda v, r=R4[ridx]: v.reciprocal(r, r),
                       waits=[a_d], inc="dve")
            dsta = cdk[:, 1024 * h + 512 * slot: 1024 * h + 512 * (slot + 1)]
            a_u = p.em("act", lambda s, d=dsta, b=b_u: s.copy(d, b[:, :512]),
                       waits=[msu], inc="act")
            p.use(("bank", 4 + slot), a_u)
            pend_n.append((ridx, r_d, a_u, dsta))

    def attn1_stage2():
        while pend_n:
            ridx, r_d, a_u, dsta = pend_n.pop(0)
            bb = 7 - (ridx % 2)
            msb = mm(pb[bb][:, :512], onesf_r[0:1, :], R4[ridx], True, True,
                     waits=[r_d] + p.takewaits(("bank", bb)), inc="pe")
            t_m = p.em("dve", lambda v, d=dsta, bb=bb: v.tensor_mul(d, d, pb[bb][:, :512]),
                       waits=[a_u, msb], inc="dve")
            p.use(("bank", bb), t_m)
            p.use(("rowsA", ridx), msb)
            p.use(("attn1",), t_m)

    dwq = {}

    def qproj_dma(h):
        wsi = h % 3
        ww = p.takewaits(wkey(wsi))
        ms4 = []
        for q in range(4):
            ms4.append(dma("gp", wpan[wsi][:, 512 * q:512 * (q + 1)],
                           wq_i[:, h * 2048 + 512 * q:h * 2048 + 512 * (q + 1)],
                           waits=ww, inc=WQ_SEMS[q] % wsi))
            ww = []
        dwq[h] = ms4

    qproj_dma(0)
    qproj_dma(1)
    qproj_dma(2)
    d_kvbf = dma("gp", kvbf[:, :], kv_out.ap(), waits=[("cc1", 1)], inc="dkv")
    ev_prev = emit_qproj(0)
    for h in range(16):
        if h + 3 < 16:
            qproj_dma(h + 3)
        attn1_stage2()
        attn1_stage1(h, ev_prev)
        if h + 1 < 16:
            ev_prev = emit_qproj(h + 1)
    attn1_stage2()

    p.barrier()

    PHASE_MARKS.append(("PH5", sum(1 for o in p.ops if o[0] == "pe")))
    def proj_resid_stats(img_ap, rhs_fn, bias_col, xres_fn, zdst_ap,
                         bf_fn=None, bft3=None, sqt3=None, xfix=None):
        pend = []
        stat_started = [False, False]
        stat_sl = [[], []]

        def flush_stats(keep=0):
            while len(pend) > keep:
                jc, slot, zb_ms, zq_ms, zrot, s1rhs, trk = pend.pop(0)
                b_s, b_q = pb[4 + slot], pb[6 + slot]
                first = not stat_started[slot]
                w1 = [zb_ms] + (p.takewaits(("bank", 4 + slot)) if first else [])
                w2 = [zq_ms] + (p.takewaits(("bank", 6 + slot)) if first else [])
                s1 = mm(b_s[0:1, :512], onesb_c[:, 0:1], s1rhs,
                        start=first, stop=(jc == NFC - 1), waits=w1, inc="pe")
                s2 = mm(b_q[0:1, :512], onesb_c[:, 0:1], sqt3[zrot],
                        start=first, stop=(jc == NFC - 1), waits=w2, inc="pe")
                stat_started[slot] = True
                if trk:
                    p.use(("zbft8", zrot), s1)
                p.use(("zsqt8", zrot), s2)
                if jc == NFC - 1:
                    stat_sl[slot] = [s1, s2]

        def w_dma(jc):
            si2 = jc % 3
            ww = p.takewaits(wkey(si2))
            ms4 = []
            for q in range(4):
                ms4.append(dma("gp", wpan[si2][:, 512 * q:512 * (q + 1)],
                               img_ap[:, jc * 2048 + 512 * q:
                                      jc * 2048 + 512 * (q + 1)],
                               waits=ww, inc=WQ_SEMS[q] % si2))
                ww = []
            return ms4

        dws = [w_dma(0), w_dma(1), w_dma(2)]
        for jc in range(16):
            par = jc % 3
            for slot in range(2):
                idx = 2 * jc + slot
                bidx = slot
                bank = pb[bidx]
                xrot = idx % 4
                d_x0 = dma("sync", x0t[xrot, :, :], xres_fn(jc, slot),
                           waits=p.takewaits(("x0t", xrot)), inc="dsx%d" % xrot)
                if xfix is not None:
                    axc, bcm8 = xfix
                    c1 = p.em("dve", lambda v, xr=xrot, s=slot: v.tensor_mul(
                        x0t[xr, :, :], x0t[xr, :, :], axc[s, 0, :, :]),
                        waits=[d_x0, bcm8[slot][0]], inc="dve")
                    c2 = p.em("dve", lambda v, xr=xrot, s=slot: v.tensor_add(
                        x0t[xr, :, :], x0t[xr, :, :], axc[s, 1, :, :]),
                        waits=[c1, bcm8[slot][1]], inc="dve")
                    p.use(("abcA", slot), c1)
                    p.use(("abcB", slot), c2)
                    xr_ms = c2
                else:
                    xr_ms = d_x0
                bw = [dws[jc][0]] + p.takewaits(("bank", bidx))
                ms = None
                for ec in range(NFC):
                    if ec in (4, 8, 12):
                        bw = bw + [dws[jc][ec // 4]]
                    ms = mm(bank[:, :512],
                            wpan[par][:, 128 * ec:128 * (ec + 1)],
                            rhs_fn(ec, slot),
                            start=(ec == 0), stop=(ec == NFC - 1),
                            waits=bw,
                            inc="pe" if ec == NFC - 1 else None)
                    bw = []
                zr = idx % 3
                if bias_col is not None:
                    ev = p.em("dve", lambda v, b=bank, jc=jc, zr=zr, xr=xrot:
                              v.scalar_tensor_tensor(zt[zr][:, :512], b[:, :512],
                                                     bias_col[:, jc:jc + 1],
                                                     x0t[xr, :, :], ALU.add, ALU.add),
                              waits=[ms, xr_ms, d_bo] + p.takewaits(("zt", zr)),
                              inc="dve")
                else:
                    ev = p.em("dve", lambda v, b=bank, zr=zr, xr=xrot:
                              v.scalar_tensor_tensor(zt[zr][:, :512], b[:, :512],
                                                     1.0, x0t[xr, :, :],
                                                     ALU.mult, ALU.add),
                              waits=[ms, xr_ms] + p.takewaits(("zt", zr)), inc="dve")
                p.use(("x0t", xrot), ev)
                p.use(("bank", bidx), ev)
                d_z = dma("sync", zdst_ap[128 * jc:128 * (jc + 1),
                                          512 * slot:512 * (slot + 1)],
                          zt[zr][:, :512], waits=[ev], inc="dsz%d" % zr)
                brot = idx % 3
                if bf_fn is not None:
                    bfdst = bf_fn(jc, slot)
                    zb = p.em("dve", lambda v, zr=zr, d=bfdst:
                              v.tensor_copy(d, zt[zr][:, :512]),
                              waits=[ev], inc="dve")
                    s1rhs, trk = bfdst, False
                else:
                    zb = p.em("dve", lambda v, zr=zr, br=brot:
                              v.tensor_copy(bft3[br], zt[zr][:, :512]),
                              waits=[ev] + p.takewaits(("zbft8", brot)), inc="dve")
                    s1rhs, trk = bft3[brot], True
                zq = p.em("act", lambda s, zr=zr, br=brot:
                          s.activation(sqt3[br], zt[zr][:, :512],
                                       AF.Square, bias=0.0, scale=1.0),
                          waits=[zb] + p.takewaits(("zsqt8", brot)), inc="act")
                p.use(("zt", zr), d_z, zq)
                pend.append((jc, slot, zb, zq, brot, s1rhs, trk))
                flush_stats(keep=2)
            p.use(wkey(par), p.now("pe"))
            if jc + 3 < 16:
                dws.append(w_dma(jc + 3))
        flush_stats()
        return stat_sl

    def ln_rows(base, stat_sl, mkbf=False):
        outm = []
        en_bf = []
        ea_bf = []
        for slot in range(2):
            b_s, b_q = pb[4 + slot], pb[6 + slot]
            rw = p.takewaits(("rowsA",)) + p.takewaits(("rowsB",))
            a1 = p.em("act", lambda s, b=b_s: s.activation(
                rows[0:1, 0:512], b[0:1, :512], AF.Identity, bias=0.0,
                scale=-1.0 / 2048.0), waits=stat_sl[slot] + rw, inc="act")
            a2 = p.em("act", lambda s, b=b_q: s.activation(
                rows[0:1, 512:1024], b[0:1, :512], AF.Identity, bias=0.0,
                scale=1.0 / 2048.0), inc="act")
            p.use(("bank", 4 + slot), a1)
            p.use(("bank", 6 + slot), a2)
            if mkbf:
                mn = mm(pb[4 + slot][:, :512], onesf_r[0:1, :], rows[0:1, 0:512],
                        True, True, waits=[a1] + p.takewaits(("bank", 4 + slot)),
                        inc="pe")
                p.use(("rowsA",), mn)
                en = p.em("act", lambda s, sl=slot: s.copy(
                    zbft[sl][:, :512], pb[4 + sl][:, :512]),
                    waits=[mn] + p.takewaits(("zbft8", slot)), inc="act")
                p.use(("bank", 4 + slot), en)
                en_bf.append(en)
            d1 = p.em("dve", lambda v: v.tensor_mul(rows2[0:1, 0:512],
                                                    rows[0:1, 0:512], rows[0:1, 0:512]),
                      waits=[a1] + p.takewaits(("rows2",)), inc="dve")
            d2 = p.em("dve", lambda v: v.tensor_sub(rows[0:1, 512:1024],
                                                    rows[0:1, 512:1024], rows2[0:1, 0:512]),
                      waits=[a2, d1], inc="dve")
            a3 = p.em("act", lambda s: s.activation(rows[0:1, 512:1024],
                                                    rows[0:1, 512:1024], AF.Sqrt,
                                                    bias=eps5_t[0:1, 0:1], scale=1.0),
                      waits=[d2, m_eps5], inc="act")
            d3 = p.em("dve", lambda v: v.reciprocal(rows2[0:1, 0:512],
                                                    rows[0:1, 512:1024]),
                      waits=[a3], inc="dve")
            dA = dma("sync", rowbuf.ap()[base + 2 * slot:base + 2 * slot + 1, :],
                     rows2[0:1, 0:512], waits=[d3], inc="dsr")
            if mkbf:
                ma = mm(pb[6 + slot][:, :512], onesf_r[0:1, :], rows2[0:1, 0:512],
                        True, True, waits=[d3] + p.takewaits(("bank", 6 + slot)),
                        inc="pe")
                p.use(("rows2",), ma)
                ea = p.em("act", lambda s, sl=slot: s.copy(
                    zsqt[sl][:, :512], pb[6 + sl][:, :512]),
                    waits=[ma] + p.takewaits(("zsqt8", slot)), inc="act")
                p.use(("bank", 6 + slot), ea)
                ea_bf.append(ea)
            d4 = p.em("dve", lambda v: v.scalar_tensor_tensor(
                rows2[0:1, 512:1024], rows[0:1, 0:512], 1.0, rows2[0:1, 0:512],
                ALU.mult, ALU.mult), waits=[d3], inc="dve")
            dB = dma("sync", rowbuf.ap()[base + 2 * slot + 1:base + 2 * slot + 2, :],
                     rows2[0:1, 512:1024], waits=[d4], inc="dsra")
            p.use(("rowsA",), d1, d4)
            p.use(("rowsB",), a3)
            p.use(("rows2",), dA, dB, d4)
            outm.append((dA, dB))
        return outm, en_bf, ea_bf

    def ln_normalize(base, rowdma, xout_ap, zsrc_ap):
        bms = {}
        for slot in range(2):
            dA = dma("gp", abc[slot, 0, :, :],
                     rowbuf.ap()[base + 2 * slot:base + 2 * slot + 1, :]
                     .partition_broadcast(128),
                     waits=[rowdma[slot][0]] + p.takewaits(("abcA", slot)),
                     inc="dgr%d" % (2 * slot))
            dB = dma("gp", abc[slot, 1, :, :],
                     rowbuf.ap()[base + 2 * slot + 1:base + 2 * slot + 2, :]
                     .partition_broadcast(128),
                     waits=[rowdma[slot][1]] + p.takewaits(("abcB", slot)),
                     inc="dgr%d" % (2 * slot + 1))
            bms[slot] = (dA, dB)
        zts = [zt[0], zt[1], zt[2], x0t[0], x0t[1], x0t[2], x0t[3]]

        def zkey(zr):
            return ("zt", zr) if zr < 3 else ("x0t", zr - 3)

        def zsem(zr):
            return "dsz%d" % zr if zr < 3 else "dsx%d" % (zr - 3)

        lastms = []
        for jc in range(NFC):
            for slot in range(2):
                idx = 2 * jc + slot
                zr = idx % 7
                d_z = dma("sync", zts[zr][:, :512],
                          zsrc_ap[128 * jc:128 * (jc + 1), 512 * slot:512 * (slot + 1)],
                          waits=p.takewaits(zkey(zr)), inc=zsem(zr))
                eng = "dve" if (idx % 2 == 0) else "gp"
                esem = "dve" if eng == "dve" else "gpc"
                t1 = p.em(eng, lambda v, zr=zr, s=slot: v.tensor_mul(
                    zts[zr][:, :512], zts[zr][:, :512], abc[s, 0, :, :]),
                    waits=[d_z, bms[slot][0]], inc=esem)
                t2 = p.em(eng, lambda v, zr=zr, s=slot: v.tensor_add(
                    zts[zr][:, :512], zts[zr][:, :512], abc[s, 1, :, :]),
                    waits=[t1, bms[slot][1]], inc=esem)
                p.use(("abcA", slot), t1)
                p.use(("abcB", slot), t2)
                d_o = dma("act", xout_ap[128 * jc:128 * (jc + 1),
                                         512 * slot:512 * (slot + 1)],
                          zts[zr][:, :512], waits=[t2], inc="dsc")
                p.use(zkey(zr), d_o)
                lastms.append(d_o)
        return lastms

    def wa_dma(pi, si):
        width = WA_RANGES[pi][1]
        ww = p.takewaits(wkey(si))
        ms4 = []
        for q in range(4):
            ms4.append(dma("gp", wpan[si][:, 4 * q * width:4 * (q + 1) * width],
                           wa_i[:, WA_OFF[pi] + 4 * q * width:
                                WA_OFF[pi] + 4 * (q + 1) * width],
                           waits=ww, inc=WQ_SEMS[q] % si))
            ww = []
        return ms4

    sm1 = proj_resid_stats(
        wo_i, lambda ec, slot: cdk[:, 1024 * ec + 512 * slot: 1024 * ec + 512 * (slot + 1)],
        boc_sb,
        lambda jc, slot: x_img[:, 1024 * jc + 512 * slot: 1024 * jc + 512 * (slot + 1)],
        zbuf.ap(),
        bf_fn=lambda jc, slot: arena1[:, SC * jc + 512 * slot:
                                      SC * jc + 512 * (slot + 1)],
        sqt3=[zsqt[0], zsqt[1], kvs[0]])
    p.use(("cdk_r",), p.now("pe"))
    dws6 = {}

    def kstd_dma(h):
        dws6[h] = wa_dma(h, h % 3)

    kstd_dma(0)
    kstd_dma(1)
    kstd_dma(2)
    lr1, en_bf, ea_bf = ln_rows(0, sm1, mkbf=True)
    x1bf_ms = {}
    cc_slot = {0: [], 1: []}
    for jc in range(NFC):
        for slot in range(2):
            idx = 2 * jc + slot
            eng = "dve" if idx % 2 == 0 else "gp"
            esem = "dve" if eng == "dve" else "gpc"
            tile = arena1[:, SC * jc + 512 * slot: SC * jc + 512 * (slot + 1)]
            c1 = p.em(eng, lambda v, t=tile, s=slot: v.tensor_add(t, t, zbft[s]),
                      waits=[en_bf[slot]], inc=esem)
            c2 = p.em(eng, lambda v, t=tile, s=slot: v.tensor_mul(t, t, zsqt[s]),
                      waits=[c1, ea_bf[slot]], inc=esem)
            x1bf_ms.setdefault(jc, []).append(c2)
            cc_slot[slot].append(c2)
    for slot in range(2):
        p.use(("zbft8", slot), *cc_slot[slot])
        p.use(("zsqt8", slot), *cc_slot[slot])

    if stop_phase <= 5:
        return _finish(nc, p, sem, stck)

    PHASE_MARKS.append(("PH6k", sum(1 for o in p.ops if o[0] == "pe")))
    for h in range(16):
        par = h % 3
        mss = {}
        for slot in range(2):
            bank = pb[slot]
            bw = [dws6[h][0]] + p.takewaits(("bank", slot))
            ms = None
            for fc in range(NFC):
                if fc in (4, 8, 12):
                    bw = bw + [dws6[h][fc // 4]]
                if h == 0:
                    bw = bw + x1bf_ms.get(fc, [])
                ms = mm(bank[0:120, :512],
                        wpan[par][:, 120 * fc:120 * (fc + 1)],
                        xbf(fc, slot * 512, 512),
                        start=(fc == 0), stop=(fc == NFC - 1),
                        waits=bw,
                        inc="pe" if fc == NFC - 1 else None)
                bw = []
            mss[slot] = ms
        p.use(wkey(par), p.now("pe"))
        if h + 3 < 16:
            kstd_dma(h + 3)
        for slot in range(2):
            bank = pb[slot]
            dst = cdk[0:120, 1024 * h + 512 * slot: 1024 * h + 512 * (slot + 1)]
            ev = p.em("act", lambda s, d=dst, b=bank: s.copy(d, b[0:120, :512]),
                      waits=[mss[slot]] + p.takewaits(("cdk_r",)), inc="act")
            p.use(("bank", slot), ev)
            p.use(("kstd",), ev)

    PHASE_MARKS.append(("PH6l", sum(1 for o in p.ops if o[0] == "pe")))
    lowbuf = mask_sb
    dwl_w = (p.takewaits(("x0t", 0)) + p.takewaits(("x0t", 1)) +
             p.takewaits(("x0t", 2)) + p.takewaits(("x0t", 3)) +
             p.takewaits(("trow", 0)) + p.takewaits(("trow", 1)))
    dwl = dma("gp", lowbuf[:, 0:4096], wa_i[:, WA_OFF[16]:WA_OFF[17]],
              waits=dwl_w, inc="dgl")
    sel_ev = []
    for wave in range(2):
        mss = []
        for i in range(4):
            sc = 4 * wave + i
            bidx = 4 + i
            bank = pb[bidx]
            bw = [dwl] + p.takewaits(("bank", bidx))
            ms = None
            for fc in range(NFC):
                ms = mm(bank[:, :256], xbf(fc, sc * 128, 128),
                        lowbuf[:, 256 * fc:256 * (fc + 1)],
                        start=(fc == 0), stop=(fc == NFC - 1),
                        waits=bw if fc == 0 else [],
                        inc="pe" if fc == NFC - 1 else None)
                bw = []
            mss.append(ms)
        for i in range(4):
            sc = 4 * wave + i
            bidx = 4 + i
            bank = pb[bidx]
            e1 = p.em("dve", lambda v, b=bank: v.tensor_mul(seltmp[0], b[:, 128:256], pat[0]),
                      waits=[mss[i], mpat] + p.takewaits(("seltmp",)) +
                      p.takewaits(("zbft8", 0)) + p.takewaits(("zbft8", 1)) +
                      p.takewaits(("zsqt8", 0)) + p.takewaits(("zsqt8", 1)), inc="dve")
            e2 = p.em("dve", lambda v, b=bank: v.tensor_mul(seltmp[1], b[:, 0:128], pat[1]),
                      waits=[mss[i]], inc="dve")
            e3 = p.em("dve", lambda v, sc=sc: v.tensor_add(selq_tm[sc], seltmp[0], seltmp[1]),
                      waits=[e1, e2] + p.takewaits(("u4q", sc)), inc="dve")
            e4 = p.em("dve", lambda v, b=bank: v.tensor_mul(seltmp[0], b[:, 0:128], pat[2]),
                      waits=[e3], inc="dve")
            e5 = p.em("dve", lambda v, b=bank: v.tensor_mul(seltmp[1], b[:, 128:256], pat[3]),
                      waits=[e3], inc="dve")
            e6 = p.em("dve", lambda v, sc=sc: v.tensor_add(selk_tm[sc], seltmp[0], seltmp[1]),
                      waits=[e4, e5] + p.takewaits(("u4k", sc)), inc="dve")
            p.use(("seltmp",), e6)
            p.use(("bank", bidx), e4, e5)
            sel_ev.append((e3, e6))
    for sc in range(8):
        t1 = trans(pb_bf[7][:, 0:128], selq_tm[sc],
                   waits=[sel_ev[sc][0], d_ident] + p.takewaits(("bank", 7)))
        c1 = p.em("act", lambda s, sc=sc: s.copy(selq_fm[:, 128 * sc:128 * (sc + 1)],
                                                 pb_bf[7][:, 0:128]),
                  waits=[t1], inc="act")
        p.use(("bank", 7), c1)
        t2 = trans(pb_bf[7][:, 0:128], selk_tm[sc],
                   waits=[sel_ev[sc][1]] + p.takewaits(("bank", 7)))
        c2 = p.em("act", lambda s, sc=sc: s.copy(selk_fm[:, 128 * sc:128 * (sc + 1)],
                                                 pb_bf[7][:, 0:128]),
                  waits=[t2], inc="act")
        p.use(("bank", 7), c2)
        p.use(("selq_fm",), c1)
        p.use(("selk_fm",), c2)

    low_last_pe = p.now("pe")

    wkt = p.takewaits(("selk_fm",)) + p.takewaits(("kstd",))
    ktail = []
    for h in range(H):
        d_t = dma("sync", cdk[120:128, 1024 * h:1024 * (h + 1)],
                  selk_fm[8 * h:8 * (h + 1), :],
                  waits=wkt if h == 0 else [], inc="dskt")
        wkt = []
        ktail.append(d_t)
    d_kf = dma("sync", kf_in.ap().rearrange("(h p) s -> p h s", p=128),
               cdk[:, :].rearrange("p (h s) -> p h s", h=H),
               waits=ktail, inc="dkf")
    coll("AllGather", ALU.bypass, [kf_in.ap()], [kf_out.ap()], waits=[d_kf], inc="cc2")

    PHASE_MARKS.append(("PH6v", sum(1 for o in p.ops if o[0] == "pe")))
    dws7 = {}

    def v_dma(h):
        dws7[h] = wa_dma(17 + h, h % 3)

    v_dma(0)
    v_dma(1)
    v_dma(2)
    for h in range(16):
        par = h % 3
        for wave in range(2):
            mss = []
            for i in range(4):
                sc = 4 * wave + i
                bank = pb[i]
                bw = [dws7[h][0]] + p.takewaits(("bank", i))
                ms = None
                for fc in range(NFC):
                    if fc in (4, 8, 12):
                        bw = bw + [dws7[h][fc // 4]]
                    ms = mm(bank[:, :128], xbf(fc, sc * 128, 128),
                            wpan[par][:, 128 * fc:128 * (fc + 1)],
                            start=(fc == 0), stop=(fc == NFC - 1),
                            waits=bw,
                            inc="pe" if fc == NFC - 1 else None)
                    bw = []
                mss.append(ms)
            for i in range(4):
                sc = 4 * wave + i
                bank = pb[i]
                dstv = cdv[:, sc * (H * 129) + 129 * h: sc * (H * 129) + 129 * h + 128]
                ev = p.em("act", lambda s, d=dstv, b=bank: s.copy(d, b[:, :128]),
                          waits=[mss[i]], inc="act")
                p.use(("bank", i), ev)
                p.use(("cdv2",), ev)
        p.use(wkey(par), p.now("pe"))
        if h + 3 < 16:
            v_dma(h + 3)
    d_vv = dma("sync", vv_in.ap().rearrange("(c p) s -> p c s", p=128),
               cdv[:, :].rearrange("p (c s) -> p c s", c=8),
               waits=p.takewaits(("cdv2",)), inc="dvv")
    coll("AllGather", ALU.bypass, [vv_in.ap()], [vv_out.ap()], waits=[d_vv], inc="cc3")

    d_mask = dma("sync", mask_sb[:, 0:4096], masks_i,
                 waits=[low_last_pe], inc="dsm")

    if "kf" in dbg:
        dma("sync", dbg["kf"], kf_out.ap(), waits=[("cc2", 1)], inc="ds")
    if "vv" in dbg:
        dma("sync", dbg["vv"], vv_out.ap(), waits=[("cc3", 1)], inc="ds")

    if stop_phase <= 6:
        return _finish(nc, p, sem, stck)

    PHASE_MARKS.append(("PH7", sum(1 for o in p.ops if o[0] == "pe")))
    dwq7 = {}

    def qstd_dma(h):
        dwq7[h] = wa_dma(33 + h, h % 2)

    class QstdFeed:

        def __init__(self, h):
            self.h = h
            self.par = h % 2
            self.dw = dwq7[h]
            self.seq = [(slot, fc) for slot in range(2) for fc in range(NFC)]
            self.i = 0
            self.evq = {}

        def feed(self, n):
            while n > 0 and self.i < len(self.seq):
                slot, fc = self.seq[self.i]
                self.i += 1
                n -= 1
                bidx = 5 + slot
                bank = pb[bidx]
                bw = []
                if fc == 0:
                    bw = [self.dw[0]] + p.takewaits(("bank", bidx))
                if fc in (4, 8, 12):
                    bw = bw + [self.dw[fc // 4]]
                ms = mm(bank[0:120, :512],
                        wbuf[self.par, :, 120 * fc:120 * (fc + 1)],
                        xbf(fc, slot * 512, 512),
                        start=(fc == 0), stop=(fc == NFC - 1),
                        waits=bw,
                        inc="pe" if fc == NFC - 1 else None)
                if fc == NFC - 1:
                    dstq = qbuf[self.par, 0:120, 512 * slot:512 * (slot + 1)]
                    ev = p.em("act", lambda s, d=dstq, b=bank, hh=self.h:
                              s.activation(d, b[0:120, :512], AF.Identity,
                                           bias=0.0, scale=wstd_c[0:120, hh:hh + 1]),
                              waits=[ms] + p.takewaits(("qbuf", self.par, slot)),
                              inc="act")
                    p.use(("bank", bidx), ev)
                    self.evq[slot] = ev

        def finish(self):
            self.feed(len(self.seq))
            h, par = self.h, self.par
            p.use(("wbuf", par), p.now("pe"))
            d_t = dma("sync", qbuf[par, 120:128, 0:1024],
                      selq_fm[8 * h:8 * (h + 1), :],
                      waits=p.takewaits(("qbuf_t", par)), inc="dst%d" % par)
            wkv = p.takewaits(("xkv", par)) + ([("cc2", 1), ("cc3", 1)] if h < 2 else [])
            l1 = dma("sync", xkv[par, :, 0:1024],
                     kf_out.ap()[128 * h:128 * (h + 1), :], waits=wkv, inc="dsk%d" % par)
            l2 = dma("sync", xkv[par, :, 1024:2048],
                     kf_out.ap()[H * 128 + 128 * h: H * 128 + 128 * (h + 1), :],
                     inc="dsk%d" % par)
            l3 = dma("sync", xkv[par, :, 2048:4112].rearrange("p (g c) -> p g c", g=16),
                     vv_out.ap().rearrange("(g p) c -> p g c", p=128)[:, :, 129 * h:129 * (h + 1)],
                     inc="dsk%d" % par)
            return [self.evq[0], self.evq[1], d_t, l1, l2, l3]

    SMIN = [0, 1, 2, 3, 0, 1, 2, 3]
    PAIR_LIVE = [[g for g in range(8) if SMIN[g] <= 1],
                 list(range(8))]

    tail_q = []
    OB7 = [2, 3, 4, 7]

    def flush_tail():
        while tail_q:
            th, tpairi, tcl = tail_q.pop(0)
            for i in range(4):
                msA = tcl[i]
                sqg = 4 * tpairi + i
                sw = p.takewaits(("seltmp",))
                r1 = p.em("dve", lambda v, b=pb[OB7[i]], q=i: v.reciprocal(
                    rec4[q], b[:, 128:129]),
                    waits=[msA] + p.takewaits(("rec", i)), inc="dve")
                a1 = p.em("dve", lambda v, b=pb[OB7[i]], q=i: v.tensor_scalar_mul(
                    atm4[q], b[:, 0:128], rec4[q]),
                    waits=[r1] + p.takewaits(("att_tm", i)) + sw, inc="dve")
                p.use(("rec", i), a1)
                tb = OB7[i]
                t1 = trans(pb_bf[tb][:, 0:128], atm4[i],
                           waits=[a1, d_ident])
                p.use(("att_tm", i), t1)
                dsta = cdk[:, SC * th + 128 * sqg: SC * th + 128 * sqg + 128]
                cm = p.em("act", lambda s, d=dsta, tb=tb: s.copy(d, pb_bf[tb][:, 0:128]),
                          waits=[t1], inc="act")
                p.use(("bank", tb), cm)
                p.use(("attn2",), cm)

    def emit_attn2(h, first_w, feeder=None):
        par = h % 2
        if "qf" in dbg and h == 13:
            dma("sync", dbg["qf"], qbuf[par, :, 0:1024], waits=list(first_w), inc="ds")
        kf = xkv[par, :, 0:2048]
        vv = xkv[par, :, 2048:4112]
        first_w = list(first_w)
        OB = [2, 3, 4, 7]
        prot = 0
        for pairi in range(2):
            base = 512 * pairi
            live = PAIR_LIVE[pairi]
            first_av = [True] * 4
            seqlist = [(g, k) for g in live for k in range(2)]
            last_touch = {}
            for i in range(4):
                s_i = (4 * pairi + i) // 2
                last_touch[i] = [gk for gk in seqlist if SMIN[gk[0]] <= s_i][-1]
            LEAD = 6
            pmeta = {}
            nseq = len(seqlist)

            def emit_S(t):
                nonlocal first_w, prot
                g, k = seqlist[t]
                w0 = max(256 * SMIN[g], base)
                width = base + 512 - w0
                sb_i = t % 2
                sbank = pb[sb_i]
                msS = mm(sbank[:, :width],
                         kf[:, 256 * g + 128 * k: 256 * g + 128 * (k + 1)],
                         qbuf[par, :, w0:base + 512], True, True,
                         waits=first_w + p.takewaits(("bank", sb_i)),
                         inc="pe")
                first_w = []
                rot = prot
                prot = (prot + 1) % 8
                pdst = pbufW[rot][:, 0:width]
                aE = p.em("act", lambda s, d=pdst, b=sbank, w=width: s.activation(
                    d, b[:, :w], AF.Exp, bias=nshift[:, 0:1], scale=SCALE2),
                    waits=[msS, m_nsh] + p.takewaits(("pbufP", rot)), inc="act")
                p.use(("bank", sb_i), aE)
                lastP = aE
                if SMIN[g] >= base // 256:
                    jm = g // 4
                    moff = ((SMIN[g] * 2 + jm) * 2 + k) * 256
                    lastP = p.em("dve", lambda v, d=pdst, mo=moff: v.tensor_mul(
                        d[:, 0:256], d[:, 0:256], mask_sb[:, mo:mo + 256]),
                        waits=[aE, d_mask], inc="dve")
                pmeta[t] = (rot, w0, lastP)

            def emit_AV(t):
                g, k = seqlist[t]
                skc = 2 * g + k
                rot, w0, lastP = pmeta.pop(t)
                n_sq = [i for i in range(4) if (4 * pairi + i) // 2 >= SMIN[g]]
                for ii, i in enumerate(n_sq):
                    col = 128 * i + base - w0
                    last_ch = (last_touch[i] == (g, k))
                    msA = mm(pb[OB[i]][:, :129],
                             pbufW[rot][:, col:col + 128],
                             vv[:, 129 * skc:129 * (skc + 1)],
                             start=first_av[i], stop=last_ch,
                             waits=[lastP] + oww[i],
                             inc="pe" if (ii == len(n_sq) - 1 or last_ch) else None)
                    oww[i] = []
                    first_av[i] = False
                    if ii == len(n_sq) - 1:
                        p.use(("pbufP", rot), msA)
                    if last_ch:
                        chunk_last[i] = msA

            chunk_last = {}
            for t in range(min(LEAD, nseq)):
                emit_S(t)
            flush_tail()
            oww = [p.takewaits(("bank", OB[i])) for i in range(4)]
            for t in range(nseq):
                if feeder:
                    feeder.feed(1)
                if t + LEAD < nseq:
                    emit_S(t + LEAD)
                if feeder:
                    feeder.feed(1)
                emit_AV(t)
            tail_q.append((h, pairi, chunk_last))
        endm = p.now("pe")
        p.use(("xkv", par), endm)
        for slot in range(2):
            p.use(("qbuf", par, slot), endm)
        p.use(("qbuf_t", par), endm)
        if PH7_SERIAL:
            p.barrier()

    qstd_dma(0)
    qstd_dma(1)
    fw_prev = QstdFeed(0).finish()
    for h in range(16):
        if h + 2 < 16:
            qstd_dma(h + 2)
        feeder = QstdFeed(h + 1) if h + 1 < 16 else None
        emit_attn2(h, fw_prev, feeder)
        fw_prev = feeder.finish() if feeder else None
    flush_tail()

    p.barrier()

    if "attn2" in dbg:
        dma("sync", dbg["attn2"], cdk[:, :], inc="ds")

    if stop_phase <= 7:
        return _finish(nc, p, sem, stck)

    PHASE_MARKS.append(("PH8", sum(1 for o in p.ops if o[0] == "pe")))
    bcm8 = []
    for slot in range(2):
        dA8 = dma("gp", abc[slot, 0, :, :],
                  rowbuf.ap()[2 * slot:2 * slot + 1, :].partition_broadcast(128),
                  waits=[lr1[slot][0]] + p.takewaits(("abcA", slot)),
                  inc="dgr%d" % (2 * slot))
        dB8 = dma("gp", abc[slot, 1, :, :],
                  rowbuf.ap()[2 * slot + 1:2 * slot + 2, :].partition_broadcast(128),
                  waits=[lr1[slot][1]] + p.takewaits(("abcB", slot)),
                  inc="dgr%d" % (2 * slot + 1))
        bcm8.append((dA8, dB8))
    sm2 = proj_resid_stats(
        wp_i, lambda ec, slot: cdk[:, SC * ec + 512 * slot: SC * ec + 512 * (slot + 1)],
        None,
        lambda jc, slot: zbuf.ap()[128 * jc:128 * (jc + 1),
                                   512 * slot:512 * (slot + 1)],
        x1buf.ap(),
        bft3=[zbft[0], zbft[1], kvs[1]],
        sqt3=[zsqt[0], zsqt[1], kvs[2]],
        xfix=(abc, bcm8))
    lr2, _, _ = ln_rows(4, sm2)
    out_ms = ln_normalize(4, lr2, out_t, x1buf.ap())

    return _finish(nc, p, sem, stck)


def _finish(nc, p, sem, stck):
    p.barrier(exclude=())
    p.em("sync", lambda e: e.nop())
    p.em("gp", lambda e: e.nop())
    p.em("pe", lambda e: e.nop())
    p.em("dve", lambda v: v.nop())
    p.em("act", lambda s: s.nop())

    with nc.Block() as block:
        def runner(engname):
            def _(eng):
                seen = {}
                for (e, fn, waits, inc, amt) in p.ops:
                    if e != engname:
                        continue
                    best = {}
                    for (s, v) in waits:
                        if v > 0 and v > seen.get(s, 0) and v > best.get(s, 0):
                            best[s] = v
                    for (s, v) in best.items():
                        eng.wait_ge(sem[s], v)
                        seen[s] = v
                    instr = fn(eng)
                    if inc is not None:
                        if inc.startswith("cc"):
                            instr.then_inc(sem[inc])
                        elif inc in ("pe", "dve", "act", "gpc"):
                            instr.then_inc(sem[inc], 1)
                        else:
                            instr.then_inc(sem[inc], 16)
            return _
        block.tensor(runner("pe"))
        block.vector(runner("dve"))
        block.scalar(runner("act"))
        block.gpsimd(runner("gp"))
        block.sync(runner("sync"))
    stck.close()
    return nc


def _host_inputs(inputs):
    f = np.float32
    wq = np.asarray(inputs["L0_attn_wq"], f)
    wk = np.asarray(inputs["L0_attn_wk"], f)
    wv = np.asarray(inputs["L0_attn_wv"], f)
    wo = np.asarray(inputs["L0_attn_wo"], f)
    Wa = np.asarray(inputs["L0_ffn_Wa"], f)
    Wp = np.asarray(inputs["L0_ffn_Wp"], f)
    def _img_bias(M, brow):
        pieces = []
        for stt, w in W16:
            blk = np.ascontiguousarray(M[:, stt:stt + w])
            im = blk.reshape(16, 128, w).transpose(1, 0, 2).reshape(128, 16 * w)
            bb = np.zeros((128, 128), np.float32)
            bb[0, :] = brow[stt:stt + w]
            pieces.append(np.concatenate([im, bb], axis=1))
        return np.ascontiguousarray(np.concatenate(pieces, axis=1))

    shared = dict(
        wq_i=_img_panels(np.ascontiguousarray(wq.T), W16),
        wk_i=_img_bias(np.ascontiguousarray(wk.T), np.asarray(inputs["L0_attn_bk"], f)),
        wv_i=_img_bias(np.ascontiguousarray(wv.T), np.asarray(inputs["L0_attn_bv"], f)),
        wo_i=_img_panels(np.ascontiguousarray(wo.T), W16),
        wp_i=_img_panels(Wp, W16),
        wa_i=_img_panels(Wa, WA_RANGES),
        bq_c=np.ascontiguousarray(np.asarray(inputs["L0_attn_bq"], f).reshape(16, 128).T),
        bo_c=np.ascontiguousarray(np.asarray(inputs["L0_attn_bo"], f).reshape(16, 128).T),
        bk_r=np.asarray(inputs["L0_attn_bk"], f).reshape(1, D),
        bv_r=np.asarray(inputs["L0_attn_bv"], f).reshape(1, D),
        wsr_i=np.concatenate([np.asarray(inputs["L0_ffn_w_std"], f),
                              np.asarray(inputs["L0_ffn_w_rec"], f)]).reshape(1, 32),
        r01_i=(np.arange(128)[None, :] // 8 == np.arange(16)[:, None]).astype(f),
        ident_i=np.eye(128, dtype=BF),
    )
    x = np.asarray(inputs["L0_attn_x"], f)
    per_core = []
    for c in range(8):
        b, parity = c // 2, c % 2
        xc = np.ascontiguousarray(x[b][_own_idx(parity)])
        xi = _img_panels(np.ascontiguousarray(xc.T), [(0, SC)])
        masks = _masks_for(parity).transpose(3, 0, 1, 2, 4).reshape(128, 16 * 256)
        per_core.append(dict(shared, x_img=xi, masks_i=np.ascontiguousarray(masks)))
    return per_core


def kernel(**inputs):
    per_core = _host_inputs(inputs)
    nc = build_program()
    res = run_bass_kernel_spmd(nc, per_core, core_ids=list(range(8)))
    x = np.asarray(inputs["L0_attn_x"])
    out = np.zeros((4, 2048, D), dtype=np.float32)
    for c in range(8):
        b, parity = c // 2, c % 2
        o = np.asarray(res.results[c]["out_fm"])
        out[b, _own_idx(parity)] = np.ascontiguousarray(o.T)
    return out


# version 21
# speedup vs baseline: 1.1495x; 1.0314x over previous
import contextlib
import numpy as np
import ml_dtypes

import concourse.bass as bass
import concourse.mybir as mybir
from concourse.bass_utils import run_bass_kernel_spmd

F32 = mybir.dt.float32
BF16 = mybir.dt.bfloat16
AF = mybir.ActivationFunctionType
ALU = mybir.AluOpType
BF = ml_dtypes.bfloat16

D = 2048
H = 16
NFC = 16
SC = 1024
CH = 256
OWN = [[0, 3, 4, 7], [1, 2, 5, 6]]
GLOB = OWN[0] + OWN[1]
LIVE = [sorted(g for g in range(8)
               if GLOB[g] <= OWN[0][s] or GLOB[g] <= OWN[1][s]) for s in range(4)]
EXP_SHIFT = 20.0
SCALE2 = float(1.0 / np.sqrt(np.float32(128.0)))
QSTD0, KSTD0, V0, LOW0 = 0, 1920, 3840, 5888
W16 = [(i * 128, 128) for i in range(16)]
WA_RANGES = ([(KSTD0 + 120 * i, 120) for i in range(16)] +
             [(LOW0, 256)] +
             [(V0 + 128 * i, 128) for i in range(16)] +
             [(QSTD0 + 120 * i, 120) for i in range(16)])
WA_OFF = np.cumsum([0] + [16 * w for _, w in WA_RANGES]).tolist()
GROUPS = [[0, 1], [2, 3], [4, 5], [6, 7]]


def _own_idx(parity):
    return np.concatenate([np.arange(CH) + c * CH for c in OWN[parity]])


def _img_panels(M, ranges):
    pieces = []
    for stt, w in ranges:
        blk = np.ascontiguousarray(M[:, stt:stt + w])
        pieces.append(blk.reshape(16, 128, w).transpose(1, 0, 2).reshape(128, 16 * w))
    return np.ascontiguousarray(np.concatenate(pieces, axis=1))


def _masks_for(parity):
    m = np.zeros((4, 2, 2, 128, 256), dtype=BF)
    for s in range(4):
        for j, g in enumerate((s, s + 4)):
            kg = GLOB[g] * CH + np.arange(CH)
            qg = OWN[parity][s] * CH + np.arange(CH)
            full = (kg[:, None] <= qg[None, :]).astype(np.float32)
            m[s, j, 0] = full[:128]
            m[s, j, 1] = full[128:]
    return m


class Views:

    def __init__(self, views):
        self.views = views

    def __getitem__(self, key):
        if isinstance(key, tuple):
            v = self.views[key[0]]
            rest = key[1:]
            if not rest:
                return v
            return v[rest if len(rest) > 1 else rest[0]]
        return self.views[key]


class Plan:
    def __init__(self):
        self.ops = []
        self.cnt = {}
        self.pending = {}
        self.last = {}

    def sem(self, n):
        self.cnt.setdefault(n, 0)

    def em(self, eng, fn, waits=(), inc=None, amt=1):
        w = list(self.pending.pop(eng, ())) + [x for x in waits if x is not None]
        self.ops.append((eng, fn, w, inc, amt))
        if inc is not None:
            self.cnt[inc] = self.cnt.get(inc, 0) + (amt if amt else 1)
            return (inc, self.cnt[inc])
        return None

    def now(self, s):
        return (s, self.cnt.get(s, 0))

    def barrier(self, exclude=("cc1", "cc2", "cc3")):
        snap = [(s, c) for s, c in self.cnt.items() if c > 0 and s not in exclude]
        for e in ("pe", "dve", "act", "gp", "sync"):
            self.pending[e] = snap

    def use(self, key, *ms):
        cur = self.last.setdefault(key, {})
        for m in ms:
            if m is not None:
                s, v = m
                cur[s] = max(cur.get(s, 0), v)

    def takewaits(self, key):
        d = self.last.pop(key, {})
        return list(d.items())


PH1_LIMIT = 16
PH0_PATTERNS = True
PH1_BIAS = True
PH1_EVAC = True
PH7_SERIAL = False

PHASE_MARKS = []


def build_program(stop_phase=99, debug_taps=()):
    PHASE_MARKS.clear()
    nc = bass.Bass("TRN2", target_bir_lowering=False, debug=False)
    p = Plan()
    SEM_NAMES = ("pe", "dve", "act", "dg", "ds", "cc1", "cc2", "cc3",
                 "dgw0", "dgw1", "dgw2", "dgh0", "dgh1", "dgh2",
                 "dgi0", "dgi1", "dgi2", "dgj0", "dgj1", "dgj2", "dgb", "dkv",
                 "dgr0", "dgr1", "dgr2", "dgr3",
                 "dsx0", "dsx1", "dsx2", "dsx3", "dsz0", "dsz1", "dsz2", "dsr", "dsra",
                 "dst0", "dst1", "dsk0", "dsk1", "dckv", "dkf", "dvv",
                 "dgl", "dskt", "dsm", "dsc", "gpc")
    for n in SEM_NAMES:
        p.sem(n)
    WQ_SEMS = ("dgw%d", "dgh%d", "dgi%d", "dgj%d")

    x_img = nc.dram_tensor("x_img", [128, NFC * SC], F32, kind="ExternalInput").ap()
    wq_i = nc.dram_tensor("wq_i", [128, 8 * 4096], F32, kind="ExternalInput").ap()
    wk_i = nc.dram_tensor("wk_i", [128, 16 * 2176], F32, kind="ExternalInput").ap()
    wv_i = nc.dram_tensor("wv_i", [128, 16 * 2176], F32, kind="ExternalInput").ap()
    wo_i = nc.dram_tensor("wo_i", [128, 8 * 4096], F32, kind="ExternalInput").ap()
    wp_i = nc.dram_tensor("wp_i", [128, 8 * 4096], F32, kind="ExternalInput").ap()
    wa_i = nc.dram_tensor("wa_i", [128, WA_OFF[-1]], F32, kind="ExternalInput").ap()
    bq_c = nc.dram_tensor("bq_c", [128, 16], F32, kind="ExternalInput").ap()
    bo_c = nc.dram_tensor("bo_c", [128, 16], F32, kind="ExternalInput").ap()
    bk_r = nc.dram_tensor("bk_r", [1, D], F32, kind="ExternalInput").ap()
    bv_r = nc.dram_tensor("bv_r", [1, D], F32, kind="ExternalInput").ap()
    wsr_i = nc.dram_tensor("wsr_i", [1, 32], F32, kind="ExternalInput").ap()
    r01_i = nc.dram_tensor("r01_i", [16, 128], F32, kind="ExternalInput").ap()
    ident_i = nc.dram_tensor("ident_i", [128, 128], BF16, kind="ExternalInput").ap()
    masks_i = nc.dram_tensor("masks_i", [128, 16 * 256], BF16, kind="ExternalInput").ap()
    out_t = nc.dram_tensor("out_fm", [D, SC], F32, kind="ExternalOutput").ap()

    kv_in = nc.dram_tensor("kv_in", [128, H * 129], F32)
    kv_out = nc.dram_tensor("kv_out", [128, H * 129], F32)
    kf_in = nc.dram_tensor("kf_in", [H * 128, SC], BF16)
    kf_out = nc.dram_tensor("kf_out", [2 * H * 128, SC], BF16)
    vv_in = nc.dram_tensor("vv_in", [8 * 128, H * 129], BF16)
    vv_out = nc.dram_tensor("vv_out", [16 * 128, H * 129], BF16)
    zbuf = nc.dram_tensor("zbuf", [D, SC], F32)
    x1buf = nc.dram_tensor("x1buf", [D, SC], F32)
    rowbuf = nc.dram_tensor("rowbuf", [48, 512], F32)

    dbg = {}
    for name in debug_taps:
        shp = {"kv": [128, H * 129], "x1": [D, SC], "z": [D, SC],
               "kf": [2 * H * 128, SC], "vv": [16 * 128, H * 129],
               "attn2": [128, NFC * SC], "qf": [128, SC]}[name]
        dt = BF16 if name in ("kf", "vv", "attn2", "qf") else F32
        dbg[name] = nc.dram_tensor("dbg_" + name, shp, dt, kind="ExternalOutput").ap()

    stck = contextlib.ExitStack()
    sem = {n: stck.enter_context(nc.semaphore(n)) for n in SEM_NAMES}

    A = nc.alloc_sbuf_tensor

    def vsplit(ap, n, w):
        return Views([ap[:, i * w:(i + 1) * w] for i in range(n)])

    arena1 = A("arena1", [128, NFC * SC], BF16)
    qbuf_t = A("qbuf", [128, 2 * 2048], BF16)
    qbuf = vsplit(qbuf_t[:, :], 2, 2048)
    cdk = A("cdk", [128, H * SC], BF16)
    cdv = A("cdv", [128, 8 * H * 129], BF16)
    wbuf_t = A("wbuf", [128, 2 * 2176], BF16)
    wbuf = vsplit(wbuf_t[:, :], 2, 2176)
    xkv_t = A("xkv", [128, 2 * 4112], BF16)
    xkv = vsplit(xkv_t[:, :], 2, 4112)
    wpan = [wbuf[0], wbuf[1], xkv[0]]

    def wkey(si):
        return ("wbuf", si) if si < 2 else ("xkv", 0)
    kvbf_u = A("kvbf_u", [128, 2064], BF16)
    kvbf = kvbf_u[:, :]
    selq_fm = kvbf_u[:, 0:1024]
    selk_fm = kvbf_u[:, 1024:2048]
    kvs = Views([kvbf_u[:, i * 512:(i + 1) * 512] for i in range(4)])
    u1 = A("u1", [128, 2048], F32)
    trow = vsplit(u1[:, :], 2, 512)
    trow8 = vsplit(u1[:, :], 8, 256)
    x0t = vsplit(u1[:, :], 4, 512)
    mask_sb = u1[:, :].bitcast(BF16)
    u2 = A("u2", [128, 1536], F32)
    zt = vsplit(u2[:, :], 3, 512)
    pbufP = vsplit(u2[:, :].bitcast(BF16), 4, 256)
    u3 = A("u3", [128, 2048], F32)
    rbc = vsplit(u3[:, :], 4, 512)
    abc = Views([vsplit(u3[:, 0:1024], 2, 512), vsplit(u3[:, 1024:2048], 2, 512)])
    pbufW = vsplit(u3[:, :].bitcast(BF16), 8, 512)
    u4 = A("u4", [128, 1024], F32)
    u4b = u4[:, :].bitcast(BF16)
    zbft = vsplit(u4b, 2, 512)
    zsqt = Views([u4b[:, 1024 + i * 512:1024 + (i + 1) * 512] for i in range(2)])
    selq_tm = Views([u4b[:, i * 128:(i + 1) * 128] for i in range(8)])
    selk_tm = Views([u4b[:, 1024 + i * 128:1024 + (i + 1) * 128] for i in range(8)])
    u5 = A("u5", [128, 392], F32)
    kvst = vsplit(u5[:, 0:258], 2, 129)
    seltmp = vsplit(u5[:, 0:256], 2, 128)
    att_tm = Views([u5[:, 256 + i * 64:256 + (i + 1) * 64].bitcast(BF16) for i in range(2)])
    rec_t = Views([u5[:, 384 + i:385 + i] for i in range(2)])
    atm4 = Views([u5[:, 64 * i:64 * (i + 1)].bitcast(BF16) for i in range(4)])
    rec4 = Views([u5[:, 384 + i:385 + i] for i in range(4)])
    rows = A("rows", [1, 1024], F32)
    rows2 = A("rows2", [1, 1024], F32)
    patrow = rows2[0:1, 0:512]
    rows_bf = A("rows_bf", [1, 128], BF16)
    onesb_r = A("onesb_r", [1, 128], BF16)
    onesb_c = A("onesb_c", [128, 1], BF16)
    onesf_r = A("onesf_r", [1, 128], F32)
    eps_t = A("eps_t", [1, 1], F32)
    eps5_t = A("eps5_t", [1, 1], F32)
    nshift = A("nshift", [128, 1], F32)
    wrow = A("wrow", [1, 32], F32)
    wclip = A("wclip", [1, 32], F32)
    urow = A("urow", [1, 64], F32)
    patcol = A("patcol", [16, 4], F32)
    pat_t = A("pat", [128, 512], BF16)
    pat = vsplit(pat_t[:, :], 4, 128)
    wstd_c = A("wstd_c", [128, 16], F32)
    bqc_sb = A("bqc_sb", [128, 16], F32)
    boc_sb = A("boc_sb", [128, 16], F32)
    r01_sb = A("r01_sb", [16, 128], F32)
    ident = A("ident", [128, 128], BF16)

    pb = [nc.alloc_psum_tensor(f"pb{i}", [128, 512], F32) for i in range(8)]
    pb_bf = [b[:, :].bitcast(BF16) for b in pb]

    def mm(out_ap, lhsT, rhs, start, stop, waits=(), inc=None):
        return p.em("pe", lambda t: t.matmul(out_ap, lhsT, rhs, start=start, stop=stop),
                    waits=waits, inc=inc)

    def trans(out_ap, in_ap, waits=(), inc="pe"):
        return p.em("pe", lambda t: t.transpose(out_ap, in_ap, ident[:, :]),
                    waits=waits, inc=inc)

    def dma(q, out_ap, in_ap, waits=(), inc=None):
        return p.em(q, lambda e: e.dma_start(out=out_ap, in_=in_ap),
                    waits=waits, inc=inc, amt=16)

    def coll(kind, op, ins, outs, waits, inc):
        def fn(e):
            return e.collective_compute(kind, op, replica_groups=GROUPS, ins=ins, outs=outs)
        return p.em("gp", fn, waits=waits, inc=inc, amt=1)

    def xbf(fc, c0, w):
        return arena1[:, SC * fc + c0: SC * fc + c0 + w]

    dma("sync", ident[:, :], ident_i, inc="dsc")
    dma("sync", bqc_sb[:, :], bq_c, inc="dsc")
    dma("sync", boc_sb[:, :], bo_c, inc="dsc")
    dma("sync", wrow[:, :], wsr_i, inc="dsc")
    d_r01 = dma("sync", r01_sb[:, :], r01_i, inc="dsc")
    d_ident = d_bq = d_bo = d_wsr = d_r01 = p.now("dsc")
    d_x = dma("gp", arena1[:, :], x_img, inc="dg")

    p.em("dve", lambda v: v.memset(onesb_r[:, :], 1.0), inc="dve")
    p.em("dve", lambda v: v.memset(onesb_c[:, :], 1.0), inc="dve")
    p.em("dve", lambda v: v.memset(onesf_r[:, :], 1.0), inc="dve")
    m_eps = p.em("dve", lambda v: v.memset(eps_t[:, :], 1e-8), inc="dve")
    m_eps5 = p.em("dve", lambda v: v.memset(eps5_t[:, :], 1e-5), inc="dve")
    m_nsh = p.em("dve", lambda v: v.memset(nshift[:, :], -EXP_SHIFT), inc="dve")
    for sc in range(8):
        view = cdv[:, sc * (H * 129):(sc + 1) * (H * 129)].rearrange(
            "p (h c) -> p h c", h=H)[:, :, 128:129]
        p.em("dve", lambda v, vw=view: v.memset(vw, 1.0), inc="dve")
    mones = p.now("dve")

    mc = p.em("dve", lambda v: v.tensor_scalar_max(wclip[:, :], wrow[:, :], 1e-8),
              waits=[d_wsr], inc="dve")
    psub = p.em("dve", lambda v: v.tensor_scalar_sub(urow[0:1, 0:16], wrow[0:1, 16:32], 0.1),
                waits=[d_wsr], inc="dve")
    asg = p.em("act", lambda s: s.activation(urow[0:1, 0:16], urow[0:1, 0:16], AF.Sign,
                                             bias=0.0, scale=1.0),
               waits=[psub], inc="act")
    m_u = p.em("dve", lambda v: v.tensor_scalar_max(urow[0:1, 0:16], urow[0:1, 0:16], 0.0),
               waits=[asg], inc="dve")
    m_a = p.em("dve", lambda v: v.tensor_mul(urow[0:1, 16:32], urow[0:1, 0:16],
                                             wclip[0:1, 16:32]),
               waits=[m_u, mc], inc="dve")
    m_b = p.em("dve", lambda v: v.tensor_sub(urow[0:1, 32:48], wclip[0:1, 16:32],
                                             urow[0:1, 16:32]),
               waits=[m_a], inc="dve")
    m_n = p.em("dve", lambda v: v.tensor_scalar(urow[0:1, 48:64], urow[0:1, 0:16],
                                                -1.0, 1.0, ALU.mult, ALU.add),
               waits=[m_u], inc="dve")
    mrows = p.now("dve")

    srcrow = {0: 16, 1: 32, 2: 0, 3: 48}
    for i in range(4 if PH0_PATTERNS else 0):
        r_ = srcrow[i]
        m1 = mm(pb[0][0:16, 0:1], urow[0:1, r_:r_ + 16], onesf_r[0:1, 0:1], True, True,
                waits=[mrows, mones] + p.takewaits(("bank", 0)), inc="pe")
        c1 = p.em("dve", lambda v, i=i: v.tensor_copy(patcol[:, i:i + 1], pb[0][0:16, 0:1]),
                  waits=[m1], inc="dve")
        p.use(("bank", 0), c1)
        m2 = mm(pb[1][0:1, 0:128], patcol[:, i:i + 1], r01_sb[:, :], True, True,
                waits=[c1, d_r01] + p.takewaits(("bank", 1)), inc="pe")
        c2 = p.em("dve", lambda v, i=i: v.tensor_copy(patrow[0:1, i * 128:(i + 1) * 128],
                                                      pb[1][0:1, 0:128]),
                  waits=[m2], inc="dve")
        p.use(("bank", 1), c2)
        m3 = mm(pb[2][:, 0:128], onesf_r[0:1, :], patrow[0:1, i * 128:(i + 1) * 128],
                True, True, waits=[c2] + p.takewaits(("bank", 2)), inc="pe")
        c3 = p.em("dve", lambda v, i=i: v.tensor_copy(pat[i], pb[2][:, 0:128]),
                  waits=[m3], inc="dve")
        p.use(("bank", 2), c3)
    if PH0_PATTERNS:
        m4 = mm(pb[3][:, 0:16], onesf_r[0:1, :], wclip[0:1, 0:16], True, True,
                waits=[mc, mones] + p.takewaits(("bank", 3)), inc="pe")
        c4 = p.em("dve", lambda v: v.tensor_copy(wstd_c[:, :], pb[3][:, 0:16]),
                  waits=[m4], inc="dve")
        p.use(("bank", 3), c4)
    mpat = p.now("dve")

    p.barrier()

    if stop_phase <= 0:
        return _finish(nc, p, sem, stck)

    PHASE_MARKS.append(("PH1", sum(1 for o in p.ops if o[0] == "pe")))
    def tm_proj_b1(img_ap, bias_row, is_k):
        dws = [dma("gp", wbuf[0, :, :], img_ap[:, 0:2176],
                   waits=p.takewaits(("wbuf", 0)), inc="dgw0"),
               dma("gp", wbuf[1, :, :], img_ap[:, 2176:4352],
                   waits=p.takewaits(("wbuf", 1)), inc="dgw1")]
        for hidx in range(PH1_LIMIT):
            par = hidx % 2
            for wave in range(2):
                mss = []
                for i in range(4):
                    sc = wave * 4 + i
                    bidx = 4 * wave + i
                    bank = pb[bidx]
                    bw = [dws[hidx], d_x] + p.takewaits(("bank", bidx))
                    for fc in range(NFC):
                        mm(bank[:, :128], xbf(fc, sc * 128, 128),
                           wbuf[par, :, 128 * fc:128 * (fc + 1)],
                           start=(fc == 0), stop=False,
                           waits=bw if fc == 0 else [])
                        bw = []
                    ms = mm(bank[:, :128], onesb_r[0:1, :],
                            wbuf[par, 0:1, 2048:2176],
                            start=False, stop=True,
                            waits=[mones], inc="pe")
                    mss.append(ms)
                for i in range(4 if PH1_EVAC else 0):
                    sc = wave * 4 + i
                    bidx = 4 * wave + i
                    bank = pb[bidx]
                    if is_k:
                        pr = i % 4
                        t_t = trow8[2 * pr][:, :128]
                        t_r = trow8[2 * pr + 1][:, :128]
                        a1 = p.em("act", lambda s, b=bank, t=t_t: s.activation(
                            t, b[:, :128], AF.Exp, bias=0.0, scale=1.0),
                            waits=[mss[i]] + p.takewaits(("trow8", pr)), inc="act")
                        a2 = p.em("dve", lambda v, b=bank, t=t_r: v.tensor_scalar_max(
                            t, b[:, :128], 0.0),
                            waits=[mss[i], a1], inc="dve")
                        dst = cdk[:, 2048 * sc + 128 * hidx: 2048 * sc + 128 * (hidx + 1)]
                        a3 = p.em("dve", lambda v, d=dst, t1=t_t, t2=t_r:
                                  v.scalar_tensor_tensor(d, t1, 1.0, t2, ALU.min, ALU.add),
                            waits=[a1, a2], inc="dve")
                        p.use(("trow8", pr), a3)
                        p.use(("bank", bidx), a1, a2)
                        p.use(("cdk_w",), a3)
                    else:
                        dstv = cdv[:, sc * (H * 129) + 129 * hidx:
                                   sc * (H * 129) + 129 * hidx + 128]
                        a1 = p.em("act", lambda s, d=dstv, b=bank: s.copy(d, b[:, :128]),
                                  waits=[mss[i]], inc="act")
                        p.use(("bank", bidx), a1)
                        p.use(("cdv_w",), a1)
            p.use(("wbuf", par), p.now("pe"))
            if hidx + 2 < 16:
                par2 = (hidx + 2) % 2
                dws.append(dma("gp", wbuf[par2, :, :],
                               img_ap[:, (hidx + 2) * 2176:(hidx + 3) * 2176],
                               waits=p.takewaits(("wbuf", par2)),
                               inc="dgw%d" % par2))

    tm_proj_b1(wk_i, bk_r, True)
    if stop_phase <= 1:
        return _finish(nc, p, sem, stck)
    PHASE_MARKS.append(("PH2", sum(1 for o in p.ops if o[0] == "pe")))
    tm_proj_b1(wv_i, bv_r, False)
    if stop_phase <= 2:
        return _finish(nc, p, sem, stck)

    PHASE_MARKS.append(("PH3", sum(1 for o in p.ops if o[0] == "pe")))
    kq = p.takewaits(("cdk_w",)) + p.takewaits(("cdv_w",))
    kv_ev = []
    for h in range(H):
        bidx = h % 2
        bank = pb[bidx]
        bw = kq + p.takewaits(("bank", bidx))
        kq = []
        ms = None
        for sc in range(8):
            ms = mm(bank[:, :129],
                    cdk[:, 2048 * sc + 128 * h: 2048 * sc + 128 * (h + 1)],
                    cdv[:, sc * H * 129 + 129 * h: sc * H * 129 + 129 * (h + 1)],
                    start=(sc == 0), stop=(sc == 7),
                    waits=bw if sc == 0 else [], inc="pe" if sc == 7 else None)
            bw = []
        ev = p.em("act", lambda s, b=bank, h=h: s.copy(kvst[h % 2], b[:, :129]),
                  waits=[ms] + p.takewaits(("kvst", h % 2)), inc="act")
        p.use(("bank", bidx), ev)
        d_h = dma("sync", kv_in.ap()[:, 129 * h:129 * (h + 1)], kvst[h % 2],
                  waits=[ev], inc="dckv")
        p.use(("kvst", h % 2), d_h)
        kv_ev.append(d_h)
    coll("AllReduce", ALU.add, [kv_in.ap()], [kv_out.ap()], waits=kv_ev, inc="cc1")

    if "kv" in dbg:
        dma("sync", dbg["kv"], kv_out.ap(), waits=[("cc1", 1)], inc="ds")

    if stop_phase <= 3:
        return _finish(nc, p, sem, stck)

    PHASE_MARKS.append(("PH4", sum(1 for o in p.ops if o[0] == "pe")))
    nrow = 8

    def emit_qproj(h):
        par = h % 2
        wsi = h % 3
        dw = dwq[h]
        ev_q = {}
        mss = {}
        for slot in range(2):
            bidx = (2 * h + slot) % 4
            bank = pb[bidx]
            bw = [dw[0], d_x] + p.takewaits(("bank", bidx))
            ms = None
            for fc in range(NFC):
                if fc in (4, 8, 12):
                    bw = bw + [dw[fc // 4]]
                ms = mm(bank[:, :512],
                        wpan[wsi][:, 128 * fc:128 * (fc + 1)],
                        xbf(fc, slot * 512, 512),
                        start=(fc == 0), stop=(fc == NFC - 1),
                        waits=bw,
                        inc="pe" if fc == NFC - 1 else None)
                bw = []
            mss[slot] = ms
        p.use(wkey(wsi), p.now("pe"))
        for slot in range(2):
            bidx = (2 * h + slot) % 4
            bank = pb[bidx]
            a1 = p.em("act", lambda s, b=bank, h=h: s.activation(
                trow[0][:, :512], b[:, :512], AF.Exp,
                bias=bqc_sb[:, h:h + 1], scale=1.0),
                waits=[mss[slot], d_bq] + p.takewaits(("trow", 0)), inc="act")
            a2 = p.em("dve", lambda v, b=bank, h=h: v.tensor_scalar(
                trow[1][:, :512], b[:, :512], bqc_sb[:, h:h + 1], 0.0,
                ALU.add, ALU.max),
                waits=[mss[slot], d_bq, a1], inc="dve")
            dstq = qbuf[par, :, 512 * slot:512 * (slot + 1)]
            a3 = p.em("dve", lambda v, d=dstq: v.scalar_tensor_tensor(
                d, trow[0][:, :512], 1.0, trow[1][:, :512], ALU.min, ALU.add),
                waits=[a1, a2] + p.takewaits(("qbuf", par, slot)), inc="dve")
            p.use(("trow", 0), a3)
            p.use(("trow", 1), a3)
            p.use(("bank", bidx), a1, a2)
            ev_q[slot] = a3
        return ev_q

    R4 = [rows[0:1, 0:512], rows[0:1, 512:1024],
          rows2[0:1, 0:512], rows2[0:1, 512:1024]]
    pend_n = []

    def attn1_stage1(h, ev_q):
        par = h % 2
        for slot in range(2):
            i2 = 2 * h + slot
            ridx = i2 % 4
            qsl = qbuf[par, :, 512 * slot:512 * (slot + 1)]
            b_u = pb[4 + slot]
            msu = mm(b_u[:, :512], kvbf[:, 129 * h:129 * h + 128], qsl, True, True,
                     waits=[ev_q[slot], d_kvbf] + p.takewaits(("bank", 4 + slot)),
                     inc="pe")
            msd = mm(pb[6][0:1, :512], kvbf[:, 129 * h + 128:129 * h + 129], qsl,
                     True, True, waits=p.takewaits(("bank", 6)), inc="pe")
            p.use(("qbuf", par, slot), msu, msd)
            a_d = p.em("act", lambda s, r=R4[ridx]: s.activation(
                r, pb[6][0:1, :512], AF.Identity,
                bias=eps_t[0:1, 0:1], scale=1.0),
                waits=[msd, m_eps] + p.takewaits(("rowsA", ridx)), inc="act")
            p.use(("bank", 6), a_d)
            r_d = p.em("dve", lambda v, r=R4[ridx]: v.reciprocal(r, r),
                       waits=[a_d], inc="dve")
            dsta = cdk[:, 1024 * h + 512 * slot: 1024 * h + 512 * (slot + 1)]
            a_u = p.em("act", lambda s, d=dsta, b=b_u: s.copy(d, b[:, :512]),
                       waits=[msu], inc="act")
            p.use(("bank", 4 + slot), a_u)
            pend_n.append((ridx, r_d, a_u, dsta))

    def attn1_stage2():
        while pend_n:
            ridx, r_d, a_u, dsta = pend_n.pop(0)
            bb = 7 - (ridx % 2)
            msb = mm(pb[bb][:, :512], onesf_r[0:1, :], R4[ridx], True, True,
                     waits=[r_d] + p.takewaits(("bank", bb)), inc="pe")
            t_m = p.em("dve", lambda v, d=dsta, bb=bb: v.tensor_mul(d, d, pb[bb][:, :512]),
                       waits=[a_u, msb], inc="dve")
            p.use(("bank", bb), t_m)
            p.use(("rowsA", ridx), msb)
            p.use(("attn1",), t_m)

    dwq = {}

    def qproj_dma(h):
        wsi = h % 3
        ww = p.takewaits(wkey(wsi))
        ms4 = []
        for q in range(4):
            ms4.append(dma("gp", wpan[wsi][:, 512 * q:512 * (q + 1)],
                           wq_i[:, h * 2048 + 512 * q:h * 2048 + 512 * (q + 1)],
                           waits=ww, inc=WQ_SEMS[q] % wsi))
            ww = []
        dwq[h] = ms4

    qproj_dma(0)
    qproj_dma(1)
    qproj_dma(2)
    d_kvbf = dma("gp", kvbf[:, :], kv_out.ap(), waits=[("cc1", 1)], inc="dkv")
    ev_prev = emit_qproj(0)
    for h in range(16):
        if h + 3 < 16:
            qproj_dma(h + 3)
        attn1_stage2()
        attn1_stage1(h, ev_prev)
        if h + 1 < 16:
            ev_prev = emit_qproj(h + 1)
    attn1_stage2()

    p.barrier()

    PHASE_MARKS.append(("PH5", sum(1 for o in p.ops if o[0] == "pe")))
    def proj_resid_stats(img_ap, rhs_fn, bias_col, xres_fn, zdst_ap,
                         bf_fn=None, bft3=None, sqt3=None, xfix=None):
        pend = []
        stat_started = [False, False]
        stat_sl = [[], []]

        def flush_stats(keep=0):
            while len(pend) > keep:
                jc, slot, zb_ms, zq_ms, zrot, s1rhs, trk = pend.pop(0)
                b_s, b_q = pb[4 + slot], pb[6 + slot]
                first = not stat_started[slot]
                w1 = [zb_ms] + (p.takewaits(("bank", 4 + slot)) if first else [])
                w2 = [zq_ms] + (p.takewaits(("bank", 6 + slot)) if first else [])
                s1 = mm(b_s[0:1, :512], onesb_c[:, 0:1], s1rhs,
                        start=first, stop=(jc == NFC - 1), waits=w1, inc="pe")
                s2 = mm(b_q[0:1, :512], onesb_c[:, 0:1], sqt3[zrot],
                        start=first, stop=(jc == NFC - 1), waits=w2, inc="pe")
                stat_started[slot] = True
                if trk:
                    p.use(("zbft8", zrot), s1)
                p.use(("zsqt8", zrot), s2)
                if jc == NFC - 1:
                    stat_sl[slot] = [s1, s2]

        def w_dma(jc):
            si2 = jc % 3
            ww = p.takewaits(wkey(si2))
            ms4 = []
            for q in range(4):
                ms4.append(dma("gp", wpan[si2][:, 512 * q:512 * (q + 1)],
                               img_ap[:, jc * 2048 + 512 * q:
                                      jc * 2048 + 512 * (q + 1)],
                               waits=ww, inc=WQ_SEMS[q] % si2))
                ww = []
            return ms4

        dws = [w_dma(0), w_dma(1), w_dma(2)]
        for jc in range(16):
            par = jc % 3
            for slot in range(2):
                idx = 2 * jc + slot
                bidx = slot
                bank = pb[bidx]
                xrot = idx % 4
                d_x0 = dma("sync", x0t[xrot, :, :], xres_fn(jc, slot),
                           waits=p.takewaits(("x0t", xrot)), inc="dsx%d" % xrot)
                if xfix is not None:
                    axc, bcm8 = xfix
                    c1 = p.em("dve", lambda v, xr=xrot, s=slot: v.tensor_mul(
                        x0t[xr, :, :], x0t[xr, :, :], axc[s, 0, :, :]),
                        waits=[d_x0, bcm8[slot][0]], inc="dve")
                    c2 = p.em("dve", lambda v, xr=xrot, s=slot: v.tensor_add(
                        x0t[xr, :, :], x0t[xr, :, :], axc[s, 1, :, :]),
                        waits=[c1, bcm8[slot][1]], inc="dve")
                    p.use(("abcA", slot), c1)
                    p.use(("abcB", slot), c2)
                    xr_ms = c2
                else:
                    xr_ms = d_x0
                bw = [dws[jc][0]] + p.takewaits(("bank", bidx))
                ms = None
                for ec in range(NFC):
                    if ec in (4, 8, 12):
                        bw = bw + [dws[jc][ec // 4]]
                    ms = mm(bank[:, :512],
                            wpan[par][:, 128 * ec:128 * (ec + 1)],
                            rhs_fn(ec, slot),
                            start=(ec == 0), stop=(ec == NFC - 1),
                            waits=bw,
                            inc="pe" if ec == NFC - 1 else None)
                    bw = []
                zr = idx % 3
                if bias_col is not None:
                    ev = p.em("dve", lambda v, b=bank, jc=jc, zr=zr, xr=xrot:
                              v.scalar_tensor_tensor(zt[zr][:, :512], b[:, :512],
                                                     bias_col[:, jc:jc + 1],
                                                     x0t[xr, :, :], ALU.add, ALU.add),
                              waits=[ms, xr_ms, d_bo] + p.takewaits(("zt", zr)),
                              inc="dve")
                else:
                    ev = p.em("dve", lambda v, b=bank, zr=zr, xr=xrot:
                              v.scalar_tensor_tensor(zt[zr][:, :512], b[:, :512],
                                                     1.0, x0t[xr, :, :],
                                                     ALU.mult, ALU.add),
                              waits=[ms, xr_ms] + p.takewaits(("zt", zr)), inc="dve")
                p.use(("x0t", xrot), ev)
                p.use(("bank", bidx), ev)
                d_z = dma("sync", zdst_ap[128 * jc:128 * (jc + 1),
                                          512 * slot:512 * (slot + 1)],
                          zt[zr][:, :512], waits=[ev], inc="dsz%d" % zr)
                brot = idx % 3
                if bf_fn is not None:
                    bfdst = bf_fn(jc, slot)
                    zb = p.em("dve", lambda v, zr=zr, d=bfdst:
                              v.tensor_copy(d, zt[zr][:, :512]),
                              waits=[ev], inc="dve")
                    s1rhs, trk = bfdst, False
                else:
                    zb = p.em("dve", lambda v, zr=zr, br=brot:
                              v.tensor_copy(bft3[br], zt[zr][:, :512]),
                              waits=[ev] + p.takewaits(("zbft8", brot)), inc="dve")
                    s1rhs, trk = bft3[brot], True
                zq = p.em("act", lambda s, zr=zr, br=brot:
                          s.activation(sqt3[br], zt[zr][:, :512],
                                       AF.Square, bias=0.0, scale=1.0),
                          waits=[zb] + p.takewaits(("zsqt8", brot)), inc="act")
                p.use(("zt", zr), d_z, zq)
                pend.append((jc, slot, zb, zq, brot, s1rhs, trk))
                flush_stats(keep=2)
            p.use(wkey(par), p.now("pe"))
            if jc + 3 < 16:
                dws.append(w_dma(jc + 3))
        flush_stats()
        return stat_sl

    def ln_rows(base, stat_sl, mkbf=False):
        outm = []
        en_bf = []
        ea_bf = []
        for slot in range(2):
            b_s, b_q = pb[4 + slot], pb[6 + slot]
            rw = p.takewaits(("rowsA",)) + p.takewaits(("rowsB",))
            a1 = p.em("act", lambda s, b=b_s: s.activation(
                rows[0:1, 0:512], b[0:1, :512], AF.Identity, bias=0.0,
                scale=-1.0 / 2048.0), waits=stat_sl[slot] + rw, inc="act")
            a2 = p.em("act", lambda s, b=b_q: s.activation(
                rows[0:1, 512:1024], b[0:1, :512], AF.Identity, bias=0.0,
                scale=1.0 / 2048.0), inc="act")
            p.use(("bank", 4 + slot), a1)
            p.use(("bank", 6 + slot), a2)
            if mkbf:
                mn = mm(pb[4 + slot][:, :512], onesf_r[0:1, :], rows[0:1, 0:512],
                        True, True, waits=[a1] + p.takewaits(("bank", 4 + slot)),
                        inc="pe")
                p.use(("rowsA",), mn)
                en = p.em("act", lambda s, sl=slot: s.copy(
                    zbft[sl][:, :512], pb[4 + sl][:, :512]),
                    waits=[mn] + p.takewaits(("zbft8", slot)), inc="act")
                p.use(("bank", 4 + slot), en)
                en_bf.append(en)
            d1 = p.em("dve", lambda v: v.tensor_mul(rows2[0:1, 0:512],
                                                    rows[0:1, 0:512], rows[0:1, 0:512]),
                      waits=[a1] + p.takewaits(("rows2",)), inc="dve")
            d2 = p.em("dve", lambda v: v.tensor_sub(rows[0:1, 512:1024],
                                                    rows[0:1, 512:1024], rows2[0:1, 0:512]),
                      waits=[a2, d1], inc="dve")
            a3 = p.em("act", lambda s: s.activation(rows[0:1, 512:1024],
                                                    rows[0:1, 512:1024], AF.Sqrt,
                                                    bias=eps5_t[0:1, 0:1], scale=1.0),
                      waits=[d2, m_eps5], inc="act")
            d3 = p.em("dve", lambda v: v.reciprocal(rows2[0:1, 0:512],
                                                    rows[0:1, 512:1024]),
                      waits=[a3], inc="dve")
            dA = dma("sync", rowbuf.ap()[base + 2 * slot:base + 2 * slot + 1, :],
                     rows2[0:1, 0:512], waits=[d3], inc="dsr")
            if mkbf:
                ma = mm(pb[6 + slot][:, :512], onesf_r[0:1, :], rows2[0:1, 0:512],
                        True, True, waits=[d3] + p.takewaits(("bank", 6 + slot)),
                        inc="pe")
                p.use(("rows2",), ma)
                ea = p.em("act", lambda s, sl=slot: s.copy(
                    zsqt[sl][:, :512], pb[6 + sl][:, :512]),
                    waits=[ma] + p.takewaits(("zsqt8", slot)), inc="act")
                p.use(("bank", 6 + slot), ea)
                ea_bf.append(ea)
            d4 = p.em("dve", lambda v: v.scalar_tensor_tensor(
                rows2[0:1, 512:1024], rows[0:1, 0:512], 1.0, rows2[0:1, 0:512],
                ALU.mult, ALU.mult), waits=[d3], inc="dve")
            dB = dma("sync", rowbuf.ap()[base + 2 * slot + 1:base + 2 * slot + 2, :],
                     rows2[0:1, 512:1024], waits=[d4], inc="dsra")
            p.use(("rowsA",), d1, d4)
            p.use(("rowsB",), a3)
            p.use(("rows2",), dA, dB, d4)
            outm.append((dA, dB))
        return outm, en_bf, ea_bf

    def ln_normalize(base, rowdma, xout_ap, zsrc_ap):
        bms = {}
        for slot in range(2):
            dA = dma("gp", abc[slot, 0, :, :],
                     rowbuf.ap()[base + 2 * slot:base + 2 * slot + 1, :]
                     .partition_broadcast(128),
                     waits=[rowdma[slot][0]] + p.takewaits(("abcA", slot)),
                     inc="dgr%d" % (2 * slot))
            dB = dma("gp", abc[slot, 1, :, :],
                     rowbuf.ap()[base + 2 * slot + 1:base + 2 * slot + 2, :]
                     .partition_broadcast(128),
                     waits=[rowdma[slot][1]] + p.takewaits(("abcB", slot)),
                     inc="dgr%d" % (2 * slot + 1))
            bms[slot] = (dA, dB)
        zts = [zt[0], zt[1], zt[2], x0t[0], x0t[1], x0t[2], x0t[3]]

        def zkey(zr):
            return ("zt", zr) if zr < 3 else ("x0t", zr - 3)

        def zsem(zr):
            return "dsz%d" % zr if zr < 3 else "dsx%d" % (zr - 3)

        lastms = []
        for jc in range(NFC):
            for slot in range(2):
                idx = 2 * jc + slot
                zr = idx % 7
                d_z = dma("sync", zts[zr][:, :512],
                          zsrc_ap[128 * jc:128 * (jc + 1), 512 * slot:512 * (slot + 1)],
                          waits=p.takewaits(zkey(zr)), inc=zsem(zr))
                eng = "dve" if (idx % 2 == 0) else "gp"
                esem = "dve" if eng == "dve" else "gpc"
                t1 = p.em(eng, lambda v, zr=zr, s=slot: v.tensor_mul(
                    zts[zr][:, :512], zts[zr][:, :512], abc[s, 0, :, :]),
                    waits=[d_z, bms[slot][0]], inc=esem)
                t2 = p.em(eng, lambda v, zr=zr, s=slot: v.tensor_add(
                    zts[zr][:, :512], zts[zr][:, :512], abc[s, 1, :, :]),
                    waits=[t1, bms[slot][1]], inc=esem)
                p.use(("abcA", slot), t1)
                p.use(("abcB", slot), t2)
                d_o = dma("act", xout_ap[128 * jc:128 * (jc + 1),
                                         512 * slot:512 * (slot + 1)],
                          zts[zr][:, :512], waits=[t2], inc="dsc")
                p.use(zkey(zr), d_o)
                lastms.append(d_o)
        return lastms

    def wa_dma(pi, si):
        width = WA_RANGES[pi][1]
        ww = p.takewaits(wkey(si))
        ms4 = []
        for q in range(4):
            ms4.append(dma("gp", wpan[si][:, 4 * q * width:4 * (q + 1) * width],
                           wa_i[:, WA_OFF[pi] + 4 * q * width:
                                WA_OFF[pi] + 4 * (q + 1) * width],
                           waits=ww, inc=WQ_SEMS[q] % si))
            ww = []
        return ms4

    sm1 = proj_resid_stats(
        wo_i, lambda ec, slot: cdk[:, 1024 * ec + 512 * slot: 1024 * ec + 512 * (slot + 1)],
        boc_sb,
        lambda jc, slot: x_img[:, 1024 * jc + 512 * slot: 1024 * jc + 512 * (slot + 1)],
        zbuf.ap(),
        bf_fn=lambda jc, slot: arena1[:, SC * jc + 512 * slot:
                                      SC * jc + 512 * (slot + 1)],
        sqt3=[zsqt[0], zsqt[1], kvs[0]])
    p.use(("cdk_r",), p.now("pe"))
    dws6 = {}

    def kstd_dma(h):
        dws6[h] = wa_dma(h, h % 3)

    kstd_dma(0)
    kstd_dma(1)
    kstd_dma(2)
    lr1, en_bf, ea_bf = ln_rows(0, sm1, mkbf=True)
    x1bf_ms = {}
    cc_slot = {0: [], 1: []}
    for jc in range(NFC):
        for slot in range(2):
            idx = 2 * jc + slot
            eng = "dve" if idx % 2 == 0 else "gp"
            esem = "dve" if eng == "dve" else "gpc"
            tile = arena1[:, SC * jc + 512 * slot: SC * jc + 512 * (slot + 1)]
            c1 = p.em(eng, lambda v, t=tile, s=slot: v.tensor_add(t, t, zbft[s]),
                      waits=[en_bf[slot]], inc=esem)
            c2 = p.em(eng, lambda v, t=tile, s=slot: v.tensor_mul(t, t, zsqt[s]),
                      waits=[c1, ea_bf[slot]], inc=esem)
            x1bf_ms.setdefault(jc, []).append(c2)
            cc_slot[slot].append(c2)
    for slot in range(2):
        p.use(("zbft8", slot), *cc_slot[slot])
        p.use(("zsqt8", slot), *cc_slot[slot])

    if stop_phase <= 5:
        return _finish(nc, p, sem, stck)

    PHASE_MARKS.append(("PH6k", sum(1 for o in p.ops if o[0] == "pe")))
    for h in range(16):
        par = h % 3
        mss = {}
        for slot in range(2):
            bank = pb[slot]
            bw = [dws6[h][0]] + p.takewaits(("bank", slot))
            ms = None
            for fc in range(NFC):
                if fc in (4, 8, 12):
                    bw = bw + [dws6[h][fc // 4]]
                if h == 0:
                    bw = bw + x1bf_ms.get(fc, [])
                ms = mm(bank[0:120, :512],
                        wpan[par][:, 120 * fc:120 * (fc + 1)],
                        xbf(fc, slot * 512, 512),
                        start=(fc == 0), stop=(fc == NFC - 1),
                        waits=bw,
                        inc="pe" if fc == NFC - 1 else None)
                bw = []
            mss[slot] = ms
        p.use(wkey(par), p.now("pe"))
        if h + 3 < 16:
            kstd_dma(h + 3)
        for slot in range(2):
            bank = pb[slot]
            dst = cdk[0:120, 1024 * h + 512 * slot: 1024 * h + 512 * (slot + 1)]
            ev = p.em("act", lambda s, d=dst, b=bank: s.copy(d, b[0:120, :512]),
                      waits=[mss[slot]] + p.takewaits(("cdk_r",)), inc="act")
            p.use(("bank", slot), ev)
            p.use(("kstd",), ev)

    PHASE_MARKS.append(("PH6l", sum(1 for o in p.ops if o[0] == "pe")))
    lowbuf = mask_sb
    dwl_w = (p.takewaits(("x0t", 0)) + p.takewaits(("x0t", 1)) +
             p.takewaits(("x0t", 2)) + p.takewaits(("x0t", 3)) +
             p.takewaits(("trow", 0)) + p.takewaits(("trow", 1)))
    dwl = dma("gp", lowbuf[:, 0:4096], wa_i[:, WA_OFF[16]:WA_OFF[17]],
              waits=dwl_w, inc="dgl")
    sel_ev = []
    for wave in range(2):
        mss = []
        for i in range(4):
            sc = 4 * wave + i
            bidx = 4 + i
            bank = pb[bidx]
            bw = [dwl] + p.takewaits(("bank", bidx))
            ms = None
            for fc in range(NFC):
                ms = mm(bank[:, :256], xbf(fc, sc * 128, 128),
                        lowbuf[:, 256 * fc:256 * (fc + 1)],
                        start=(fc == 0), stop=(fc == NFC - 1),
                        waits=bw if fc == 0 else [],
                        inc="pe" if fc == NFC - 1 else None)
                bw = []
            mss.append(ms)
        for i in range(4):
            sc = 4 * wave + i
            bidx = 4 + i
            bank = pb[bidx]
            e1 = p.em("dve", lambda v, b=bank: v.tensor_mul(seltmp[0], b[:, 128:256], pat[0]),
                      waits=[mss[i], mpat] + p.takewaits(("seltmp",)) +
                      p.takewaits(("zbft8", 0)) + p.takewaits(("zbft8", 1)) +
                      p.takewaits(("zsqt8", 0)) + p.takewaits(("zsqt8", 1)), inc="dve")
            e2 = p.em("dve", lambda v, b=bank: v.tensor_mul(seltmp[1], b[:, 0:128], pat[1]),
                      waits=[mss[i]], inc="dve")
            e3 = p.em("dve", lambda v, sc=sc: v.tensor_add(selq_tm[sc], seltmp[0], seltmp[1]),
                      waits=[e1, e2] + p.takewaits(("u4q", sc)), inc="dve")
            e4 = p.em("dve", lambda v, b=bank: v.tensor_mul(seltmp[0], b[:, 0:128], pat[2]),
                      waits=[e3], inc="dve")
            e5 = p.em("dve", lambda v, b=bank: v.tensor_mul(seltmp[1], b[:, 128:256], pat[3]),
                      waits=[e3], inc="dve")
            e6 = p.em("dve", lambda v, sc=sc: v.tensor_add(selk_tm[sc], seltmp[0], seltmp[1]),
                      waits=[e4, e5] + p.takewaits(("u4k", sc)), inc="dve")
            p.use(("seltmp",), e6)
            p.use(("bank", bidx), e4, e5)
            sel_ev.append((e3, e6))
    for sc in range(8):
        t1 = trans(pb_bf[7][:, 0:128], selq_tm[sc],
                   waits=[sel_ev[sc][0], d_ident] + p.takewaits(("bank", 7)))
        c1 = p.em("act", lambda s, sc=sc: s.copy(selq_fm[:, 128 * sc:128 * (sc + 1)],
                                                 pb_bf[7][:, 0:128]),
                  waits=[t1], inc="act")
        p.use(("bank", 7), c1)
        t2 = trans(pb_bf[7][:, 0:128], selk_tm[sc],
                   waits=[sel_ev[sc][1]] + p.takewaits(("bank", 7)))
        c2 = p.em("act", lambda s, sc=sc: s.copy(selk_fm[:, 128 * sc:128 * (sc + 1)],
                                                 pb_bf[7][:, 0:128]),
                  waits=[t2], inc="act")
        p.use(("bank", 7), c2)
        p.use(("selq_fm",), c1)
        p.use(("selk_fm",), c2)

    low_last_pe = p.now("pe")

    wkt = p.takewaits(("selk_fm",)) + p.takewaits(("kstd",))
    ktail = []
    for h in range(H):
        d_t = dma("sync", cdk[120:128, 1024 * h:1024 * (h + 1)],
                  selk_fm[8 * h:8 * (h + 1), :],
                  waits=wkt if h == 0 else [], inc="dskt")
        wkt = []
        ktail.append(d_t)
    d_kf = dma("sync", kf_in.ap().rearrange("(h p) s -> p h s", p=128),
               cdk[:, :].rearrange("p (h s) -> p h s", h=H),
               waits=ktail, inc="dkf")
    coll("AllGather", ALU.bypass, [kf_in.ap()], [kf_out.ap()], waits=[d_kf], inc="cc2")

    PHASE_MARKS.append(("PH6v", sum(1 for o in p.ops if o[0] == "pe")))
    dws7 = {}

    def v_dma(h):
        dws7[h] = wa_dma(17 + h, h % 3)

    v_dma(0)
    v_dma(1)
    v_dma(2)
    for h in range(16):
        par = h % 3
        for wave in range(2):
            mss = []
            for i in range(4):
                sc = 4 * wave + i
                bank = pb[i]
                bw = [dws7[h][0]] + p.takewaits(("bank", i))
                ms = None
                for fc in range(NFC):
                    if fc in (4, 8, 12):
                        bw = bw + [dws7[h][fc // 4]]
                    ms = mm(bank[:, :128], xbf(fc, sc * 128, 128),
                            wpan[par][:, 128 * fc:128 * (fc + 1)],
                            start=(fc == 0), stop=(fc == NFC - 1),
                            waits=bw,
                            inc="pe" if fc == NFC - 1 else None)
                    bw = []
                mss.append(ms)
            for i in range(4):
                sc = 4 * wave + i
                bank = pb[i]
                dstv = cdv[:, sc * (H * 129) + 129 * h: sc * (H * 129) + 129 * h + 128]
                ev = p.em("act", lambda s, d=dstv, b=bank: s.copy(d, b[:, :128]),
                          waits=[mss[i]], inc="act")
                p.use(("bank", i), ev)
                p.use(("cdv2",), ev)
        p.use(wkey(par), p.now("pe"))
        if h + 3 < 16:
            v_dma(h + 3)
    d_vv = dma("sync", vv_in.ap().rearrange("(c p) s -> p c s", p=128),
               cdv[:, :].rearrange("p (c s) -> p c s", c=8),
               waits=p.takewaits(("cdv2",)), inc="dvv")
    coll("AllGather", ALU.bypass, [vv_in.ap()], [vv_out.ap()], waits=[d_vv], inc="cc3")

    d_mask = dma("sync", mask_sb[:, 0:4096], masks_i,
                 waits=[low_last_pe], inc="dsm")

    if "kf" in dbg:
        dma("sync", dbg["kf"], kf_out.ap(), waits=[("cc2", 1)], inc="ds")
    if "vv" in dbg:
        dma("sync", dbg["vv"], vv_out.ap(), waits=[("cc3", 1)], inc="ds")

    if stop_phase <= 6:
        return _finish(nc, p, sem, stck)

    PHASE_MARKS.append(("PH7", sum(1 for o in p.ops if o[0] == "pe")))
    dwq7 = {}

    def qstd_dma(h):
        dwq7[h] = wa_dma(33 + h, h % 2)

    class QstdFeed:

        def __init__(self, h):
            self.h = h
            self.par = h % 2
            self.dw = dwq7[h]
            self.seq = [(slot, fc) for slot in range(2) for fc in range(NFC)]
            self.i = 0
            self.evq = {}

        def feed(self, n):
            while n > 0 and self.i < len(self.seq):
                slot, fc = self.seq[self.i]
                self.i += 1
                n -= 1
                bidx = 5 + slot
                bank = pb[bidx]
                bw = []
                if fc == 0:
                    bw = [self.dw[0]] + p.takewaits(("bank", bidx))
                if fc in (4, 8, 12):
                    bw = bw + [self.dw[fc // 4]]
                ms = mm(bank[0:120, :512],
                        wbuf[self.par, :, 120 * fc:120 * (fc + 1)],
                        xbf(fc, slot * 512, 512),
                        start=(fc == 0), stop=(fc == NFC - 1),
                        waits=bw,
                        inc="pe" if fc == NFC - 1 else None)
                if fc == NFC - 1:
                    dstq = qbuf[self.par, 0:120, 512 * slot:512 * (slot + 1)]
                    ev = p.em("act", lambda s, d=dstq, b=bank, hh=self.h:
                              s.activation(d, b[0:120, :512], AF.Identity,
                                           bias=0.0, scale=wstd_c[0:120, hh:hh + 1]),
                              waits=[ms] + p.takewaits(("qbuf", self.par, slot)),
                              inc="act")
                    p.use(("bank", bidx), ev)
                    self.evq[slot] = ev

        def finish(self):
            self.feed(len(self.seq))
            h, par = self.h, self.par
            p.use(("wbuf", par), p.now("pe"))
            d_t = dma("sync", qbuf[par, 120:128, 0:1024],
                      selq_fm[8 * h:8 * (h + 1), :],
                      waits=p.takewaits(("qbuf_t", par)), inc="dst%d" % par)
            wkv = p.takewaits(("xkv", par)) + ([("cc2", 1), ("cc3", 1)] if h < 2 else [])
            l1 = dma("sync", xkv[par, :, 0:1024],
                     kf_out.ap()[128 * h:128 * (h + 1), :], waits=wkv, inc="dsk%d" % par)
            l2 = dma("sync", xkv[par, :, 1024:2048],
                     kf_out.ap()[H * 128 + 128 * h: H * 128 + 128 * (h + 1), :],
                     inc="dsk%d" % par)
            l3 = dma("sync", xkv[par, :, 2048:4112].rearrange("p (g c) -> p g c", g=16),
                     vv_out.ap().rearrange("(g p) c -> p g c", p=128)[:, :, 129 * h:129 * (h + 1)],
                     inc="dsk%d" % par)
            return [self.evq[0], self.evq[1], d_t, l1, l2, l3]

    SMIN = [0, 1, 2, 3, 0, 1, 2, 3]
    PAIR_LIVE = [[g for g in range(8) if SMIN[g] <= 1],
                 list(range(8))]

    tail_q = []
    OB7 = [2, 3, 4, 7]

    def flush_tail():
        while tail_q:
            th, tpairi, tcl = tail_q.pop(0)
            for i in range(4):
                msA = tcl[i]
                sqg = 4 * tpairi + i
                sw = p.takewaits(("seltmp",))
                r1 = p.em("dve", lambda v, b=pb[OB7[i]], q=i: v.reciprocal(
                    rec4[q], b[:, 128:129]),
                    waits=[msA] + p.takewaits(("rec", i)), inc="dve")
                a1 = p.em("dve", lambda v, b=pb[OB7[i]], q=i: v.tensor_scalar_mul(
                    atm4[q], b[:, 0:128], rec4[q]),
                    waits=[r1] + p.takewaits(("att_tm", i)) + sw, inc="dve")
                p.use(("rec", i), a1)
                tb = OB7[i]
                t1 = trans(pb_bf[tb][:, 0:128], atm4[i],
                           waits=[a1, d_ident])
                p.use(("att_tm", i), t1)
                dsta = cdk[:, SC * th + 128 * sqg: SC * th + 128 * sqg + 128]
                cm = p.em("act", lambda s, d=dsta, tb=tb: s.copy(d, pb_bf[tb][:, 0:128]),
                          waits=[t1], inc="act")
                p.use(("bank", tb), cm)
                p.use(("attn2",), cm)

    def emit_attn2(h, first_w, feeder=None):
        par = h % 2
        if "qf" in dbg and h == 13:
            dma("sync", dbg["qf"], qbuf[par, :, 0:1024], waits=list(first_w), inc="ds")
        kf = xkv[par, :, 0:2048]
        vv = xkv[par, :, 2048:4112]
        first_w = list(first_w)
        OB = [2, 3, 4, 7]
        prot = 0
        for pairi in range(2):
            base = 512 * pairi
            live = PAIR_LIVE[pairi]
            first_av = [True] * 4
            seqlist = [(g, k) for g in live for k in range(2)]
            last_touch = {}
            for i in range(4):
                s_i = (4 * pairi + i) // 2
                last_touch[i] = [gk for gk in seqlist if SMIN[gk[0]] <= s_i][-1]
            LEAD = 6
            pmeta = {}
            nseq = len(seqlist)

            def emit_S(t):
                nonlocal first_w, prot
                g, k = seqlist[t]
                w0 = max(256 * SMIN[g], base)
                width = base + 512 - w0
                sb_i = t % 2
                sbank = pb[sb_i]
                msS = mm(sbank[:, :width],
                         kf[:, 256 * g + 128 * k: 256 * g + 128 * (k + 1)],
                         qbuf[par, :, w0:base + 512], True, True,
                         waits=first_w + p.takewaits(("bank", sb_i)),
                         inc="pe")
                first_w = []
                rot = prot
                prot = (prot + 1) % 8
                pdst = pbufW[rot][:, 0:width]
                aE = p.em("act", lambda s, d=pdst, b=sbank, w=width: s.activation(
                    d, b[:, :w], AF.Exp, bias=nshift[:, 0:1], scale=SCALE2),
                    waits=[msS, m_nsh] + p.takewaits(("pbufP", rot)), inc="act")
                p.use(("bank", sb_i), aE)
                lastP = aE
                if SMIN[g] >= base // 256:
                    jm = g // 4
                    moff = ((SMIN[g] * 2 + jm) * 2 + k) * 256
                    lastP = p.em("dve", lambda v, d=pdst, mo=moff: v.tensor_mul(
                        d[:, 0:256], d[:, 0:256], mask_sb[:, mo:mo + 256]),
                        waits=[aE, d_mask], inc="dve")
                pmeta[t] = (rot, w0, lastP)

            def emit_AV(t):
                g, k = seqlist[t]
                skc = 2 * g + k
                rot, w0, lastP = pmeta.pop(t)
                n_sq = [i for i in range(4) if (4 * pairi + i) // 2 >= SMIN[g]]
                for ii, i in enumerate(n_sq):
                    col = 128 * i + base - w0
                    last_ch = (last_touch[i] == (g, k))
                    msA = mm(pb[OB[i]][:, :129],
                             pbufW[rot][:, col:col + 128],
                             vv[:, 129 * skc:129 * (skc + 1)],
                             start=first_av[i], stop=last_ch,
                             waits=[lastP] + oww[i],
                             inc="pe" if (ii == len(n_sq) - 1 or last_ch) else None)
                    oww[i] = []
                    first_av[i] = False
                    if ii == len(n_sq) - 1:
                        p.use(("pbufP", rot), msA)
                    if last_ch:
                        chunk_last[i] = msA

            chunk_last = {}
            for t in range(min(LEAD, nseq)):
                emit_S(t)
            oww = [p.takewaits(("bank", OB[i])) for i in range(4)]
            for t in range(nseq):
                if feeder:
                    feeder.feed(1)
                if t + LEAD < nseq:
                    emit_S(t + LEAD)
                if feeder:
                    feeder.feed(1)
                emit_AV(t)
            tail_q.append((h, pairi, chunk_last))
            flush_tail()
        endm = p.now("pe")
        p.use(("xkv", par), endm)
        for slot in range(2):
            p.use(("qbuf", par, slot), endm)
        p.use(("qbuf_t", par), endm)
        if PH7_SERIAL:
            p.barrier()

    qstd_dma(0)
    qstd_dma(1)
    fw_prev = QstdFeed(0).finish()
    for h in range(16):
        if h + 2 < 16:
            qstd_dma(h + 2)
        feeder = QstdFeed(h + 1) if h + 1 < 16 else None
        emit_attn2(h, fw_prev, feeder)
        fw_prev = feeder.finish() if feeder else None
    flush_tail()

    p.barrier()

    if "attn2" in dbg:
        dma("sync", dbg["attn2"], cdk[:, :], inc="ds")

    if stop_phase <= 7:
        return _finish(nc, p, sem, stck)

    PHASE_MARKS.append(("PH8", sum(1 for o in p.ops if o[0] == "pe")))
    bcm8 = []
    for slot in range(2):
        dA8 = dma("gp", abc[slot, 0, :, :],
                  rowbuf.ap()[2 * slot:2 * slot + 1, :].partition_broadcast(128),
                  waits=[lr1[slot][0]] + p.takewaits(("abcA", slot)),
                  inc="dgr%d" % (2 * slot))
        dB8 = dma("gp", abc[slot, 1, :, :],
                  rowbuf.ap()[2 * slot + 1:2 * slot + 2, :].partition_broadcast(128),
                  waits=[lr1[slot][1]] + p.takewaits(("abcB", slot)),
                  inc="dgr%d" % (2 * slot + 1))
        bcm8.append((dA8, dB8))
    sm2 = proj_resid_stats(
        wp_i, lambda ec, slot: cdk[:, SC * ec + 512 * slot: SC * ec + 512 * (slot + 1)],
        None,
        lambda jc, slot: zbuf.ap()[128 * jc:128 * (jc + 1),
                                   512 * slot:512 * (slot + 1)],
        x1buf.ap(),
        bft3=[zbft[0], zbft[1], kvs[1]],
        sqt3=[zsqt[0], zsqt[1], kvs[2]],
        xfix=(abc, bcm8))
    lr2, _, _ = ln_rows(4, sm2)
    out_ms = ln_normalize(4, lr2, out_t, x1buf.ap())

    return _finish(nc, p, sem, stck)


def _finish(nc, p, sem, stck):
    p.barrier(exclude=())
    p.em("sync", lambda e: e.nop())
    p.em("gp", lambda e: e.nop())
    p.em("pe", lambda e: e.nop())
    p.em("dve", lambda v: v.nop())
    p.em("act", lambda s: s.nop())

    with nc.Block() as block:
        def runner(engname):
            def _(eng):
                seen = {}
                for (e, fn, waits, inc, amt) in p.ops:
                    if e != engname:
                        continue
                    best = {}
                    for (s, v) in waits:
                        if v > 0 and v > seen.get(s, 0) and v > best.get(s, 0):
                            best[s] = v
                    for (s, v) in best.items():
                        eng.wait_ge(sem[s], v)
                        seen[s] = v
                    instr = fn(eng)
                    if inc is not None:
                        if inc.startswith("cc"):
                            instr.then_inc(sem[inc])
                        elif inc in ("pe", "dve", "act", "gpc"):
                            instr.then_inc(sem[inc], 1)
                        else:
                            instr.then_inc(sem[inc], 16)
            return _
        block.tensor(runner("pe"))
        block.vector(runner("dve"))
        block.scalar(runner("act"))
        block.gpsimd(runner("gp"))
        block.sync(runner("sync"))
    stck.close()
    return nc


def _host_inputs(inputs):
    f = np.float32
    wq = np.asarray(inputs["L0_attn_wq"], f)
    wk = np.asarray(inputs["L0_attn_wk"], f)
    wv = np.asarray(inputs["L0_attn_wv"], f)
    wo = np.asarray(inputs["L0_attn_wo"], f)
    Wa = np.asarray(inputs["L0_ffn_Wa"], f)
    Wp = np.asarray(inputs["L0_ffn_Wp"], f)
    def _img_bias(M, brow):
        pieces = []
        for stt, w in W16:
            blk = np.ascontiguousarray(M[:, stt:stt + w])
            im = blk.reshape(16, 128, w).transpose(1, 0, 2).reshape(128, 16 * w)
            bb = np.zeros((128, 128), np.float32)
            bb[0, :] = brow[stt:stt + w]
            pieces.append(np.concatenate([im, bb], axis=1))
        return np.ascontiguousarray(np.concatenate(pieces, axis=1))

    shared = dict(
        wq_i=_img_panels(np.ascontiguousarray(wq.T), W16),
        wk_i=_img_bias(np.ascontiguousarray(wk.T), np.asarray(inputs["L0_attn_bk"], f)),
        wv_i=_img_bias(np.ascontiguousarray(wv.T), np.asarray(inputs["L0_attn_bv"], f)),
        wo_i=_img_panels(np.ascontiguousarray(wo.T), W16),
        wp_i=_img_panels(Wp, W16),
        wa_i=_img_panels(Wa, WA_RANGES),
        bq_c=np.ascontiguousarray(np.asarray(inputs["L0_attn_bq"], f).reshape(16, 128).T),
        bo_c=np.ascontiguousarray(np.asarray(inputs["L0_attn_bo"], f).reshape(16, 128).T),
        bk_r=np.asarray(inputs["L0_attn_bk"], f).reshape(1, D),
        bv_r=np.asarray(inputs["L0_attn_bv"], f).reshape(1, D),
        wsr_i=np.concatenate([np.asarray(inputs["L0_ffn_w_std"], f),
                              np.asarray(inputs["L0_ffn_w_rec"], f)]).reshape(1, 32),
        r01_i=(np.arange(128)[None, :] // 8 == np.arange(16)[:, None]).astype(f),
        ident_i=np.eye(128, dtype=BF),
    )
    x = np.asarray(inputs["L0_attn_x"], f)
    per_core = []
    for c in range(8):
        b, parity = c // 2, c % 2
        xc = np.ascontiguousarray(x[b][_own_idx(parity)])
        xi = _img_panels(np.ascontiguousarray(xc.T), [(0, SC)])
        masks = _masks_for(parity).transpose(3, 0, 1, 2, 4).reshape(128, 16 * 256)
        per_core.append(dict(shared, x_img=xi, masks_i=np.ascontiguousarray(masks)))
    return per_core


def kernel(**inputs):
    per_core = _host_inputs(inputs)
    nc = build_program()
    res = run_bass_kernel_spmd(nc, per_core, core_ids=list(range(8)))
    x = np.asarray(inputs["L0_attn_x"])
    out = np.zeros((4, 2048, D), dtype=np.float32)
    for c in range(8):
        b, parity = c // 2, c % 2
        o = np.asarray(res.results[c]["out_fm"])
        out[b, _own_idx(parity)] = np.ascontiguousarray(o.T)
    return out


# version 22
# speedup vs baseline: 1.1758x; 1.0229x over previous
import contextlib
import numpy as np
import ml_dtypes

import concourse.bass as bass
import concourse.mybir as mybir
from concourse.bass_utils import run_bass_kernel_spmd

F32 = mybir.dt.float32
BF16 = mybir.dt.bfloat16
AF = mybir.ActivationFunctionType
ALU = mybir.AluOpType
BF = ml_dtypes.bfloat16

D = 2048
H = 16
NFC = 16
SC = 1024
CH = 256
OWN = [[0, 3, 4, 7], [1, 2, 5, 6]]
GLOB = OWN[0] + OWN[1]
LIVE = [sorted(g for g in range(8)
               if GLOB[g] <= OWN[0][s] or GLOB[g] <= OWN[1][s]) for s in range(4)]
EXP_SHIFT = 20.0
SCALE2 = float(1.0 / np.sqrt(np.float32(128.0)))
QSTD0, KSTD0, V0, LOW0 = 0, 1920, 3840, 5888
W16 = [(i * 128, 128) for i in range(16)]
WA_RANGES = ([(KSTD0 + 120 * i, 120) for i in range(16)] +
             [(LOW0, 256)] +
             [(V0 + 128 * i, 128) for i in range(16)] +
             [(QSTD0 + 120 * i, 120) for i in range(16)])
WA_OFF = np.cumsum([0] + [16 * w for _, w in WA_RANGES]).tolist()
GROUPS = [[0, 1], [2, 3], [4, 5], [6, 7]]


def _own_idx(parity):
    return np.concatenate([np.arange(CH) + c * CH for c in OWN[parity]])


def _img_panels(M, ranges):
    pieces = []
    for stt, w in ranges:
        blk = np.ascontiguousarray(M[:, stt:stt + w])
        pieces.append(blk.reshape(16, 128, w).transpose(1, 0, 2).reshape(128, 16 * w))
    return np.ascontiguousarray(np.concatenate(pieces, axis=1))


def _masks_for(parity):
    m = np.zeros((4, 2, 2, 128, 256), dtype=BF)
    for s in range(4):
        for j, g in enumerate((s, s + 4)):
            kg = GLOB[g] * CH + np.arange(CH)
            qg = OWN[parity][s] * CH + np.arange(CH)
            full = (kg[:, None] <= qg[None, :]).astype(np.float32)
            m[s, j, 0] = full[:128]
            m[s, j, 1] = full[128:]
    return m


class Views:

    def __init__(self, views):
        self.views = views

    def __getitem__(self, key):
        if isinstance(key, tuple):
            v = self.views[key[0]]
            rest = key[1:]
            if not rest:
                return v
            return v[rest if len(rest) > 1 else rest[0]]
        return self.views[key]


class Plan:
    def __init__(self):
        self.ops = []
        self.cnt = {}
        self.pending = {}
        self.last = {}

    def sem(self, n):
        self.cnt.setdefault(n, 0)

    def em(self, eng, fn, waits=(), inc=None, amt=1):
        w = list(self.pending.pop(eng, ())) + [x for x in waits if x is not None]
        self.ops.append((eng, fn, w, inc, amt))
        if inc is not None:
            self.cnt[inc] = self.cnt.get(inc, 0) + (amt if amt else 1)
            return (inc, self.cnt[inc])
        return None

    def now(self, s):
        return (s, self.cnt.get(s, 0))

    def barrier(self, exclude=("cc1", "cc2", "cc3", "cc4", "cc5", "cc6")):
        snap = [(s, c) for s, c in self.cnt.items() if c > 0 and s not in exclude]
        for e in ("pe", "dve", "act", "gp", "sync"):
            self.pending[e] = snap

    def use(self, key, *ms):
        cur = self.last.setdefault(key, {})
        for m in ms:
            if m is not None:
                s, v = m
                cur[s] = max(cur.get(s, 0), v)

    def takewaits(self, key):
        d = self.last.pop(key, {})
        return list(d.items())


PH1_LIMIT = 16
PH0_PATTERNS = True
PH1_BIAS = True
PH1_EVAC = True
PH7_SERIAL = False

PHASE_MARKS = []


def build_program(stop_phase=99, debug_taps=()):
    PHASE_MARKS.clear()
    nc = bass.Bass("TRN2", target_bir_lowering=False, debug=False)
    p = Plan()
    SEM_NAMES = ("pe", "dve", "act", "dg", "ds", "cc1", "cc2", "cc3",
                 "cc4", "cc5", "cc6",
                 "dgw0", "dgw1", "dgw2", "dgh0", "dgh1", "dgh2",
                 "dgi0", "dgi1", "dgi2", "dgj0", "dgj1", "dgj2", "dgb", "dkv",
                 "dgr0", "dgr1", "dgr2", "dgr3",
                 "dsx0", "dsx1", "dsx2", "dsx3", "dsz0", "dsz1", "dsz2", "dsr", "dsra",
                 "dst0", "dst1", "dsk0", "dsk1", "dckv", "dkf", "dvv",
                 "dgl", "dskt", "dsm", "dsc", "gpc")
    for n in SEM_NAMES:
        p.sem(n)
    WQ_SEMS = ("dgw%d", "dgh%d", "dgi%d", "dgj%d")

    x_img = nc.dram_tensor("x_img", [128, NFC * SC], F32, kind="ExternalInput").ap()
    wq_i = nc.dram_tensor("wq_i", [128, 8 * 4096], F32, kind="ExternalInput").ap()
    wk_i = nc.dram_tensor("wk_i", [128, 16 * 2176], F32, kind="ExternalInput").ap()
    wv_i = nc.dram_tensor("wv_i", [128, 16 * 2176], F32, kind="ExternalInput").ap()
    wo_i = nc.dram_tensor("wo_i", [128, 8 * 4096], F32, kind="ExternalInput").ap()
    wp_i = nc.dram_tensor("wp_i", [128, 8 * 4096], F32, kind="ExternalInput").ap()
    wa_i = nc.dram_tensor("wa_i", [128, WA_OFF[-1]], F32, kind="ExternalInput").ap()
    bq_c = nc.dram_tensor("bq_c", [128, 16], F32, kind="ExternalInput").ap()
    bo_c = nc.dram_tensor("bo_c", [128, 16], F32, kind="ExternalInput").ap()
    bk_r = nc.dram_tensor("bk_r", [1, D], F32, kind="ExternalInput").ap()
    bv_r = nc.dram_tensor("bv_r", [1, D], F32, kind="ExternalInput").ap()
    wsr_i = nc.dram_tensor("wsr_i", [1, 32], F32, kind="ExternalInput").ap()
    r01_i = nc.dram_tensor("r01_i", [16, 128], F32, kind="ExternalInput").ap()
    ident_i = nc.dram_tensor("ident_i", [128, 128], BF16, kind="ExternalInput").ap()
    masks_i = nc.dram_tensor("masks_i", [128, 16 * 256], BF16, kind="ExternalInput").ap()
    out_t = nc.dram_tensor("out_fm", [D, SC], F32, kind="ExternalOutput").ap()

    kv_in = nc.dram_tensor("kv_in", [128, H * 129], F32)
    kv_out = nc.dram_tensor("kv_out", [128, H * 129], F32)
    kf_in = nc.dram_tensor("kf_in", [H * 128, SC], BF16)
    kf_out = nc.dram_tensor("kf_out", [2 * H * 128, SC], BF16)
    vv_ins = [nc.dram_tensor("vv_in%d" % k, [8 * 128, 8 * 129], BF16)
              for k in range(2)]
    vv_outs = [nc.dram_tensor("vv_out%d" % k, [16 * 128, 8 * 129], BF16)
               for k in range(2)]
    zbuf = nc.dram_tensor("zbuf", [D, SC], F32)
    x1buf = nc.dram_tensor("x1buf", [D, SC], F32)
    rowbuf = nc.dram_tensor("rowbuf", [48, 512], F32)

    dbg = {}
    for name in debug_taps:
        shp = {"kv": [128, H * 129], "x1": [D, SC], "z": [D, SC],
               "kf": [2 * H * 128, SC], "vv": [16 * 128, H * 129],
               "attn2": [128, NFC * SC], "qf": [128, SC]}[name]
        dt = BF16 if name in ("kf", "vv", "attn2", "qf") else F32
        dbg[name] = nc.dram_tensor("dbg_" + name, shp, dt, kind="ExternalOutput").ap()

    stck = contextlib.ExitStack()
    sem = {n: stck.enter_context(nc.semaphore(n)) for n in SEM_NAMES}

    A = nc.alloc_sbuf_tensor

    def vsplit(ap, n, w):
        return Views([ap[:, i * w:(i + 1) * w] for i in range(n)])

    arena1 = A("arena1", [128, NFC * SC], BF16)
    qbuf_t = A("qbuf", [128, 2 * 2048], BF16)
    qbuf = vsplit(qbuf_t[:, :], 2, 2048)
    cdk = A("cdk", [128, H * SC], BF16)
    cdv = A("cdv", [128, 8 * H * 129], BF16)
    wbuf_t = A("wbuf", [128, 2 * 2176], BF16)
    wbuf = vsplit(wbuf_t[:, :], 2, 2176)
    xkv_t = A("xkv", [128, 2 * 4112], BF16)
    xkv = vsplit(xkv_t[:, :], 2, 4112)
    wpan = [wbuf[0], wbuf[1], xkv[0]]

    def wkey(si):
        return ("wbuf", si) if si < 2 else ("xkv", 0)
    kvbf_u = A("kvbf_u", [128, 2064], BF16)
    kvbf = kvbf_u[:, :]
    selq_fm = kvbf_u[:, 0:1024]
    selk_fm = kvbf_u[:, 1024:2048]
    kvs = Views([kvbf_u[:, i * 512:(i + 1) * 512] for i in range(4)])
    u1 = A("u1", [128, 2048], F32)
    trow = vsplit(u1[:, :], 2, 512)
    trow8 = vsplit(u1[:, :], 8, 256)
    x0t = vsplit(u1[:, :], 4, 512)
    mask_sb = u1[:, :].bitcast(BF16)
    u2 = A("u2", [128, 1536], F32)
    zt = vsplit(u2[:, :], 3, 512)
    pbufP = vsplit(u2[:, :].bitcast(BF16), 4, 256)
    u3 = A("u3", [128, 2048], F32)
    rbc = vsplit(u3[:, :], 4, 512)
    abc = Views([vsplit(u3[:, 0:1024], 2, 512), vsplit(u3[:, 1024:2048], 2, 512)])
    pbufW = vsplit(u3[:, :].bitcast(BF16), 8, 512)
    u4 = A("u4", [128, 1024], F32)
    u4b = u4[:, :].bitcast(BF16)
    zbft = vsplit(u4b, 2, 512)
    zsqt = Views([u4b[:, 1024 + i * 512:1024 + (i + 1) * 512] for i in range(2)])
    selq_tm = Views([u4b[:, i * 128:(i + 1) * 128] for i in range(8)])
    selk_tm = Views([u4b[:, 1024 + i * 128:1024 + (i + 1) * 128] for i in range(8)])
    u5 = A("u5", [128, 392], F32)
    kvst = vsplit(u5[:, 0:258], 2, 129)
    seltmp = vsplit(u5[:, 0:256], 2, 128)
    att_tm = Views([u5[:, 256 + i * 64:256 + (i + 1) * 64].bitcast(BF16) for i in range(2)])
    rec_t = Views([u5[:, 384 + i:385 + i] for i in range(2)])
    atm4 = Views([u5[:, 64 * i:64 * (i + 1)].bitcast(BF16) for i in range(4)])
    rec4 = Views([u5[:, 384 + i:385 + i] for i in range(4)])
    rows = A("rows", [1, 1024], F32)
    rows2 = A("rows2", [1, 1024], F32)
    patrow = rows2[0:1, 0:512]
    rows_bf = A("rows_bf", [1, 128], BF16)
    onesb_r = A("onesb_r", [1, 128], BF16)
    onesb_c = A("onesb_c", [128, 1], BF16)
    onesf_r = A("onesf_r", [1, 128], F32)
    eps_t = A("eps_t", [1, 1], F32)
    eps5_t = A("eps5_t", [1, 1], F32)
    nshift = A("nshift", [128, 1], F32)
    wrow = A("wrow", [1, 32], F32)
    wclip = A("wclip", [1, 32], F32)
    urow = A("urow", [1, 64], F32)
    patcol = A("patcol", [16, 4], F32)
    pat_t = A("pat", [128, 512], BF16)
    pat = vsplit(pat_t[:, :], 4, 128)
    wstd_c = A("wstd_c", [128, 16], F32)
    bqc_sb = A("bqc_sb", [128, 16], F32)
    boc_sb = A("boc_sb", [128, 16], F32)
    r01_sb = A("r01_sb", [16, 128], F32)
    ident = A("ident", [128, 128], BF16)

    pb = [nc.alloc_psum_tensor(f"pb{i}", [128, 512], F32) for i in range(8)]
    pb_bf = [b[:, :].bitcast(BF16) for b in pb]

    def mm(out_ap, lhsT, rhs, start, stop, waits=(), inc=None):
        return p.em("pe", lambda t: t.matmul(out_ap, lhsT, rhs, start=start, stop=stop),
                    waits=waits, inc=inc)

    def trans(out_ap, in_ap, waits=(), inc="pe"):
        return p.em("pe", lambda t: t.transpose(out_ap, in_ap, ident[:, :]),
                    waits=waits, inc=inc)

    def dma(q, out_ap, in_ap, waits=(), inc=None):
        return p.em(q, lambda e: e.dma_start(out=out_ap, in_=in_ap),
                    waits=waits, inc=inc, amt=16)

    def coll(kind, op, ins, outs, waits, inc):
        def fn(e):
            return e.collective_compute(kind, op, replica_groups=GROUPS, ins=ins, outs=outs)
        return p.em("gp", fn, waits=waits, inc=inc, amt=1)

    def xbf(fc, c0, w):
        return arena1[:, SC * fc + c0: SC * fc + c0 + w]

    dma("sync", ident[:, :], ident_i, inc="dsc")
    dma("sync", bqc_sb[:, :], bq_c, inc="dsc")
    dma("sync", boc_sb[:, :], bo_c, inc="dsc")
    dma("sync", wrow[:, :], wsr_i, inc="dsc")
    d_r01 = dma("sync", r01_sb[:, :], r01_i, inc="dsc")
    d_ident = d_bq = d_bo = d_wsr = d_r01 = p.now("dsc")
    d_x = dma("gp", arena1[:, :], x_img, inc="dg")

    p.em("dve", lambda v: v.memset(onesb_r[:, :], 1.0), inc="dve")
    p.em("dve", lambda v: v.memset(onesb_c[:, :], 1.0), inc="dve")
    p.em("dve", lambda v: v.memset(onesf_r[:, :], 1.0), inc="dve")
    m_eps = p.em("dve", lambda v: v.memset(eps_t[:, :], 1e-8), inc="dve")
    m_eps5 = p.em("dve", lambda v: v.memset(eps5_t[:, :], 1e-5), inc="dve")
    m_nsh = p.em("dve", lambda v: v.memset(nshift[:, :], -EXP_SHIFT), inc="dve")
    for sc in range(8):
        view = cdv[:, sc * (H * 129):(sc + 1) * (H * 129)].rearrange(
            "p (h c) -> p h c", h=H)[:, :, 128:129]
        p.em("dve", lambda v, vw=view: v.memset(vw, 1.0), inc="dve")
    mones = p.now("dve")

    mc = p.em("dve", lambda v: v.tensor_scalar_max(wclip[:, :], wrow[:, :], 1e-8),
              waits=[d_wsr], inc="dve")
    psub = p.em("dve", lambda v: v.tensor_scalar_sub(urow[0:1, 0:16], wrow[0:1, 16:32], 0.1),
                waits=[d_wsr], inc="dve")
    asg = p.em("act", lambda s: s.activation(urow[0:1, 0:16], urow[0:1, 0:16], AF.Sign,
                                             bias=0.0, scale=1.0),
               waits=[psub], inc="act")
    m_u = p.em("dve", lambda v: v.tensor_scalar_max(urow[0:1, 0:16], urow[0:1, 0:16], 0.0),
               waits=[asg], inc="dve")
    m_a = p.em("dve", lambda v: v.tensor_mul(urow[0:1, 16:32], urow[0:1, 0:16],
                                             wclip[0:1, 16:32]),
               waits=[m_u, mc], inc="dve")
    m_b = p.em("dve", lambda v: v.tensor_sub(urow[0:1, 32:48], wclip[0:1, 16:32],
                                             urow[0:1, 16:32]),
               waits=[m_a], inc="dve")
    m_n = p.em("dve", lambda v: v.tensor_scalar(urow[0:1, 48:64], urow[0:1, 0:16],
                                                -1.0, 1.0, ALU.mult, ALU.add),
               waits=[m_u], inc="dve")
    mrows = p.now("dve")

    srcrow = {0: 16, 1: 32, 2: 0, 3: 48}
    for i in range(4 if PH0_PATTERNS else 0):
        r_ = srcrow[i]
        m1 = mm(pb[0][0:16, 0:1], urow[0:1, r_:r_ + 16], onesf_r[0:1, 0:1], True, True,
                waits=[mrows, mones] + p.takewaits(("bank", 0)), inc="pe")
        c1 = p.em("dve", lambda v, i=i: v.tensor_copy(patcol[:, i:i + 1], pb[0][0:16, 0:1]),
                  waits=[m1], inc="dve")
        p.use(("bank", 0), c1)
        m2 = mm(pb[1][0:1, 0:128], patcol[:, i:i + 1], r01_sb[:, :], True, True,
                waits=[c1, d_r01] + p.takewaits(("bank", 1)), inc="pe")
        c2 = p.em("dve", lambda v, i=i: v.tensor_copy(patrow[0:1, i * 128:(i + 1) * 128],
                                                      pb[1][0:1, 0:128]),
                  waits=[m2], inc="dve")
        p.use(("bank", 1), c2)
        m3 = mm(pb[2][:, 0:128], onesf_r[0:1, :], patrow[0:1, i * 128:(i + 1) * 128],
                True, True, waits=[c2] + p.takewaits(("bank", 2)), inc="pe")
        c3 = p.em("dve", lambda v, i=i: v.tensor_copy(pat[i], pb[2][:, 0:128]),
                  waits=[m3], inc="dve")
        p.use(("bank", 2), c3)
    if PH0_PATTERNS:
        m4 = mm(pb[3][:, 0:16], onesf_r[0:1, :], wclip[0:1, 0:16], True, True,
                waits=[mc, mones] + p.takewaits(("bank", 3)), inc="pe")
        c4 = p.em("dve", lambda v: v.tensor_copy(wstd_c[:, :], pb[3][:, 0:16]),
                  waits=[m4], inc="dve")
        p.use(("bank", 3), c4)
    mpat = p.now("dve")

    p.barrier()

    if stop_phase <= 0:
        return _finish(nc, p, sem, stck)

    PHASE_MARKS.append(("PH1", sum(1 for o in p.ops if o[0] == "pe")))
    def tm_proj_b1(img_ap, bias_row, is_k):
        dws = [dma("gp", wbuf[0, :, :], img_ap[:, 0:2176],
                   waits=p.takewaits(("wbuf", 0)), inc="dgw0"),
               dma("gp", wbuf[1, :, :], img_ap[:, 2176:4352],
                   waits=p.takewaits(("wbuf", 1)), inc="dgw1")]
        for hidx in range(PH1_LIMIT):
            par = hidx % 2
            for wave in range(2):
                mss = []
                for i in range(4):
                    sc = wave * 4 + i
                    bidx = 4 * wave + i
                    bank = pb[bidx]
                    bw = [dws[hidx], d_x] + p.takewaits(("bank", bidx))
                    for fc in range(NFC):
                        mm(bank[:, :128], xbf(fc, sc * 128, 128),
                           wbuf[par, :, 128 * fc:128 * (fc + 1)],
                           start=(fc == 0), stop=False,
                           waits=bw if fc == 0 else [])
                        bw = []
                    ms = mm(bank[:, :128], onesb_r[0:1, :],
                            wbuf[par, 0:1, 2048:2176],
                            start=False, stop=True,
                            waits=[mones], inc="pe")
                    mss.append(ms)
                for i in range(4 if PH1_EVAC else 0):
                    sc = wave * 4 + i
                    bidx = 4 * wave + i
                    bank = pb[bidx]
                    if is_k:
                        pr = i % 4
                        t_t = trow8[2 * pr][:, :128]
                        t_r = trow8[2 * pr + 1][:, :128]
                        a1 = p.em("act", lambda s, b=bank, t=t_t: s.activation(
                            t, b[:, :128], AF.Exp, bias=0.0, scale=1.0),
                            waits=[mss[i]] + p.takewaits(("trow8", pr)), inc="act")
                        a2 = p.em("dve", lambda v, b=bank, t=t_r: v.tensor_scalar_max(
                            t, b[:, :128], 0.0),
                            waits=[mss[i], a1], inc="dve")
                        dst = cdk[:, 2048 * sc + 128 * hidx: 2048 * sc + 128 * (hidx + 1)]
                        a3 = p.em("dve", lambda v, d=dst, t1=t_t, t2=t_r:
                                  v.scalar_tensor_tensor(d, t1, 1.0, t2, ALU.min, ALU.add),
                            waits=[a1, a2], inc="dve")
                        p.use(("trow8", pr), a3)
                        p.use(("bank", bidx), a1, a2)
                        p.use(("cdk_w",), a3)
                    else:
                        dstv = cdv[:, sc * (H * 129) + 129 * hidx:
                                   sc * (H * 129) + 129 * hidx + 128]
                        a1 = p.em("act", lambda s, d=dstv, b=bank: s.copy(d, b[:, :128]),
                                  waits=[mss[i]], inc="act")
                        p.use(("bank", bidx), a1)
                        p.use(("cdv_w",), a1)
            p.use(("wbuf", par), p.now("pe"))
            if hidx + 2 < 16:
                par2 = (hidx + 2) % 2
                dws.append(dma("gp", wbuf[par2, :, :],
                               img_ap[:, (hidx + 2) * 2176:(hidx + 3) * 2176],
                               waits=p.takewaits(("wbuf", par2)),
                               inc="dgw%d" % par2))

    tm_proj_b1(wk_i, bk_r, True)
    if stop_phase <= 1:
        return _finish(nc, p, sem, stck)
    PHASE_MARKS.append(("PH2", sum(1 for o in p.ops if o[0] == "pe")))
    tm_proj_b1(wv_i, bv_r, False)
    if stop_phase <= 2:
        return _finish(nc, p, sem, stck)

    PHASE_MARKS.append(("PH3", sum(1 for o in p.ops if o[0] == "pe")))
    kq = p.takewaits(("cdk_w",)) + p.takewaits(("cdv_w",))
    kv_ev = []
    for h in range(H):
        bidx = h % 2
        bank = pb[bidx]
        bw = kq + p.takewaits(("bank", bidx))
        kq = []
        ms = None
        for sc in range(8):
            ms = mm(bank[:, :129],
                    cdk[:, 2048 * sc + 128 * h: 2048 * sc + 128 * (h + 1)],
                    cdv[:, sc * H * 129 + 129 * h: sc * H * 129 + 129 * (h + 1)],
                    start=(sc == 0), stop=(sc == 7),
                    waits=bw if sc == 0 else [], inc="pe" if sc == 7 else None)
            bw = []
        ev = p.em("act", lambda s, b=bank, h=h: s.copy(kvst[h % 2], b[:, :129]),
                  waits=[ms] + p.takewaits(("kvst", h % 2)), inc="act")
        p.use(("bank", bidx), ev)
        d_h = dma("sync", kv_in.ap()[:, 129 * h:129 * (h + 1)], kvst[h % 2],
                  waits=[ev], inc="dckv")
        p.use(("kvst", h % 2), d_h)
        kv_ev.append(d_h)

    dwq = {}

    def qproj_dma(h):
        wsi = h % 3
        ww = p.takewaits(wkey(wsi))
        ms4 = []
        for q in range(4):
            ms4.append(dma("gp", wpan[wsi][:, 512 * q:512 * (q + 1)],
                           wq_i[:, h * 2048 + 512 * q:h * 2048 + 512 * (q + 1)],
                           waits=ww, inc=WQ_SEMS[q] % wsi))
            ww = []
        dwq[h] = ms4

    qproj_dma(0)
    qproj_dma(1)
    qproj_dma(2)
    coll("AllReduce", ALU.add, [kv_in.ap()], [kv_out.ap()], waits=kv_ev, inc="cc1")

    if "kv" in dbg:
        dma("sync", dbg["kv"], kv_out.ap(), waits=[("cc1", 1)], inc="ds")

    if stop_phase <= 3:
        return _finish(nc, p, sem, stck)

    PHASE_MARKS.append(("PH4", sum(1 for o in p.ops if o[0] == "pe")))
    nrow = 8

    def emit_qproj(h):
        par = h % 2
        wsi = h % 3
        dw = dwq[h]
        ev_q = {}
        mss = {}
        for slot in range(2):
            bidx = (2 * h + slot) % 4
            bank = pb[bidx]
            bw = [dw[0], d_x] + p.takewaits(("bank", bidx))
            ms = None
            for fc in range(NFC):
                if fc in (4, 8, 12):
                    bw = bw + [dw[fc // 4]]
                ms = mm(bank[:, :512],
                        wpan[wsi][:, 128 * fc:128 * (fc + 1)],
                        xbf(fc, slot * 512, 512),
                        start=(fc == 0), stop=(fc == NFC - 1),
                        waits=bw,
                        inc="pe" if fc == NFC - 1 else None)
                bw = []
            mss[slot] = ms
        p.use(wkey(wsi), p.now("pe"))
        for slot in range(2):
            bidx = (2 * h + slot) % 4
            bank = pb[bidx]
            a1 = p.em("act", lambda s, b=bank, h=h: s.activation(
                trow[0][:, :512], b[:, :512], AF.Exp,
                bias=bqc_sb[:, h:h + 1], scale=1.0),
                waits=[mss[slot], d_bq] + p.takewaits(("trow", 0)), inc="act")
            a2 = p.em("dve", lambda v, b=bank, h=h: v.tensor_scalar(
                trow[1][:, :512], b[:, :512], bqc_sb[:, h:h + 1], 0.0,
                ALU.add, ALU.max),
                waits=[mss[slot], d_bq, a1], inc="dve")
            dstq = qbuf[par, :, 512 * slot:512 * (slot + 1)]
            a3 = p.em("dve", lambda v, d=dstq: v.scalar_tensor_tensor(
                d, trow[0][:, :512], 1.0, trow[1][:, :512], ALU.min, ALU.add),
                waits=[a1, a2] + p.takewaits(("qbuf", par, slot)), inc="dve")
            p.use(("trow", 0), a3)
            p.use(("trow", 1), a3)
            p.use(("bank", bidx), a1, a2)
            ev_q[slot] = a3
        return ev_q

    R4 = [rows[0:1, 0:512], rows[0:1, 512:1024],
          rows2[0:1, 0:512], rows2[0:1, 512:1024]]
    pend_n = []

    def attn1_stage1(h, ev_q):
        par = h % 2
        for slot in range(2):
            i2 = 2 * h + slot
            ridx = i2 % 4
            qsl = qbuf[par, :, 512 * slot:512 * (slot + 1)]
            b_u = pb[4 + slot]
            msu = mm(b_u[:, :512], kvbf[:, 129 * h:129 * h + 128], qsl, True, True,
                     waits=[ev_q[slot], d_kvbf] + p.takewaits(("bank", 4 + slot)),
                     inc="pe")
            msd = mm(pb[6][0:1, :512], kvbf[:, 129 * h + 128:129 * h + 129], qsl,
                     True, True, waits=p.takewaits(("bank", 6)), inc="pe")
            p.use(("qbuf", par, slot), msu, msd)
            a_d = p.em("act", lambda s, r=R4[ridx]: s.activation(
                r, pb[6][0:1, :512], AF.Identity,
                bias=eps_t[0:1, 0:1], scale=1.0),
                waits=[msd, m_eps] + p.takewaits(("rowsA", ridx)), inc="act")
            p.use(("bank", 6), a_d)
            r_d = p.em("dve", lambda v, r=R4[ridx]: v.reciprocal(r, r),
                       waits=[a_d], inc="dve")
            dsta = cdk[:, 1024 * h + 512 * slot: 1024 * h + 512 * (slot + 1)]
            a_u = p.em("act", lambda s, d=dsta, b=b_u: s.copy(d, b[:, :512]),
                       waits=[msu], inc="act")
            p.use(("bank", 4 + slot), a_u)
            pend_n.append((ridx, r_d, a_u, dsta))

    def attn1_stage2():
        while pend_n:
            ridx, r_d, a_u, dsta = pend_n.pop(0)
            bb = 7 - (ridx % 2)
            msb = mm(pb[bb][:, :512], onesf_r[0:1, :], R4[ridx], True, True,
                     waits=[r_d] + p.takewaits(("bank", bb)), inc="pe")
            t_m = p.em("dve", lambda v, d=dsta, bb=bb: v.tensor_mul(d, d, pb[bb][:, :512]),
                       waits=[a_u, msb], inc="dve")
            p.use(("bank", bb), t_m)
            p.use(("rowsA", ridx), msb)
            p.use(("attn1",), t_m)

    d_kvbf = dma("gp", kvbf[:, :], kv_out.ap(), waits=[("cc1", 1)], inc="dkv")
    ev_prev = emit_qproj(0)
    for h in range(16):
        if h + 3 < 16:
            qproj_dma(h + 3)
        attn1_stage2()
        attn1_stage1(h, ev_prev)
        if h + 1 < 16:
            ev_prev = emit_qproj(h + 1)
    attn1_stage2()

    p.barrier()

    PHASE_MARKS.append(("PH5", sum(1 for o in p.ops if o[0] == "pe")))
    def proj_resid_stats(img_ap, rhs_fn, bias_col, xres_fn, zdst_ap,
                         bf_fn=None, bft3=None, sqt3=None, xfix=None):
        pend = []
        stat_started = [False, False]
        stat_sl = [[], []]

        def flush_stats(keep=0):
            while len(pend) > keep:
                jc, slot, zb_ms, zq_ms, zrot, s1rhs, trk = pend.pop(0)
                b_s, b_q = pb[4 + slot], pb[6 + slot]
                first = not stat_started[slot]
                w1 = [zb_ms] + (p.takewaits(("bank", 4 + slot)) if first else [])
                w2 = [zq_ms] + (p.takewaits(("bank", 6 + slot)) if first else [])
                s1 = mm(b_s[0:1, :512], onesb_c[:, 0:1], s1rhs,
                        start=first, stop=(jc == NFC - 1), waits=w1, inc="pe")
                s2 = mm(b_q[0:1, :512], onesb_c[:, 0:1], sqt3[zrot],
                        start=first, stop=(jc == NFC - 1), waits=w2, inc="pe")
                stat_started[slot] = True
                if trk:
                    p.use(("zbft8", zrot), s1)
                p.use(("zsqt8", zrot), s2)
                if jc == NFC - 1:
                    stat_sl[slot] = [s1, s2]

        def w_dma(jc):
            si2 = jc % 3
            ww = p.takewaits(wkey(si2))
            ms4 = []
            for q in range(4):
                ms4.append(dma("gp", wpan[si2][:, 512 * q:512 * (q + 1)],
                               img_ap[:, jc * 2048 + 512 * q:
                                      jc * 2048 + 512 * (q + 1)],
                               waits=ww, inc=WQ_SEMS[q] % si2))
                ww = []
            return ms4

        dws = [w_dma(0), w_dma(1), w_dma(2)]
        for jc in range(16):
            par = jc % 3
            for slot in range(2):
                idx = 2 * jc + slot
                bidx = slot
                bank = pb[bidx]
                xrot = idx % 4
                d_x0 = dma("sync", x0t[xrot, :, :], xres_fn(jc, slot),
                           waits=p.takewaits(("x0t", xrot)), inc="dsx%d" % xrot)
                if xfix is not None:
                    axc, bcm8 = xfix
                    c1 = p.em("dve", lambda v, xr=xrot, s=slot: v.tensor_mul(
                        x0t[xr, :, :], x0t[xr, :, :], axc[s, 0, :, :]),
                        waits=[d_x0, bcm8[slot][0]], inc="dve")
                    c2 = p.em("dve", lambda v, xr=xrot, s=slot: v.tensor_add(
                        x0t[xr, :, :], x0t[xr, :, :], axc[s, 1, :, :]),
                        waits=[c1, bcm8[slot][1]], inc="dve")
                    p.use(("abcA", slot), c1)
                    p.use(("abcB", slot), c2)
                    xr_ms = c2
                else:
                    xr_ms = d_x0
                bw = [dws[jc][0]] + p.takewaits(("bank", bidx))
                ms = None
                for ec in range(NFC):
                    if ec in (4, 8, 12):
                        bw = bw + [dws[jc][ec // 4]]
                    ms = mm(bank[:, :512],
                            wpan[par][:, 128 * ec:128 * (ec + 1)],
                            rhs_fn(ec, slot),
                            start=(ec == 0), stop=(ec == NFC - 1),
                            waits=bw,
                            inc="pe" if ec == NFC - 1 else None)
                    bw = []
                zr = idx % 3
                if bias_col is not None:
                    ev = p.em("dve", lambda v, b=bank, jc=jc, zr=zr, xr=xrot:
                              v.scalar_tensor_tensor(zt[zr][:, :512], b[:, :512],
                                                     bias_col[:, jc:jc + 1],
                                                     x0t[xr, :, :], ALU.add, ALU.add),
                              waits=[ms, xr_ms, d_bo] + p.takewaits(("zt", zr)),
                              inc="dve")
                else:
                    ev = p.em("dve", lambda v, b=bank, zr=zr, xr=xrot:
                              v.scalar_tensor_tensor(zt[zr][:, :512], b[:, :512],
                                                     1.0, x0t[xr, :, :],
                                                     ALU.mult, ALU.add),
                              waits=[ms, xr_ms] + p.takewaits(("zt", zr)), inc="dve")
                p.use(("x0t", xrot), ev)
                p.use(("bank", bidx), ev)
                d_z = dma("sync", zdst_ap[128 * jc:128 * (jc + 1),
                                          512 * slot:512 * (slot + 1)],
                          zt[zr][:, :512], waits=[ev], inc="dsz%d" % zr)
                brot = idx % 3
                if bf_fn is not None:
                    bfdst = bf_fn(jc, slot)
                    zb = p.em("dve", lambda v, zr=zr, d=bfdst:
                              v.tensor_copy(d, zt[zr][:, :512]),
                              waits=[ev], inc="dve")
                    s1rhs, trk = bfdst, False
                else:
                    zb = p.em("dve", lambda v, zr=zr, br=brot:
                              v.tensor_copy(bft3[br], zt[zr][:, :512]),
                              waits=[ev] + p.takewaits(("zbft8", brot)), inc="dve")
                    s1rhs, trk = bft3[brot], True
                zq = p.em("act", lambda s, zr=zr, br=brot:
                          s.activation(sqt3[br], zt[zr][:, :512],
                                       AF.Square, bias=0.0, scale=1.0),
                          waits=[zb] + p.takewaits(("zsqt8", brot)), inc="act")
                p.use(("zt", zr), d_z, zq)
                pend.append((jc, slot, zb, zq, brot, s1rhs, trk))
                flush_stats(keep=2)
            p.use(wkey(par), p.now("pe"))
            if jc + 3 < 16:
                dws.append(w_dma(jc + 3))
        flush_stats()
        return stat_sl

    def ln_rows(base, stat_sl, mkbf=False):
        outm = []
        en_bf = []
        ea_bf = []
        for slot in range(2):
            b_s, b_q = pb[4 + slot], pb[6 + slot]
            rw = p.takewaits(("rowsA",)) + p.takewaits(("rowsB",))
            a1 = p.em("act", lambda s, b=b_s: s.activation(
                rows[0:1, 0:512], b[0:1, :512], AF.Identity, bias=0.0,
                scale=-1.0 / 2048.0), waits=stat_sl[slot] + rw, inc="act")
            a2 = p.em("act", lambda s, b=b_q: s.activation(
                rows[0:1, 512:1024], b[0:1, :512], AF.Identity, bias=0.0,
                scale=1.0 / 2048.0), inc="act")
            p.use(("bank", 4 + slot), a1)
            p.use(("bank", 6 + slot), a2)
            if mkbf:
                mn = mm(pb[4 + slot][:, :512], onesf_r[0:1, :], rows[0:1, 0:512],
                        True, True, waits=[a1] + p.takewaits(("bank", 4 + slot)),
                        inc="pe")
                p.use(("rowsA",), mn)
                en = p.em("act", lambda s, sl=slot: s.copy(
                    zbft[sl][:, :512], pb[4 + sl][:, :512]),
                    waits=[mn] + p.takewaits(("zbft8", slot)), inc="act")
                p.use(("bank", 4 + slot), en)
                en_bf.append(en)
            d1 = p.em("dve", lambda v: v.tensor_mul(rows2[0:1, 0:512],
                                                    rows[0:1, 0:512], rows[0:1, 0:512]),
                      waits=[a1] + p.takewaits(("rows2",)), inc="dve")
            d2 = p.em("dve", lambda v: v.tensor_sub(rows[0:1, 512:1024],
                                                    rows[0:1, 512:1024], rows2[0:1, 0:512]),
                      waits=[a2, d1], inc="dve")
            a3 = p.em("act", lambda s: s.activation(rows[0:1, 512:1024],
                                                    rows[0:1, 512:1024], AF.Sqrt,
                                                    bias=eps5_t[0:1, 0:1], scale=1.0),
                      waits=[d2, m_eps5], inc="act")
            d3 = p.em("dve", lambda v: v.reciprocal(rows2[0:1, 0:512],
                                                    rows[0:1, 512:1024]),
                      waits=[a3], inc="dve")
            dA = dma("sync", rowbuf.ap()[base + 2 * slot:base + 2 * slot + 1, :],
                     rows2[0:1, 0:512], waits=[d3], inc="dsr")
            if mkbf:
                ma = mm(pb[6 + slot][:, :512], onesf_r[0:1, :], rows2[0:1, 0:512],
                        True, True, waits=[d3] + p.takewaits(("bank", 6 + slot)),
                        inc="pe")
                p.use(("rows2",), ma)
                ea = p.em("act", lambda s, sl=slot: s.copy(
                    zsqt[sl][:, :512], pb[6 + sl][:, :512]),
                    waits=[ma] + p.takewaits(("zsqt8", slot)), inc="act")
                p.use(("bank", 6 + slot), ea)
                ea_bf.append(ea)
            d4 = p.em("dve", lambda v: v.scalar_tensor_tensor(
                rows2[0:1, 512:1024], rows[0:1, 0:512], 1.0, rows2[0:1, 0:512],
                ALU.mult, ALU.mult), waits=[d3], inc="dve")
            dB = dma("sync", rowbuf.ap()[base + 2 * slot + 1:base + 2 * slot + 2, :],
                     rows2[0:1, 512:1024], waits=[d4], inc="dsra")
            p.use(("rowsA",), d1, d4)
            p.use(("rowsB",), a3)
            p.use(("rows2",), dA, dB, d4)
            outm.append((dA, dB))
        return outm, en_bf, ea_bf

    def ln_normalize(base, rowdma, xout_ap, zsrc_ap):
        bms = {}
        for slot in range(2):
            dA = dma("gp", abc[slot, 0, :, :],
                     rowbuf.ap()[base + 2 * slot:base + 2 * slot + 1, :]
                     .partition_broadcast(128),
                     waits=[rowdma[slot][0]] + p.takewaits(("abcA", slot)),
                     inc="dgr%d" % (2 * slot))
            dB = dma("gp", abc[slot, 1, :, :],
                     rowbuf.ap()[base + 2 * slot + 1:base + 2 * slot + 2, :]
                     .partition_broadcast(128),
                     waits=[rowdma[slot][1]] + p.takewaits(("abcB", slot)),
                     inc="dgr%d" % (2 * slot + 1))
            bms[slot] = (dA, dB)
        zts = [zt[0], zt[1], zt[2], x0t[0], x0t[1], x0t[2], x0t[3]]

        def zkey(zr):
            return ("zt", zr) if zr < 3 else ("x0t", zr - 3)

        def zsem(zr):
            return "dsz%d" % zr if zr < 3 else "dsx%d" % (zr - 3)

        lastms = []
        for jc in range(NFC):
            for slot in range(2):
                idx = 2 * jc + slot
                zr = idx % 7
                d_z = dma("sync", zts[zr][:, :512],
                          zsrc_ap[128 * jc:128 * (jc + 1), 512 * slot:512 * (slot + 1)],
                          waits=p.takewaits(zkey(zr)), inc=zsem(zr))
                eng = "dve" if (idx % 2 == 0) else "gp"
                esem = "dve" if eng == "dve" else "gpc"
                t1 = p.em(eng, lambda v, zr=zr, s=slot: v.tensor_mul(
                    zts[zr][:, :512], zts[zr][:, :512], abc[s, 0, :, :]),
                    waits=[d_z, bms[slot][0]], inc=esem)
                t2 = p.em(eng, lambda v, zr=zr, s=slot: v.tensor_add(
                    zts[zr][:, :512], zts[zr][:, :512], abc[s, 1, :, :]),
                    waits=[t1, bms[slot][1]], inc=esem)
                p.use(("abcA", slot), t1)
                p.use(("abcB", slot), t2)
                d_o = dma("act", xout_ap[128 * jc:128 * (jc + 1),
                                         512 * slot:512 * (slot + 1)],
                          zts[zr][:, :512], waits=[t2], inc="dsc")
                p.use(zkey(zr), d_o)
                lastms.append(d_o)
        return lastms

    def wa_dma(pi, si):
        width = WA_RANGES[pi][1]
        ww = p.takewaits(wkey(si))
        ms4 = []
        for q in range(4):
            ms4.append(dma("gp", wpan[si][:, 4 * q * width:4 * (q + 1) * width],
                           wa_i[:, WA_OFF[pi] + 4 * q * width:
                                WA_OFF[pi] + 4 * (q + 1) * width],
                           waits=ww, inc=WQ_SEMS[q] % si))
            ww = []
        return ms4

    sm1 = proj_resid_stats(
        wo_i, lambda ec, slot: cdk[:, 1024 * ec + 512 * slot: 1024 * ec + 512 * (slot + 1)],
        boc_sb,
        lambda jc, slot: x_img[:, 1024 * jc + 512 * slot: 1024 * jc + 512 * (slot + 1)],
        zbuf.ap(),
        bf_fn=lambda jc, slot: arena1[:, SC * jc + 512 * slot:
                                      SC * jc + 512 * (slot + 1)],
        sqt3=[zsqt[0], zsqt[1], kvs[0]])
    p.use(("cdk_r",), p.now("pe"))
    dws6 = {}

    def kstd_dma(h):
        dws6[h] = wa_dma(h, h % 3)

    kstd_dma(0)
    kstd_dma(1)
    kstd_dma(2)
    lowbuf = mask_sb
    dwl_w = (p.takewaits(("x0t", 0)) + p.takewaits(("x0t", 1)) +
             p.takewaits(("x0t", 2)) + p.takewaits(("x0t", 3)) +
             p.takewaits(("trow", 0)) + p.takewaits(("trow", 1)))
    dwl = dma("gp", lowbuf[:, 0:4096], wa_i[:, WA_OFF[16]:WA_OFF[17]],
              waits=dwl_w, inc="dgl")
    lr1, en_bf, ea_bf = ln_rows(0, sm1, mkbf=True)
    x1bf_ms = {}
    cc_slot = {0: [], 1: []}
    for jc in range(NFC):
        for slot in range(2):
            idx = 2 * jc + slot
            eng = "dve" if idx % 2 == 0 else "gp"
            esem = "dve" if eng == "dve" else "gpc"
            tile = arena1[:, SC * jc + 512 * slot: SC * jc + 512 * (slot + 1)]
            c1 = p.em(eng, lambda v, t=tile, s=slot: v.tensor_add(t, t, zbft[s]),
                      waits=[en_bf[slot]], inc=esem)
            c2 = p.em(eng, lambda v, t=tile, s=slot: v.tensor_mul(t, t, zsqt[s]),
                      waits=[c1, ea_bf[slot]], inc=esem)
            x1bf_ms.setdefault(jc, []).append(c2)
            cc_slot[slot].append(c2)
    for slot in range(2):
        p.use(("zbft8", slot), *cc_slot[slot])
        p.use(("zsqt8", slot), *cc_slot[slot])

    if stop_phase <= 5:
        return _finish(nc, p, sem, stck)

    PHASE_MARKS.append(("PH6k", sum(1 for o in p.ops if o[0] == "pe")))
    for h in range(16):
        par = h % 3
        mss = {}
        for slot in range(2):
            bank = pb[slot]
            bw = [dws6[h][0]] + p.takewaits(("bank", slot))
            ms = None
            for fc in range(NFC):
                if fc in (4, 8, 12):
                    bw = bw + [dws6[h][fc // 4]]
                if h == 0:
                    bw = bw + x1bf_ms.get(fc, [])
                ms = mm(bank[0:120, :512],
                        wpan[par][:, 120 * fc:120 * (fc + 1)],
                        xbf(fc, slot * 512, 512),
                        start=(fc == 0), stop=(fc == NFC - 1),
                        waits=bw,
                        inc="pe" if fc == NFC - 1 else None)
                bw = []
            mss[slot] = ms
        p.use(wkey(par), p.now("pe"))
        if h + 3 < 16:
            kstd_dma(h + 3)
        for slot in range(2):
            bank = pb[slot]
            dst = cdk[0:120, 1024 * h + 512 * slot: 1024 * h + 512 * (slot + 1)]
            ev = p.em("act", lambda s, d=dst, b=bank: s.copy(d, b[0:120, :512]),
                      waits=[mss[slot]] + p.takewaits(("cdk_r",)), inc="act")
            p.use(("bank", slot), ev)
            p.use(("kstd",), ev)

    PHASE_MARKS.append(("PH6l", sum(1 for o in p.ops if o[0] == "pe")))
    sel_ev = []
    for wave in range(2):
        mss = []
        for i in range(4):
            sc = 4 * wave + i
            bidx = 4 + i
            bank = pb[bidx]
            bw = [dwl] + p.takewaits(("bank", bidx))
            ms = None
            for fc in range(NFC):
                ms = mm(bank[:, :256], xbf(fc, sc * 128, 128),
                        lowbuf[:, 256 * fc:256 * (fc + 1)],
                        start=(fc == 0), stop=(fc == NFC - 1),
                        waits=bw if fc == 0 else [],
                        inc="pe" if fc == NFC - 1 else None)
                bw = []
            mss.append(ms)
        for i in range(4):
            sc = 4 * wave + i
            bidx = 4 + i
            bank = pb[bidx]
            e1 = p.em("dve", lambda v, b=bank: v.tensor_mul(seltmp[0], b[:, 128:256], pat[0]),
                      waits=[mss[i], mpat] + p.takewaits(("seltmp",)) +
                      p.takewaits(("zbft8", 0)) + p.takewaits(("zbft8", 1)) +
                      p.takewaits(("zsqt8", 0)) + p.takewaits(("zsqt8", 1)), inc="dve")
            e2 = p.em("dve", lambda v, b=bank: v.tensor_mul(seltmp[1], b[:, 0:128], pat[1]),
                      waits=[mss[i]], inc="dve")
            e3 = p.em("dve", lambda v, sc=sc: v.tensor_add(selq_tm[sc], seltmp[0], seltmp[1]),
                      waits=[e1, e2] + p.takewaits(("u4q", sc)), inc="dve")
            e4 = p.em("dve", lambda v, b=bank: v.tensor_mul(seltmp[0], b[:, 0:128], pat[2]),
                      waits=[e3], inc="dve")
            e5 = p.em("dve", lambda v, b=bank: v.tensor_mul(seltmp[1], b[:, 128:256], pat[3]),
                      waits=[e3], inc="dve")
            e6 = p.em("dve", lambda v, sc=sc: v.tensor_add(selk_tm[sc], seltmp[0], seltmp[1]),
                      waits=[e4, e5] + p.takewaits(("u4k", sc)), inc="dve")
            p.use(("seltmp",), e6)
            p.use(("bank", bidx), e4, e5)
            sel_ev.append((e3, e6))
    for sc in range(8):
        t1 = trans(pb_bf[7][:, 0:128], selq_tm[sc],
                   waits=[sel_ev[sc][0], d_ident] + p.takewaits(("bank", 7)))
        c1 = p.em("act", lambda s, sc=sc: s.copy(selq_fm[:, 128 * sc:128 * (sc + 1)],
                                                 pb_bf[7][:, 0:128]),
                  waits=[t1], inc="act")
        p.use(("bank", 7), c1)
        t2 = trans(pb_bf[7][:, 0:128], selk_tm[sc],
                   waits=[sel_ev[sc][1]] + p.takewaits(("bank", 7)))
        c2 = p.em("act", lambda s, sc=sc: s.copy(selk_fm[:, 128 * sc:128 * (sc + 1)],
                                                 pb_bf[7][:, 0:128]),
                  waits=[t2], inc="act")
        p.use(("bank", 7), c2)
        p.use(("selq_fm",), c1)
        p.use(("selk_fm",), c2)

    low_last_pe = p.now("pe")

    wkt = p.takewaits(("selk_fm",)) + p.takewaits(("kstd",))
    ktail = []
    for h in range(H):
        d_t = dma("sync", cdk[120:128, 1024 * h:1024 * (h + 1)],
                  selk_fm[8 * h:8 * (h + 1), :],
                  waits=wkt if h == 0 else [], inc="dskt")
        wkt = []
        ktail.append(d_t)
    d_kf = dma("sync", kf_in.ap().rearrange("(h p) s -> p h s", p=128),
               cdk[:, :].rearrange("p (h s) -> p h s", h=H),
               waits=ktail, inc="dkf")

    dws7 = {}

    def v_dma(h):
        dws7[h] = wa_dma(17 + h, h % 3)

    v_dma(0)
    v_dma(1)
    v_dma(2)
    coll("AllGather", ALU.bypass, [kf_in.ap()], [kf_out.ap()], waits=[d_kf], inc="cc2")

    PHASE_MARKS.append(("PH6v", sum(1 for o in p.ops if o[0] == "pe")))
    vstg = {}
    for h in range(16):
        par = h % 3
        for wave in range(2):
            mss = []
            for i in range(4):
                sc = 4 * wave + i
                bank = pb[i]
                bw = [dws7[h][0]] + p.takewaits(("bank", i))
                ms = None
                for fc in range(NFC):
                    if fc in (4, 8, 12):
                        bw = bw + [dws7[h][fc // 4]]
                    ms = mm(bank[:, :128], xbf(fc, sc * 128, 128),
                            wpan[par][:, 128 * fc:128 * (fc + 1)],
                            start=(fc == 0), stop=(fc == NFC - 1),
                            waits=bw,
                            inc="pe" if fc == NFC - 1 else None)
                    bw = []
                mss.append(ms)
            for i in range(4):
                sc = 4 * wave + i
                bank = pb[i]
                dstv = cdv[:, sc * (H * 129) + 129 * h: sc * (H * 129) + 129 * h + 128]
                ev = p.em("act", lambda s, d=dstv, b=bank: s.copy(d, b[:, :128]),
                          waits=[mss[i]], inc="act")
                p.use(("bank", i), ev)
                p.use(("cdv2", h // 8), ev)
        p.use(wkey(par), p.now("pe"))
        if h + 3 < 16:
            v_dma(h + 3)
        if h % 8 == 7:
            k = h // 8
            vstg[k] = dma("sync",
                          vv_ins[k].ap().rearrange("(c p) s -> p c s", p=128),
                          cdv[:, :].rearrange("p (c hs) -> p c hs", c=8)
                          [:, :, 129 * 8 * k:129 * 8 * (k + 1)],
                          waits=p.takewaits(("cdv2", k)), inc="dvv")

    dwq7 = {}

    def qstd_dma(h):
        dwq7[h] = wa_dma(33 + h, h % 2)

    qstd_dma(0)
    qstd_dma(1)
    for k in range(2):
        coll("AllGather", ALU.bypass, [vv_ins[k].ap()], [vv_outs[k].ap()],
             waits=[vstg[k]], inc="cc%d" % (3 + k))

    d_mask = dma("sync", mask_sb[:, 0:4096], masks_i,
                 waits=[low_last_pe], inc="dsm")

    if "kf" in dbg:
        dma("sync", dbg["kf"], kf_out.ap(), waits=[("cc2", 1)], inc="ds")

    if stop_phase <= 6:
        return _finish(nc, p, sem, stck)

    PHASE_MARKS.append(("PH7", sum(1 for o in p.ops if o[0] == "pe")))
    class QstdFeed:

        def __init__(self, h):
            self.h = h
            self.par = h % 2
            self.dw = dwq7[h]
            self.seq = [(slot, fc) for slot in range(2) for fc in range(NFC)]
            self.i = 0
            self.evq = {}

        def feed(self, n):
            while n > 0 and self.i < len(self.seq):
                slot, fc = self.seq[self.i]
                self.i += 1
                n -= 1
                bidx = 5 + slot
                bank = pb[bidx]
                bw = []
                if fc == 0:
                    bw = [self.dw[0]] + p.takewaits(("bank", bidx))
                if fc in (4, 8, 12):
                    bw = bw + [self.dw[fc // 4]]
                ms = mm(bank[0:120, :512],
                        wbuf[self.par, :, 120 * fc:120 * (fc + 1)],
                        xbf(fc, slot * 512, 512),
                        start=(fc == 0), stop=(fc == NFC - 1),
                        waits=bw,
                        inc="pe" if fc == NFC - 1 else None)
                if fc == NFC - 1:
                    dstq = qbuf[self.par, 0:120, 512 * slot:512 * (slot + 1)]
                    ev = p.em("act", lambda s, d=dstq, b=bank, hh=self.h:
                              s.activation(d, b[0:120, :512], AF.Identity,
                                           bias=0.0, scale=wstd_c[0:120, hh:hh + 1]),
                              waits=[ms] + p.takewaits(("qbuf", self.par, slot)),
                              inc="act")
                    p.use(("bank", bidx), ev)
                    self.evq[slot] = ev

        def finish(self):
            self.feed(len(self.seq))
            h, par = self.h, self.par
            p.use(("wbuf", par), p.now("pe"))
            d_t = dma("sync", qbuf[par, 120:128, 0:1024],
                      selq_fm[8 * h:8 * (h + 1), :],
                      waits=p.takewaits(("qbuf_t", par)), inc="dst%d" % par)
            wkv = p.takewaits(("xkv", par)) + ([("cc2", 1)] if h < 2 else [])
            l1 = dma("sync", xkv[par, :, 0:1024],
                     kf_out.ap()[128 * h:128 * (h + 1), :], waits=wkv, inc="dsk%d" % par)
            l2 = dma("sync", xkv[par, :, 1024:2048],
                     kf_out.ap()[H * 128 + 128 * h: H * 128 + 128 * (h + 1), :],
                     inc="dsk%d" % par)
            l3 = dma("sync", xkv[par, :, 2048:4112].rearrange("p (g c) -> p g c", g=16),
                     vv_outs[h // 8].ap().rearrange("(g p) c -> p g c", p=128)
                     [:, :, 129 * (h % 8):129 * (h % 8 + 1)],
                     waits=[("cc%d" % (3 + h // 8), 1)], inc="dsk%d" % par)
            return [self.evq[0], self.evq[1], d_t, l1, l2, l3]

    SMIN = [0, 1, 2, 3, 0, 1, 2, 3]
    PAIR_LIVE = [[g for g in range(8) if SMIN[g] <= 1],
                 list(range(8))]

    tail_q = []
    OB7 = [2, 3, 4, 7]

    def flush_tail():
        while tail_q:
            th, tpairi, tcl = tail_q.pop(0)
            for i in range(4):
                msA = tcl[i]
                sqg = 4 * tpairi + i
                sw = p.takewaits(("seltmp",))
                r1 = p.em("dve", lambda v, b=pb[OB7[i]], q=i: v.reciprocal(
                    rec4[q], b[:, 128:129]),
                    waits=[msA] + p.takewaits(("rec", i)), inc="dve")
                a1 = p.em("dve", lambda v, b=pb[OB7[i]], q=i: v.tensor_scalar_mul(
                    atm4[q], b[:, 0:128], rec4[q]),
                    waits=[r1] + p.takewaits(("att_tm", i)) + sw, inc="dve")
                p.use(("rec", i), a1)
                tb = OB7[i]
                t1 = trans(pb_bf[tb][:, 0:128], atm4[i],
                           waits=[a1, d_ident])
                p.use(("att_tm", i), t1)
                dsta = cdk[:, SC * th + 128 * sqg: SC * th + 128 * sqg + 128]
                cm = p.em("act", lambda s, d=dsta, tb=tb: s.copy(d, pb_bf[tb][:, 0:128]),
                          waits=[t1], inc="act")
                p.use(("bank", tb), cm)
                p.use(("attn2",), cm)

    def emit_attn2(h, first_w, feeder=None):
        par = h % 2
        if "qf" in dbg and h == 13:
            dma("sync", dbg["qf"], qbuf[par, :, 0:1024], waits=list(first_w), inc="ds")
        kf = xkv[par, :, 0:2048]
        vv = xkv[par, :, 2048:4112]
        first_w = list(first_w)
        OB = [2, 3, 4, 7]
        prot = 0
        for pairi in range(2):
            base = 512 * pairi
            live = PAIR_LIVE[pairi]
            first_av = [True] * 4
            seqlist = [(g, k) for g in live for k in range(2)]
            last_touch = {}
            for i in range(4):
                s_i = (4 * pairi + i) // 2
                last_touch[i] = [gk for gk in seqlist if SMIN[gk[0]] <= s_i][-1]
            LEAD = 6
            pmeta = {}
            nseq = len(seqlist)

            def emit_S(t):
                nonlocal first_w, prot
                g, k = seqlist[t]
                w0 = max(256 * SMIN[g], base)
                width = base + 512 - w0
                sb_i = t % 2
                sbank = pb[sb_i]
                msS = mm(sbank[:, :width],
                         kf[:, 256 * g + 128 * k: 256 * g + 128 * (k + 1)],
                         qbuf[par, :, w0:base + 512], True, True,
                         waits=first_w + p.takewaits(("bank", sb_i)),
                         inc="pe")
                first_w = []
                rot = prot
                prot = (prot + 1) % 8
                pdst = pbufW[rot][:, 0:width]
                aE = p.em("act", lambda s, d=pdst, b=sbank, w=width: s.activation(
                    d, b[:, :w], AF.Exp, bias=nshift[:, 0:1], scale=SCALE2),
                    waits=[msS, m_nsh] + p.takewaits(("pbufP", rot)), inc="act")
                p.use(("bank", sb_i), aE)
                lastP = aE
                if SMIN[g] >= base // 256:
                    jm = g // 4
                    moff = ((SMIN[g] * 2 + jm) * 2 + k) * 256
                    lastP = p.em("dve", lambda v, d=pdst, mo=moff: v.tensor_mul(
                        d[:, 0:256], d[:, 0:256], mask_sb[:, mo:mo + 256]),
                        waits=[aE, d_mask], inc="dve")
                pmeta[t] = (rot, w0, lastP)

            def emit_AV(t):
                g, k = seqlist[t]
                skc = 2 * g + k
                rot, w0, lastP = pmeta.pop(t)
                n_sq = [i for i in range(4) if (4 * pairi + i) // 2 >= SMIN[g]]
                for ii, i in enumerate(n_sq):
                    col = 128 * i + base - w0
                    last_ch = (last_touch[i] == (g, k))
                    msA = mm(pb[OB[i]][:, :129],
                             pbufW[rot][:, col:col + 128],
                             vv[:, 129 * skc:129 * (skc + 1)],
                             start=first_av[i], stop=last_ch,
                             waits=[lastP] + oww[i],
                             inc="pe" if (ii == len(n_sq) - 1 or last_ch) else None)
                    oww[i] = []
                    first_av[i] = False
                    if ii == len(n_sq) - 1:
                        p.use(("pbufP", rot), msA)
                    if last_ch:
                        chunk_last[i] = msA

            chunk_last = {}
            for t in range(min(LEAD, nseq)):
                emit_S(t)
            oww = [p.takewaits(("bank", OB[i])) for i in range(4)]
            for t in range(nseq):
                if feeder:
                    feeder.feed(1)
                if t + LEAD < nseq:
                    emit_S(t + LEAD)
                if feeder:
                    feeder.feed(1)
                emit_AV(t)
            tail_q.append((h, pairi, chunk_last))
            flush_tail()
        endm = p.now("pe")
        p.use(("xkv", par), endm)
        for slot in range(2):
            p.use(("qbuf", par, slot), endm)
        p.use(("qbuf_t", par), endm)
        if PH7_SERIAL:
            p.barrier()

    fw_prev = QstdFeed(0).finish()
    for h in range(16):
        if h + 2 < 16:
            qstd_dma(h + 2)
        feeder = QstdFeed(h + 1) if h + 1 < 16 else None
        emit_attn2(h, fw_prev, feeder)
        fw_prev = feeder.finish() if feeder else None
    flush_tail()

    p.barrier()

    if "attn2" in dbg:
        dma("sync", dbg["attn2"], cdk[:, :], inc="ds")

    if stop_phase <= 7:
        return _finish(nc, p, sem, stck)

    PHASE_MARKS.append(("PH8", sum(1 for o in p.ops if o[0] == "pe")))
    bcm8 = []
    for slot in range(2):
        dA8 = dma("gp", abc[slot, 0, :, :],
                  rowbuf.ap()[2 * slot:2 * slot + 1, :].partition_broadcast(128),
                  waits=[lr1[slot][0]] + p.takewaits(("abcA", slot)),
                  inc="dgr%d" % (2 * slot))
        dB8 = dma("gp", abc[slot, 1, :, :],
                  rowbuf.ap()[2 * slot + 1:2 * slot + 2, :].partition_broadcast(128),
                  waits=[lr1[slot][1]] + p.takewaits(("abcB", slot)),
                  inc="dgr%d" % (2 * slot + 1))
        bcm8.append((dA8, dB8))
    sm2 = proj_resid_stats(
        wp_i, lambda ec, slot: cdk[:, SC * ec + 512 * slot: SC * ec + 512 * (slot + 1)],
        None,
        lambda jc, slot: zbuf.ap()[128 * jc:128 * (jc + 1),
                                   512 * slot:512 * (slot + 1)],
        x1buf.ap(),
        bft3=[zbft[0], zbft[1], kvs[1]],
        sqt3=[zsqt[0], zsqt[1], kvs[2]],
        xfix=(abc, bcm8))
    lr2, _, _ = ln_rows(4, sm2)
    out_ms = ln_normalize(4, lr2, out_t, x1buf.ap())

    return _finish(nc, p, sem, stck)


def _finish(nc, p, sem, stck):
    p.barrier(exclude=())
    p.em("sync", lambda e: e.nop())
    p.em("gp", lambda e: e.nop())
    p.em("pe", lambda e: e.nop())
    p.em("dve", lambda v: v.nop())
    p.em("act", lambda s: s.nop())

    with nc.Block() as block:
        def runner(engname):
            def _(eng):
                seen = {}
                for (e, fn, waits, inc, amt) in p.ops:
                    if e != engname:
                        continue
                    best = {}
                    for (s, v) in waits:
                        if v > 0 and v > seen.get(s, 0) and v > best.get(s, 0):
                            best[s] = v
                    for (s, v) in best.items():
                        eng.wait_ge(sem[s], v)
                        seen[s] = v
                    instr = fn(eng)
                    if inc is not None:
                        if inc.startswith("cc"):
                            instr.then_inc(sem[inc])
                        elif inc in ("pe", "dve", "act", "gpc"):
                            instr.then_inc(sem[inc], 1)
                        else:
                            instr.then_inc(sem[inc], 16)
            return _
        block.tensor(runner("pe"))
        block.vector(runner("dve"))
        block.scalar(runner("act"))
        block.gpsimd(runner("gp"))
        block.sync(runner("sync"))
    stck.close()
    return nc


def _host_inputs(inputs):
    f = np.float32
    wq = np.asarray(inputs["L0_attn_wq"], f)
    wk = np.asarray(inputs["L0_attn_wk"], f)
    wv = np.asarray(inputs["L0_attn_wv"], f)
    wo = np.asarray(inputs["L0_attn_wo"], f)
    Wa = np.asarray(inputs["L0_ffn_Wa"], f)
    Wp = np.asarray(inputs["L0_ffn_Wp"], f)
    def _img_bias(M, brow):
        pieces = []
        for stt, w in W16:
            blk = np.ascontiguousarray(M[:, stt:stt + w])
            im = blk.reshape(16, 128, w).transpose(1, 0, 2).reshape(128, 16 * w)
            bb = np.zeros((128, 128), np.float32)
            bb[0, :] = brow[stt:stt + w]
            pieces.append(np.concatenate([im, bb], axis=1))
        return np.ascontiguousarray(np.concatenate(pieces, axis=1))

    shared = dict(
        wq_i=_img_panels(np.ascontiguousarray(wq.T), W16),
        wk_i=_img_bias(np.ascontiguousarray(wk.T), np.asarray(inputs["L0_attn_bk"], f)),
        wv_i=_img_bias(np.ascontiguousarray(wv.T), np.asarray(inputs["L0_attn_bv"], f)),
        wo_i=_img_panels(np.ascontiguousarray(wo.T), W16),
        wp_i=_img_panels(Wp, W16),
        wa_i=_img_panels(Wa, WA_RANGES),
        bq_c=np.ascontiguousarray(np.asarray(inputs["L0_attn_bq"], f).reshape(16, 128).T),
        bo_c=np.ascontiguousarray(np.asarray(inputs["L0_attn_bo"], f).reshape(16, 128).T),
        bk_r=np.asarray(inputs["L0_attn_bk"], f).reshape(1, D),
        bv_r=np.asarray(inputs["L0_attn_bv"], f).reshape(1, D),
        wsr_i=np.concatenate([np.asarray(inputs["L0_ffn_w_std"], f),
                              np.asarray(inputs["L0_ffn_w_rec"], f)]).reshape(1, 32),
        r01_i=(np.arange(128)[None, :] // 8 == np.arange(16)[:, None]).astype(f),
        ident_i=np.eye(128, dtype=BF),
    )
    x = np.asarray(inputs["L0_attn_x"], f)
    per_core = []
    for c in range(8):
        b, parity = c // 2, c % 2
        xc = np.ascontiguousarray(x[b][_own_idx(parity)])
        xi = _img_panels(np.ascontiguousarray(xc.T), [(0, SC)])
        masks = _masks_for(parity).transpose(3, 0, 1, 2, 4).reshape(128, 16 * 256)
        per_core.append(dict(shared, x_img=xi, masks_i=np.ascontiguousarray(masks)))
    return per_core


def kernel(**inputs):
    per_core = _host_inputs(inputs)
    nc = build_program()
    res = run_bass_kernel_spmd(nc, per_core, core_ids=list(range(8)))
    x = np.asarray(inputs["L0_attn_x"])
    out = np.zeros((4, 2048, D), dtype=np.float32)
    for c in range(8):
        b, parity = c // 2, c % 2
        o = np.asarray(res.results[c]["out_fm"])
        out[b, _own_idx(parity)] = np.ascontiguousarray(o.T)
    return out


# version 23
# speedup vs baseline: 1.2146x; 1.0330x over previous
import contextlib
import numpy as np
import ml_dtypes

import concourse.bass as bass
import concourse.mybir as mybir
from concourse.bass_utils import run_bass_kernel_spmd

F32 = mybir.dt.float32
BF16 = mybir.dt.bfloat16
AF = mybir.ActivationFunctionType
ALU = mybir.AluOpType
BF = ml_dtypes.bfloat16

D = 2048
H = 16
NFC = 16
SC = 1024
CH = 256
OWN = [[0, 3, 4, 7], [1, 2, 5, 6]]
GLOB = OWN[0] + OWN[1]
LIVE = [sorted(g for g in range(8)
               if GLOB[g] <= OWN[0][s] or GLOB[g] <= OWN[1][s]) for s in range(4)]
EXP_SHIFT = 20.0
SCALE2 = float(1.0 / np.sqrt(np.float32(128.0)))
QSTD0, KSTD0, V0, LOW0 = 0, 1920, 3840, 5888
W16 = [(i * 128, 128) for i in range(16)]
WA_RANGES = ([(KSTD0 + 120 * i, 120) for i in range(16)] +
             [(LOW0, 256)] +
             [(V0 + 128 * i, 128) for i in range(16)] +
             [(QSTD0 + 120 * i, 120) for i in range(16)])
WA_OFF = np.cumsum([0] + [16 * w for _, w in WA_RANGES]).tolist()
GROUPS = [[0, 1], [2, 3], [4, 5], [6, 7]]


def _own_idx(parity):
    return np.concatenate([np.arange(CH) + c * CH for c in OWN[parity]])


def _img_panels(M, ranges):
    pieces = []
    for stt, w in ranges:
        blk = np.ascontiguousarray(M[:, stt:stt + w])
        pieces.append(blk.reshape(16, 128, w).transpose(1, 0, 2).reshape(128, 16 * w))
    return np.ascontiguousarray(np.concatenate(pieces, axis=1))


def _masks_for(parity):
    m = np.zeros((4, 2, 2, 128, 256), dtype=BF)
    for s in range(4):
        for j, g in enumerate((s, s + 4)):
            kg = GLOB[g] * CH + np.arange(CH)
            qg = OWN[parity][s] * CH + np.arange(CH)
            full = (kg[:, None] <= qg[None, :]).astype(np.float32)
            m[s, j, 0] = full[:128]
            m[s, j, 1] = full[128:]
    return m


class Views:

    def __init__(self, views):
        self.views = views

    def __getitem__(self, key):
        if isinstance(key, tuple):
            v = self.views[key[0]]
            rest = key[1:]
            if not rest:
                return v
            return v[rest if len(rest) > 1 else rest[0]]
        return self.views[key]


class Plan:
    def __init__(self):
        self.ops = []
        self.cnt = {}
        self.pending = {}
        self.last = {}

    def sem(self, n):
        self.cnt.setdefault(n, 0)

    def em(self, eng, fn, waits=(), inc=None, amt=1):
        w = list(self.pending.pop(eng, ())) + [x for x in waits if x is not None]
        self.ops.append((eng, fn, w, inc, amt))
        if inc is not None:
            self.cnt[inc] = self.cnt.get(inc, 0) + (amt if amt else 1)
            return (inc, self.cnt[inc])
        return None

    def now(self, s):
        return (s, self.cnt.get(s, 0))

    def barrier(self, exclude=("cc1", "cc2", "cc3", "cc4", "cc5", "cc6")):
        snap = [(s, c) for s, c in self.cnt.items() if c > 0 and s not in exclude]
        for e in ("pe", "dve", "act", "gp", "sync"):
            self.pending[e] = snap

    def use(self, key, *ms):
        cur = self.last.setdefault(key, {})
        for m in ms:
            if m is not None:
                s, v = m
                cur[s] = max(cur.get(s, 0), v)

    def takewaits(self, key):
        d = self.last.pop(key, {})
        return list(d.items())


PH1_LIMIT = 16
PH0_PATTERNS = True
PH1_BIAS = True
PH1_EVAC = True
PH7_SERIAL = False

PHASE_MARKS = []


def build_program(stop_phase=99, debug_taps=()):
    PHASE_MARKS.clear()
    nc = bass.Bass("TRN2", target_bir_lowering=False, debug=False)
    p = Plan()
    SEM_NAMES = ("pe", "dve", "act", "dg", "ds", "cc1", "cc2", "cc3",
                 "cc4", "cc5", "cc6",
                 "dgw0", "dgw1", "dgw2", "dgh0", "dgh1", "dgh2",
                 "dgi0", "dgi1", "dgi2", "dgj0", "dgj1", "dgj2", "dgb", "dkv",
                 "dgr0", "dgr1", "dgr2", "dgr3",
                 "dsx0", "dsx1", "dsx2", "dsx3", "dsz0", "dsz1", "dsz2", "dsr", "dsra",
                 "dst0", "dst1", "dsk0", "dsk1", "dckv", "dkf", "dvv",
                 "dgl", "dskt", "dsm", "dsc", "gpc")
    for n in SEM_NAMES:
        p.sem(n)
    WQ_SEMS = ("dgw%d", "dgh%d", "dgi%d", "dgj%d")

    x_img = nc.dram_tensor("x_img", [128, NFC * SC], F32, kind="ExternalInput").ap()
    wq_i = nc.dram_tensor("wq_i", [128, 8 * 4096], F32, kind="ExternalInput").ap()
    wk_i = nc.dram_tensor("wk_i", [128, 16 * 2176], F32, kind="ExternalInput").ap()
    wv_i = nc.dram_tensor("wv_i", [128, 16 * 2176], F32, kind="ExternalInput").ap()
    wo_i = nc.dram_tensor("wo_i", [128, 8 * 4096], F32, kind="ExternalInput").ap()
    wp_i = nc.dram_tensor("wp_i", [128, 8 * 4096], F32, kind="ExternalInput").ap()
    wa_i = nc.dram_tensor("wa_i", [128, WA_OFF[-1]], F32, kind="ExternalInput").ap()
    bq_c = nc.dram_tensor("bq_c", [128, 16], F32, kind="ExternalInput").ap()
    bo_c = nc.dram_tensor("bo_c", [128, 16], F32, kind="ExternalInput").ap()
    bk_r = nc.dram_tensor("bk_r", [1, D], F32, kind="ExternalInput").ap()
    bv_r = nc.dram_tensor("bv_r", [1, D], F32, kind="ExternalInput").ap()
    wsr_i = nc.dram_tensor("wsr_i", [1, 32], F32, kind="ExternalInput").ap()
    r01_i = nc.dram_tensor("r01_i", [16, 128], F32, kind="ExternalInput").ap()
    ident_i = nc.dram_tensor("ident_i", [128, 128], BF16, kind="ExternalInput").ap()
    masks_i = nc.dram_tensor("masks_i", [128, 16 * 256], BF16, kind="ExternalInput").ap()
    out_t = nc.dram_tensor("out_fm", [D, SC], F32, kind="ExternalOutput").ap()

    kv_ins = [nc.dram_tensor("kv_in%d" % k, [128, 8 * 129], F32) for k in range(2)]
    kv_outs = [nc.dram_tensor("kv_out%d" % k, [128, 8 * 129], F32) for k in range(2)]
    kf_in = nc.dram_tensor("kf_in", [H * 128, SC], BF16)
    kf_out = nc.dram_tensor("kf_out", [2 * H * 128, SC], BF16)
    vv_ins = [nc.dram_tensor("vv_in%d" % k, [8 * 128, 8 * 129], BF16)
              for k in range(2)]
    vv_outs = [nc.dram_tensor("vv_out%d" % k, [16 * 128, 8 * 129], BF16)
               for k in range(2)]
    zbuf = nc.dram_tensor("zbuf", [D, SC], F32)
    x1buf = nc.dram_tensor("x1buf", [D, SC], F32)
    rowbuf = nc.dram_tensor("rowbuf", [48, 512], F32)

    dbg = {}
    for name in debug_taps:
        shp = {"kv": [128, H * 129], "x1": [D, SC], "z": [D, SC],
               "kf": [2 * H * 128, SC], "vv": [16 * 128, H * 129],
               "attn2": [128, NFC * SC], "qf": [128, SC]}[name]
        dt = BF16 if name in ("kf", "vv", "attn2", "qf") else F32
        dbg[name] = nc.dram_tensor("dbg_" + name, shp, dt, kind="ExternalOutput").ap()

    stck = contextlib.ExitStack()
    sem = {n: stck.enter_context(nc.semaphore(n)) for n in SEM_NAMES}

    A = nc.alloc_sbuf_tensor

    def vsplit(ap, n, w):
        return Views([ap[:, i * w:(i + 1) * w] for i in range(n)])

    arena1 = A("arena1", [128, NFC * SC], BF16)
    qbuf_t = A("qbuf", [128, 2 * 2048], BF16)
    qbuf = vsplit(qbuf_t[:, :], 2, 2048)
    cdk = A("cdk", [128, H * SC], BF16)
    cdv = A("cdv", [128, 8 * H * 129], BF16)
    wbuf_t = A("wbuf", [128, 2 * 2176], BF16)
    wbuf = vsplit(wbuf_t[:, :], 2, 2176)
    xkv_t = A("xkv", [128, 2 * 4112], BF16)
    xkv = vsplit(xkv_t[:, :], 2, 4112)
    wpan = [wbuf[0], wbuf[1], xkv[0]]

    def wkey(si):
        return ("wbuf", si) if si < 2 else ("xkv", 0)
    kvbf_u = A("kvbf_u", [128, 2064], BF16)
    kvbf = kvbf_u[:, :]
    selq_fm = kvbf_u[:, 0:1024]
    selk_fm = kvbf_u[:, 1024:2048]
    kvs = Views([kvbf_u[:, i * 512:(i + 1) * 512] for i in range(4)])
    u1 = A("u1", [128, 2048], F32)
    trow = vsplit(u1[:, :], 2, 512)
    trow8 = vsplit(u1[:, :], 8, 256)
    x0t = vsplit(u1[:, :], 4, 512)
    mask_sb = u1[:, :].bitcast(BF16)
    u2 = A("u2", [128, 1536], F32)
    zt = vsplit(u2[:, :], 3, 512)
    pbufP = vsplit(u2[:, :].bitcast(BF16), 4, 256)
    u3 = A("u3", [128, 2048], F32)
    rbc = vsplit(u3[:, :], 4, 512)
    abc = Views([vsplit(u3[:, 0:1024], 2, 512), vsplit(u3[:, 1024:2048], 2, 512)])
    pbufW = vsplit(u3[:, :].bitcast(BF16), 8, 512)
    u4 = A("u4", [128, 1024], F32)
    u4b = u4[:, :].bitcast(BF16)
    zbft = vsplit(u4b, 2, 512)
    zsqt = Views([u4b[:, 1024 + i * 512:1024 + (i + 1) * 512] for i in range(2)])
    selq_tm = Views([u4b[:, i * 128:(i + 1) * 128] for i in range(8)])
    selk_tm = Views([u4b[:, 1024 + i * 128:1024 + (i + 1) * 128] for i in range(8)])
    u5 = A("u5", [128, 392], F32)
    kvst = vsplit(u5[:, 0:258], 2, 129)
    seltmp = vsplit(u5[:, 0:256], 2, 128)
    att_tm = Views([u5[:, 256 + i * 64:256 + (i + 1) * 64].bitcast(BF16) for i in range(2)])
    rec_t = Views([u5[:, 384 + i:385 + i] for i in range(2)])
    atm4 = Views([u5[:, 64 * i:64 * (i + 1)].bitcast(BF16) for i in range(4)])
    rec4 = Views([u5[:, 384 + i:385 + i] for i in range(4)])
    rows = A("rows", [1, 1024], F32)
    rows2 = A("rows2", [1, 1024], F32)
    patrow = rows2[0:1, 0:512]
    rows_bf = A("rows_bf", [1, 128], BF16)
    onesb_r = A("onesb_r", [1, 128], BF16)
    onesb_c = A("onesb_c", [128, 1], BF16)
    onesf_r = A("onesf_r", [1, 128], F32)
    eps_t = A("eps_t", [1, 1], F32)
    eps5_t = A("eps5_t", [1, 1], F32)
    nshift = A("nshift", [128, 1], F32)
    wrow = A("wrow", [1, 32], F32)
    wclip = A("wclip", [1, 32], F32)
    urow = A("urow", [1, 64], F32)
    patcol = A("patcol", [16, 4], F32)
    pat_t = A("pat", [128, 512], BF16)
    pat = vsplit(pat_t[:, :], 4, 128)
    wstd_c = A("wstd_c", [128, 16], F32)
    bqc_sb = A("bqc_sb", [128, 16], F32)
    boc_sb = A("boc_sb", [128, 16], F32)
    r01_sb = A("r01_sb", [16, 128], F32)
    ident = A("ident", [128, 128], BF16)

    pb = [nc.alloc_psum_tensor(f"pb{i}", [128, 512], F32) for i in range(8)]
    pb_bf = [b[:, :].bitcast(BF16) for b in pb]

    def mm(out_ap, lhsT, rhs, start, stop, waits=(), inc=None):
        return p.em("pe", lambda t: t.matmul(out_ap, lhsT, rhs, start=start, stop=stop),
                    waits=waits, inc=inc)

    def trans(out_ap, in_ap, waits=(), inc="pe"):
        return p.em("pe", lambda t: t.transpose(out_ap, in_ap, ident[:, :]),
                    waits=waits, inc=inc)

    def dma(q, out_ap, in_ap, waits=(), inc=None):
        return p.em(q, lambda e: e.dma_start(out=out_ap, in_=in_ap),
                    waits=waits, inc=inc, amt=16)

    def coll(kind, op, ins, outs, waits, inc):
        def fn(e):
            return e.collective_compute(kind, op, replica_groups=GROUPS, ins=ins, outs=outs)
        return p.em("gp", fn, waits=waits, inc=inc, amt=1)

    def xbf(fc, c0, w):
        return arena1[:, SC * fc + c0: SC * fc + c0 + w]

    dma("sync", ident[:, :], ident_i, inc="dsc")
    dma("sync", bqc_sb[:, :], bq_c, inc="dsc")
    dma("sync", boc_sb[:, :], bo_c, inc="dsc")
    dma("sync", wrow[:, :], wsr_i, inc="dsc")
    d_r01 = dma("sync", r01_sb[:, :], r01_i, inc="dsc")
    d_ident = d_bq = d_bo = d_wsr = d_r01 = p.now("dsc")
    d_x = dma("gp", arena1[:, :], x_img, inc="dg")

    p.em("dve", lambda v: v.memset(onesb_r[:, :], 1.0), inc="dve")
    p.em("dve", lambda v: v.memset(onesb_c[:, :], 1.0), inc="dve")
    p.em("dve", lambda v: v.memset(onesf_r[:, :], 1.0), inc="dve")
    m_eps = p.em("dve", lambda v: v.memset(eps_t[:, :], 1e-8), inc="dve")
    m_eps5 = p.em("dve", lambda v: v.memset(eps5_t[:, :], 1e-5), inc="dve")
    m_nsh = p.em("dve", lambda v: v.memset(nshift[:, :], -EXP_SHIFT), inc="dve")
    for sc in range(8):
        view = cdv[:, sc * (H * 129):(sc + 1) * (H * 129)].rearrange(
            "p (h c) -> p h c", h=H)[:, :, 128:129]
        p.em("dve", lambda v, vw=view: v.memset(vw, 1.0), inc="dve")
    mones = p.now("dve")

    mc = p.em("dve", lambda v: v.tensor_scalar_max(wclip[:, :], wrow[:, :], 1e-8),
              waits=[d_wsr], inc="dve")
    psub = p.em("dve", lambda v: v.tensor_scalar_sub(urow[0:1, 0:16], wrow[0:1, 16:32], 0.1),
                waits=[d_wsr], inc="dve")
    asg = p.em("act", lambda s: s.activation(urow[0:1, 0:16], urow[0:1, 0:16], AF.Sign,
                                             bias=0.0, scale=1.0),
               waits=[psub], inc="act")
    m_u = p.em("dve", lambda v: v.tensor_scalar_max(urow[0:1, 0:16], urow[0:1, 0:16], 0.0),
               waits=[asg], inc="dve")
    m_a = p.em("dve", lambda v: v.tensor_mul(urow[0:1, 16:32], urow[0:1, 0:16],
                                             wclip[0:1, 16:32]),
               waits=[m_u, mc], inc="dve")
    m_b = p.em("dve", lambda v: v.tensor_sub(urow[0:1, 32:48], wclip[0:1, 16:32],
                                             urow[0:1, 16:32]),
               waits=[m_a], inc="dve")
    m_n = p.em("dve", lambda v: v.tensor_scalar(urow[0:1, 48:64], urow[0:1, 0:16],
                                                -1.0, 1.0, ALU.mult, ALU.add),
               waits=[m_u], inc="dve")
    mrows = p.now("dve")

    srcrow = {0: 16, 1: 32, 2: 0, 3: 48}
    for i in range(4 if PH0_PATTERNS else 0):
        r_ = srcrow[i]
        m1 = mm(pb[0][0:16, 0:1], urow[0:1, r_:r_ + 16], onesf_r[0:1, 0:1], True, True,
                waits=[mrows, mones] + p.takewaits(("bank", 0)), inc="pe")
        c1 = p.em("dve", lambda v, i=i: v.tensor_copy(patcol[:, i:i + 1], pb[0][0:16, 0:1]),
                  waits=[m1], inc="dve")
        p.use(("bank", 0), c1)
        m2 = mm(pb[1][0:1, 0:128], patcol[:, i:i + 1], r01_sb[:, :], True, True,
                waits=[c1, d_r01] + p.takewaits(("bank", 1)), inc="pe")
        c2 = p.em("dve", lambda v, i=i: v.tensor_copy(patrow[0:1, i * 128:(i + 1) * 128],
                                                      pb[1][0:1, 0:128]),
                  waits=[m2], inc="dve")
        p.use(("bank", 1), c2)
        m3 = mm(pb[2][:, 0:128], onesf_r[0:1, :], patrow[0:1, i * 128:(i + 1) * 128],
                True, True, waits=[c2] + p.takewaits(("bank", 2)), inc="pe")
        c3 = p.em("dve", lambda v, i=i: v.tensor_copy(pat[i], pb[2][:, 0:128]),
                  waits=[m3], inc="dve")
        p.use(("bank", 2), c3)
    if PH0_PATTERNS:
        m4 = mm(pb[3][:, 0:16], onesf_r[0:1, :], wclip[0:1, 0:16], True, True,
                waits=[mc, mones] + p.takewaits(("bank", 3)), inc="pe")
        c4 = p.em("dve", lambda v: v.tensor_copy(wstd_c[:, :], pb[3][:, 0:16]),
                  waits=[m4], inc="dve")
        p.use(("bank", 3), c4)
    mpat = p.now("dve")

    p.barrier()

    if stop_phase <= 0:
        return _finish(nc, p, sem, stck)

    PHASE_MARKS.append(("PH1", sum(1 for o in p.ops if o[0] == "pe")))
    def tm_proj_b1(img_ap, bias_row, is_k):
        dws = [dma("gp", wbuf[0, :, :], img_ap[:, 0:2176],
                   waits=p.takewaits(("wbuf", 0)), inc="dgw0"),
               dma("gp", wbuf[1, :, :], img_ap[:, 2176:4352],
                   waits=p.takewaits(("wbuf", 1)), inc="dgw1")]
        for hidx in range(PH1_LIMIT):
            par = hidx % 2
            for wave in range(2):
                mss = []
                for i in range(4):
                    sc = wave * 4 + i
                    bidx = 4 * wave + i
                    bank = pb[bidx]
                    bw = [dws[hidx], d_x] + p.takewaits(("bank", bidx))
                    for fc in range(NFC):
                        mm(bank[:, :128], xbf(fc, sc * 128, 128),
                           wbuf[par, :, 128 * fc:128 * (fc + 1)],
                           start=(fc == 0), stop=False,
                           waits=bw if fc == 0 else [])
                        bw = []
                    ms = mm(bank[:, :128], onesb_r[0:1, :],
                            wbuf[par, 0:1, 2048:2176],
                            start=False, stop=True,
                            waits=[mones], inc="pe")
                    mss.append(ms)
                for i in range(4 if PH1_EVAC else 0):
                    sc = wave * 4 + i
                    bidx = 4 * wave + i
                    bank = pb[bidx]
                    if is_k:
                        pr = i % 4
                        t_t = trow8[2 * pr][:, :128]
                        t_r = trow8[2 * pr + 1][:, :128]
                        a1 = p.em("act", lambda s, b=bank, t=t_t: s.activation(
                            t, b[:, :128], AF.Exp, bias=0.0, scale=1.0),
                            waits=[mss[i]] + p.takewaits(("trow8", pr)), inc="act")
                        a2 = p.em("dve", lambda v, b=bank, t=t_r: v.tensor_scalar_max(
                            t, b[:, :128], 0.0),
                            waits=[mss[i], a1], inc="dve")
                        dst = cdk[:, 2048 * sc + 128 * hidx: 2048 * sc + 128 * (hidx + 1)]
                        a3 = p.em("dve", lambda v, d=dst, t1=t_t, t2=t_r:
                                  v.scalar_tensor_tensor(d, t1, 1.0, t2, ALU.min, ALU.add),
                            waits=[a1, a2], inc="dve")
                        p.use(("trow8", pr), a3)
                        p.use(("bank", bidx), a1, a2)
                        p.use(("cdk_w",), a3)
                    else:
                        dstv = cdv[:, sc * (H * 129) + 129 * hidx:
                                   sc * (H * 129) + 129 * hidx + 128]
                        a1 = p.em("act", lambda s, d=dstv, b=bank: s.copy(d, b[:, :128]),
                                  waits=[mss[i]], inc="act")
                        p.use(("bank", bidx), a1)
                        p.use(("cdv_w",), a1)
            p.use(("wbuf", par), p.now("pe"))
            if hidx + 2 < 16:
                par2 = (hidx + 2) % 2
                dws.append(dma("gp", wbuf[par2, :, :],
                               img_ap[:, (hidx + 2) * 2176:(hidx + 3) * 2176],
                               waits=p.takewaits(("wbuf", par2)),
                               inc="dgw%d" % par2))

    tm_proj_b1(wk_i, bk_r, True)
    if stop_phase <= 1:
        return _finish(nc, p, sem, stck)
    PHASE_MARKS.append(("PH2", sum(1 for o in p.ops if o[0] == "pe")))
    tm_proj_b1(wv_i, bv_r, False)
    if stop_phase <= 2:
        return _finish(nc, p, sem, stck)

    PHASE_MARKS.append(("PH3", sum(1 for o in p.ops if o[0] == "pe")))
    kq = p.takewaits(("cdk_w",)) + p.takewaits(("cdv_w",))
    kv_ev = []
    for h in range(H):
        bidx = h % 2
        bank = pb[bidx]
        bw = kq + p.takewaits(("bank", bidx))
        kq = []
        ms = None
        for sc in range(8):
            ms = mm(bank[:, :129],
                    cdk[:, 2048 * sc + 128 * h: 2048 * sc + 128 * (h + 1)],
                    cdv[:, sc * H * 129 + 129 * h: sc * H * 129 + 129 * (h + 1)],
                    start=(sc == 0), stop=(sc == 7),
                    waits=bw if sc == 0 else [], inc="pe" if sc == 7 else None)
            bw = []
        ev = p.em("act", lambda s, b=bank, h=h: s.copy(kvst[h % 2], b[:, :129]),
                  waits=[ms] + p.takewaits(("kvst", h % 2)), inc="act")
        p.use(("bank", bidx), ev)
        d_h = dma("sync", kv_ins[h // 8].ap()[:, 129 * (h % 8):129 * (h % 8) + 129],
                  kvst[h % 2], waits=[ev], inc="dckv")
        p.use(("kvst", h % 2), d_h)
        kv_ev.append(d_h)
        if h == 7:
            coll("AllReduce", ALU.add, [kv_ins[0].ap()], [kv_outs[0].ap()],
                 waits=list(kv_ev), inc="cc1")

    dwq = {}

    def qproj_dma(h):
        wsi = h % 3
        ww = p.takewaits(wkey(wsi))
        ms4 = []
        for q in range(4):
            ms4.append(dma("gp", wpan[wsi][:, 512 * q:512 * (q + 1)],
                           wq_i[:, h * 2048 + 512 * q:h * 2048 + 512 * (q + 1)],
                           waits=ww, inc=WQ_SEMS[q] % wsi))
            ww = []
        dwq[h] = ms4

    coll("AllReduce", ALU.add, [kv_ins[1].ap()], [kv_outs[1].ap()],
         waits=kv_ev[8:], inc="cc5")
    qproj_dma(0)
    qproj_dma(1)
    qproj_dma(2)

    if "kv" in dbg:
        dma("sync", dbg["kv"][:, 0:1032], kv_outs[0].ap(), waits=[("cc1", 1)], inc="ds")
        dma("sync", dbg["kv"][:, 1032:2064], kv_outs[1].ap(), waits=[("cc5", 1)], inc="ds")

    if stop_phase <= 3:
        return _finish(nc, p, sem, stck)

    PHASE_MARKS.append(("PH4", sum(1 for o in p.ops if o[0] == "pe")))
    nrow = 8

    def emit_qproj(h):
        par = h % 2
        wsi = h % 3
        dw = dwq[h]
        ev_q = {}
        mss = {}
        for slot in range(2):
            bidx = (2 * h + slot) % 4
            bank = pb[bidx]
            bw = [dw[0], d_x] + p.takewaits(("bank", bidx))
            ms = None
            for fc in range(NFC):
                if fc in (4, 8, 12):
                    bw = bw + [dw[fc // 4]]
                ms = mm(bank[:, :512],
                        wpan[wsi][:, 128 * fc:128 * (fc + 1)],
                        xbf(fc, slot * 512, 512),
                        start=(fc == 0), stop=(fc == NFC - 1),
                        waits=bw,
                        inc="pe" if fc == NFC - 1 else None)
                bw = []
            mss[slot] = ms
        p.use(wkey(wsi), p.now("pe"))
        for slot in range(2):
            bidx = (2 * h + slot) % 4
            bank = pb[bidx]
            a1 = p.em("act", lambda s, b=bank, h=h: s.activation(
                trow[0][:, :512], b[:, :512], AF.Exp,
                bias=bqc_sb[:, h:h + 1], scale=1.0),
                waits=[mss[slot], d_bq] + p.takewaits(("trow", 0)), inc="act")
            a2 = p.em("dve", lambda v, b=bank, h=h: v.tensor_scalar(
                trow[1][:, :512], b[:, :512], bqc_sb[:, h:h + 1], 0.0,
                ALU.add, ALU.max),
                waits=[mss[slot], d_bq, a1], inc="dve")
            dstq = qbuf[par, :, 512 * slot:512 * (slot + 1)]
            a3 = p.em("dve", lambda v, d=dstq: v.scalar_tensor_tensor(
                d, trow[0][:, :512], 1.0, trow[1][:, :512], ALU.min, ALU.add),
                waits=[a1, a2] + p.takewaits(("qbuf", par, slot)), inc="dve")
            p.use(("trow", 0), a3)
            p.use(("trow", 1), a3)
            p.use(("bank", bidx), a1, a2)
            ev_q[slot] = a3
        return ev_q

    R4 = [rows[0:1, 0:512], rows[0:1, 512:1024],
          rows2[0:1, 0:512], rows2[0:1, 512:1024]]
    pend_n = []

    def attn1_stage1(h, ev_q):
        par = h % 2
        for slot in range(2):
            i2 = 2 * h + slot
            ridx = i2 % 4
            qsl = qbuf[par, :, 512 * slot:512 * (slot + 1)]
            b_u = pb[4 + slot]
            msu = mm(b_u[:, :512], kvbf[:, 129 * h:129 * h + 128], qsl, True, True,
                     waits=[ev_q[slot], d_kvbf0 if h < 8 else d_kvbf1] +
                     p.takewaits(("bank", 4 + slot)),
                     inc="pe")
            msd = mm(pb[6][0:1, :512], kvbf[:, 129 * h + 128:129 * h + 129], qsl,
                     True, True, waits=p.takewaits(("bank", 6)), inc="pe")
            p.use(("qbuf", par, slot), msu, msd)
            a_d = p.em("act", lambda s, r=R4[ridx]: s.activation(
                r, pb[6][0:1, :512], AF.Identity,
                bias=eps_t[0:1, 0:1], scale=1.0),
                waits=[msd, m_eps] + p.takewaits(("rowsA", ridx)), inc="act")
            p.use(("bank", 6), a_d)
            r_d = p.em("dve", lambda v, r=R4[ridx]: v.reciprocal(r, r),
                       waits=[a_d], inc="dve")
            dsta = cdk[:, 1024 * h + 512 * slot: 1024 * h + 512 * (slot + 1)]
            a_u = p.em("act", lambda s, d=dsta, b=b_u: s.copy(d, b[:, :512]),
                       waits=[msu], inc="act")
            p.use(("bank", 4 + slot), a_u)
            pend_n.append((ridx, r_d, a_u, dsta))

    def attn1_stage2():
        while pend_n:
            ridx, r_d, a_u, dsta = pend_n.pop(0)
            bb = 7 - (ridx % 2)
            msb = mm(pb[bb][:, :512], onesf_r[0:1, :], R4[ridx], True, True,
                     waits=[r_d] + p.takewaits(("bank", bb)), inc="pe")
            t_m = p.em("dve", lambda v, d=dsta, bb=bb: v.tensor_mul(d, d, pb[bb][:, :512]),
                       waits=[a_u, msb], inc="dve")
            p.use(("bank", bb), t_m)
            p.use(("rowsA", ridx), msb)
            p.use(("attn1",), t_m)

    d_kvbf0 = dma("gp", kvbf[:, 0:1032], kv_outs[0].ap(), waits=[("cc1", 1)], inc="dkv")
    d_kvbf1 = dma("gp", kvbf[:, 1032:2064], kv_outs[1].ap(), waits=[("cc5", 1)],
                  inc="dkv")
    ev_prev = emit_qproj(0)
    for h in range(16):
        if h + 3 < 16:
            qproj_dma(h + 3)
        attn1_stage2()
        attn1_stage1(h, ev_prev)
        if h + 1 < 16:
            ev_prev = emit_qproj(h + 1)
    attn1_stage2()

    p.barrier()

    PHASE_MARKS.append(("PH5", sum(1 for o in p.ops if o[0] == "pe")))
    def proj_resid_stats(img_ap, rhs_fn, bias_col, xres_fn, zdst_ap,
                         bf_fn=None, bft3=None, sqt3=None, xfix=None):
        pend = []
        stat_started = [False, False]
        stat_sl = [[], []]

        def flush_stats(keep=0):
            while len(pend) > keep:
                jc, slot, zb_ms, zq_ms, zrot, s1rhs, trk = pend.pop(0)
                b_s, b_q = pb[4 + slot], pb[6 + slot]
                first = not stat_started[slot]
                w1 = [zb_ms] + (p.takewaits(("bank", 4 + slot)) if first else [])
                w2 = [zq_ms] + (p.takewaits(("bank", 6 + slot)) if first else [])
                s1 = mm(b_s[0:1, :512], onesb_c[:, 0:1], s1rhs,
                        start=first, stop=(jc == NFC - 1), waits=w1, inc="pe")
                s2 = mm(b_q[0:1, :512], onesb_c[:, 0:1], sqt3[zrot],
                        start=first, stop=(jc == NFC - 1), waits=w2, inc="pe")
                stat_started[slot] = True
                if trk:
                    p.use(("zbft8", zrot), s1)
                p.use(("zsqt8", zrot), s2)
                if jc == NFC - 1:
                    stat_sl[slot] = [s1, s2]

        def w_dma(jc):
            si2 = jc % 3
            ww = p.takewaits(wkey(si2))
            ms4 = []
            for q in range(4):
                ms4.append(dma("gp", wpan[si2][:, 512 * q:512 * (q + 1)],
                               img_ap[:, jc * 2048 + 512 * q:
                                      jc * 2048 + 512 * (q + 1)],
                               waits=ww, inc=WQ_SEMS[q] % si2))
                ww = []
            return ms4

        dws = [w_dma(0), w_dma(1), w_dma(2)]
        for jc in range(16):
            par = jc % 3
            for slot in range(2):
                idx = 2 * jc + slot
                bidx = slot
                bank = pb[bidx]
                xrot = idx % 4
                d_x0 = dma("sync", x0t[xrot, :, :], xres_fn(jc, slot),
                           waits=p.takewaits(("x0t", xrot)), inc="dsx%d" % xrot)
                if xfix is not None:
                    axc, bcm8 = xfix
                    c1 = p.em("dve", lambda v, xr=xrot, s=slot: v.tensor_mul(
                        x0t[xr, :, :], x0t[xr, :, :], axc[s, 0, :, :]),
                        waits=[d_x0, bcm8[slot][0]], inc="dve")
                    c2 = p.em("dve", lambda v, xr=xrot, s=slot: v.tensor_add(
                        x0t[xr, :, :], x0t[xr, :, :], axc[s, 1, :, :]),
                        waits=[c1, bcm8[slot][1]], inc="dve")
                    p.use(("abcA", slot), c1)
                    p.use(("abcB", slot), c2)
                    xr_ms = c2
                else:
                    xr_ms = d_x0
                bw = [dws[jc][0]] + p.takewaits(("bank", bidx))
                ms = None
                for ec in range(NFC):
                    if ec in (4, 8, 12):
                        bw = bw + [dws[jc][ec // 4]]
                    ms = mm(bank[:, :512],
                            wpan[par][:, 128 * ec:128 * (ec + 1)],
                            rhs_fn(ec, slot),
                            start=(ec == 0), stop=(ec == NFC - 1),
                            waits=bw,
                            inc="pe" if ec == NFC - 1 else None)
                    bw = []
                zr = idx % 3
                if bias_col is not None:
                    ev = p.em("dve", lambda v, b=bank, jc=jc, zr=zr, xr=xrot:
                              v.scalar_tensor_tensor(zt[zr][:, :512], b[:, :512],
                                                     bias_col[:, jc:jc + 1],
                                                     x0t[xr, :, :], ALU.add, ALU.add),
                              waits=[ms, xr_ms, d_bo] + p.takewaits(("zt", zr)),
                              inc="dve")
                else:
                    ev = p.em("dve", lambda v, b=bank, zr=zr, xr=xrot:
                              v.scalar_tensor_tensor(zt[zr][:, :512], b[:, :512],
                                                     1.0, x0t[xr, :, :],
                                                     ALU.mult, ALU.add),
                              waits=[ms, xr_ms] + p.takewaits(("zt", zr)), inc="dve")
                p.use(("x0t", xrot), ev)
                p.use(("bank", bidx), ev)
                d_z = dma("sync", zdst_ap[128 * jc:128 * (jc + 1),
                                          512 * slot:512 * (slot + 1)],
                          zt[zr][:, :512], waits=[ev], inc="dsz%d" % zr)
                brot = idx % 3
                if bf_fn is not None:
                    bfdst = bf_fn(jc, slot)
                    zb = p.em("dve", lambda v, zr=zr, d=bfdst:
                              v.tensor_copy(d, zt[zr][:, :512]),
                              waits=[ev], inc="dve")
                    s1rhs, trk = bfdst, False
                else:
                    zb = p.em("dve", lambda v, zr=zr, br=brot:
                              v.tensor_copy(bft3[br], zt[zr][:, :512]),
                              waits=[ev] + p.takewaits(("zbft8", brot)), inc="dve")
                    s1rhs, trk = bft3[brot], True
                zq = p.em("act", lambda s, zr=zr, br=brot:
                          s.activation(sqt3[br], zt[zr][:, :512],
                                       AF.Square, bias=0.0, scale=1.0),
                          waits=[zb] + p.takewaits(("zsqt8", brot)), inc="act")
                p.use(("zt", zr), d_z, zq)
                pend.append((jc, slot, zb, zq, brot, s1rhs, trk))
                flush_stats(keep=2)
            p.use(wkey(par), p.now("pe"))
            if jc + 3 < 16:
                dws.append(w_dma(jc + 3))
        flush_stats()
        return stat_sl

    def ln_rows(base, stat_sl, mkbf=False):
        outm = []
        en_bf = []
        ea_bf = []
        for slot in range(2):
            b_s, b_q = pb[4 + slot], pb[6 + slot]
            rw = p.takewaits(("rowsA",)) + p.takewaits(("rowsB",))
            a1 = p.em("act", lambda s, b=b_s: s.activation(
                rows[0:1, 0:512], b[0:1, :512], AF.Identity, bias=0.0,
                scale=-1.0 / 2048.0), waits=stat_sl[slot] + rw, inc="act")
            a2 = p.em("act", lambda s, b=b_q: s.activation(
                rows[0:1, 512:1024], b[0:1, :512], AF.Identity, bias=0.0,
                scale=1.0 / 2048.0), inc="act")
            p.use(("bank", 4 + slot), a1)
            p.use(("bank", 6 + slot), a2)
            if mkbf:
                mn = mm(pb[4 + slot][:, :512], onesf_r[0:1, :], rows[0:1, 0:512],
                        True, True, waits=[a1] + p.takewaits(("bank", 4 + slot)),
                        inc="pe")
                p.use(("rowsA",), mn)
                en = p.em("act", lambda s, sl=slot: s.copy(
                    zbft[sl][:, :512], pb[4 + sl][:, :512]),
                    waits=[mn] + p.takewaits(("zbft8", slot)), inc="act")
                p.use(("bank", 4 + slot), en)
                en_bf.append(en)
            d1 = p.em("dve", lambda v: v.tensor_mul(rows2[0:1, 0:512],
                                                    rows[0:1, 0:512], rows[0:1, 0:512]),
                      waits=[a1] + p.takewaits(("rows2",)), inc="dve")
            d2 = p.em("dve", lambda v: v.tensor_sub(rows[0:1, 512:1024],
                                                    rows[0:1, 512:1024], rows2[0:1, 0:512]),
                      waits=[a2, d1], inc="dve")
            a3 = p.em("act", lambda s: s.activation(rows[0:1, 512:1024],
                                                    rows[0:1, 512:1024], AF.Sqrt,
                                                    bias=eps5_t[0:1, 0:1], scale=1.0),
                      waits=[d2, m_eps5], inc="act")
            d3 = p.em("dve", lambda v: v.reciprocal(rows2[0:1, 0:512],
                                                    rows[0:1, 512:1024]),
                      waits=[a3], inc="dve")
            dA = dma("sync", rowbuf.ap()[base + 2 * slot:base + 2 * slot + 1, :],
                     rows2[0:1, 0:512], waits=[d3], inc="dsr")
            if mkbf:
                ma = mm(pb[6 + slot][:, :512], onesf_r[0:1, :], rows2[0:1, 0:512],
                        True, True, waits=[d3] + p.takewaits(("bank", 6 + slot)),
                        inc="pe")
                p.use(("rows2",), ma)
                ea = p.em("act", lambda s, sl=slot: s.copy(
                    zsqt[sl][:, :512], pb[6 + sl][:, :512]),
                    waits=[ma] + p.takewaits(("zsqt8", slot)), inc="act")
                p.use(("bank", 6 + slot), ea)
                ea_bf.append(ea)
            d4 = p.em("dve", lambda v: v.scalar_tensor_tensor(
                rows2[0:1, 512:1024], rows[0:1, 0:512], 1.0, rows2[0:1, 0:512],
                ALU.mult, ALU.mult), waits=[d3], inc="dve")
            dB = dma("sync", rowbuf.ap()[base + 2 * slot + 1:base + 2 * slot + 2, :],
                     rows2[0:1, 512:1024], waits=[d4], inc="dsra")
            p.use(("rowsA",), d1, d4)
            p.use(("rowsB",), a3)
            p.use(("rows2",), dA, dB, d4)
            outm.append((dA, dB))
        return outm, en_bf, ea_bf

    def ln_normalize(base, rowdma, xout_ap, zsrc_ap):
        bms = {}
        for slot in range(2):
            dA = dma("gp", abc[slot, 0, :, :],
                     rowbuf.ap()[base + 2 * slot:base + 2 * slot + 1, :]
                     .partition_broadcast(128),
                     waits=[rowdma[slot][0]] + p.takewaits(("abcA", slot)),
                     inc="dgr%d" % (2 * slot))
            dB = dma("gp", abc[slot, 1, :, :],
                     rowbuf.ap()[base + 2 * slot + 1:base + 2 * slot + 2, :]
                     .partition_broadcast(128),
                     waits=[rowdma[slot][1]] + p.takewaits(("abcB", slot)),
                     inc="dgr%d" % (2 * slot + 1))
            bms[slot] = (dA, dB)
        zts = [zt[0], zt[1], zt[2], x0t[0], x0t[1], x0t[2], x0t[3]]

        def zkey(zr):
            return ("zt", zr) if zr < 3 else ("x0t", zr - 3)

        def zsem(zr):
            return "dsz%d" % zr if zr < 3 else "dsx%d" % (zr - 3)

        lastms = []
        for jc in range(NFC):
            for slot in range(2):
                idx = 2 * jc + slot
                zr = idx % 7
                d_z = dma("sync", zts[zr][:, :512],
                          zsrc_ap[128 * jc:128 * (jc + 1), 512 * slot:512 * (slot + 1)],
                          waits=p.takewaits(zkey(zr)), inc=zsem(zr))
                eng = "dve" if (idx % 2 == 0) else "gp"
                esem = "dve" if eng == "dve" else "gpc"
                t1 = p.em(eng, lambda v, zr=zr, s=slot: v.tensor_mul(
                    zts[zr][:, :512], zts[zr][:, :512], abc[s, 0, :, :]),
                    waits=[d_z, bms[slot][0]], inc=esem)
                t2 = p.em(eng, lambda v, zr=zr, s=slot: v.tensor_add(
                    zts[zr][:, :512], zts[zr][:, :512], abc[s, 1, :, :]),
                    waits=[t1, bms[slot][1]], inc=esem)
                p.use(("abcA", slot), t1)
                p.use(("abcB", slot), t2)
                d_o = dma("act", xout_ap[128 * jc:128 * (jc + 1),
                                         512 * slot:512 * (slot + 1)],
                          zts[zr][:, :512], waits=[t2], inc="dsc")
                p.use(zkey(zr), d_o)
                lastms.append(d_o)
        return lastms

    def wa_dma(pi, si):
        width = WA_RANGES[pi][1]
        ww = p.takewaits(wkey(si))
        ms4 = []
        for q in range(4):
            ms4.append(dma("gp", wpan[si][:, 4 * q * width:4 * (q + 1) * width],
                           wa_i[:, WA_OFF[pi] + 4 * q * width:
                                WA_OFF[pi] + 4 * (q + 1) * width],
                           waits=ww, inc=WQ_SEMS[q] % si))
            ww = []
        return ms4

    sm1 = proj_resid_stats(
        wo_i, lambda ec, slot: cdk[:, 1024 * ec + 512 * slot: 1024 * ec + 512 * (slot + 1)],
        boc_sb,
        lambda jc, slot: x_img[:, 1024 * jc + 512 * slot: 1024 * jc + 512 * (slot + 1)],
        zbuf.ap(),
        bf_fn=lambda jc, slot: arena1[:, SC * jc + 512 * slot:
                                      SC * jc + 512 * (slot + 1)],
        sqt3=[zsqt[0], zsqt[1], kvs[0]])
    p.use(("cdk_r",), p.now("pe"))
    dws6 = {}

    def kstd_dma(h):
        dws6[h] = wa_dma(h, h % 3)

    kstd_dma(0)
    kstd_dma(1)
    kstd_dma(2)
    lowbuf = mask_sb
    dwl_w = (p.takewaits(("x0t", 0)) + p.takewaits(("x0t", 1)) +
             p.takewaits(("x0t", 2)) + p.takewaits(("x0t", 3)) +
             p.takewaits(("trow", 0)) + p.takewaits(("trow", 1)))
    dwl = dma("gp", lowbuf[:, 0:4096], wa_i[:, WA_OFF[16]:WA_OFF[17]],
              waits=dwl_w, inc="dgl")
    lr1, en_bf, ea_bf = ln_rows(0, sm1, mkbf=True)
    x1bf_ms = {}
    cc_slot = {0: [], 1: []}
    for jc in range(NFC):
        for slot in range(2):
            idx = 2 * jc + slot
            eng = "dve" if idx % 2 == 0 else "gp"
            esem = "dve" if eng == "dve" else "gpc"
            tile = arena1[:, SC * jc + 512 * slot: SC * jc + 512 * (slot + 1)]
            c1 = p.em(eng, lambda v, t=tile, s=slot: v.tensor_add(t, t, zbft[s]),
                      waits=[en_bf[slot]], inc=esem)
            c2 = p.em(eng, lambda v, t=tile, s=slot: v.tensor_mul(t, t, zsqt[s]),
                      waits=[c1, ea_bf[slot]], inc=esem)
            x1bf_ms.setdefault(jc, []).append(c2)
            cc_slot[slot].append(c2)
    for slot in range(2):
        p.use(("zbft8", slot), *cc_slot[slot])
        p.use(("zsqt8", slot), *cc_slot[slot])

    if stop_phase <= 5:
        return _finish(nc, p, sem, stck)

    PHASE_MARKS.append(("PH6k", sum(1 for o in p.ops if o[0] == "pe")))
    for h in range(16):
        par = h % 3
        mss = {}
        for slot in range(2):
            bank = pb[slot]
            bw = [dws6[h][0]] + p.takewaits(("bank", slot))
            ms = None
            for fc in range(NFC):
                if fc in (4, 8, 12):
                    bw = bw + [dws6[h][fc // 4]]
                if h == 0:
                    bw = bw + x1bf_ms.get(fc, [])
                ms = mm(bank[0:120, :512],
                        wpan[par][:, 120 * fc:120 * (fc + 1)],
                        xbf(fc, slot * 512, 512),
                        start=(fc == 0), stop=(fc == NFC - 1),
                        waits=bw,
                        inc="pe" if fc == NFC - 1 else None)
                bw = []
            mss[slot] = ms
        p.use(wkey(par), p.now("pe"))
        if h + 3 < 16:
            kstd_dma(h + 3)
        evs = []
        for slot in range(2):
            bank = pb[slot]
            dst = cdk[0:120, 1024 * h + 512 * slot: 1024 * h + 512 * (slot + 1)]
            ev = p.em("act", lambda s, d=dst, b=bank: s.copy(d, b[0:120, :512]),
                      waits=[mss[slot]] + p.takewaits(("cdk_r",)), inc="act")
            p.use(("bank", slot), ev)
            p.use(("kstd",), ev)
            evs.append(ev)
        dma("sync", kf_in.ap()[128 * h:128 * h + 120, :],
            cdk[0:120, 1024 * h:1024 * (h + 1)], waits=evs, inc="dkf")

    PHASE_MARKS.append(("PH6l", sum(1 for o in p.ops if o[0] == "pe")))
    sel_ev = []
    for wave in range(2):
        mss = []
        for i in range(4):
            sc = 4 * wave + i
            bidx = 4 + i
            bank = pb[bidx]
            bw = [dwl] + p.takewaits(("bank", bidx))
            ms = None
            for fc in range(NFC):
                ms = mm(bank[:, :256], xbf(fc, sc * 128, 128),
                        lowbuf[:, 256 * fc:256 * (fc + 1)],
                        start=(fc == 0), stop=(fc == NFC - 1),
                        waits=bw if fc == 0 else [],
                        inc="pe" if fc == NFC - 1 else None)
                bw = []
            mss.append(ms)
        for i in range(4):
            sc = 4 * wave + i
            bidx = 4 + i
            bank = pb[bidx]
            e1 = p.em("dve", lambda v, b=bank: v.tensor_mul(seltmp[0], b[:, 128:256], pat[0]),
                      waits=[mss[i], mpat] + p.takewaits(("seltmp",)) +
                      p.takewaits(("zbft8", 0)) + p.takewaits(("zbft8", 1)) +
                      p.takewaits(("zsqt8", 0)) + p.takewaits(("zsqt8", 1)), inc="dve")
            e2 = p.em("dve", lambda v, b=bank: v.tensor_mul(seltmp[1], b[:, 0:128], pat[1]),
                      waits=[mss[i]], inc="dve")
            e3 = p.em("dve", lambda v, sc=sc: v.tensor_add(selq_tm[sc], seltmp[0], seltmp[1]),
                      waits=[e1, e2] + p.takewaits(("u4q", sc)), inc="dve")
            e4 = p.em("dve", lambda v, b=bank: v.tensor_mul(seltmp[0], b[:, 0:128], pat[2]),
                      waits=[e3], inc="dve")
            e5 = p.em("dve", lambda v, b=bank: v.tensor_mul(seltmp[1], b[:, 128:256], pat[3]),
                      waits=[e3], inc="dve")
            e6 = p.em("dve", lambda v, sc=sc: v.tensor_add(selk_tm[sc], seltmp[0], seltmp[1]),
                      waits=[e4, e5] + p.takewaits(("u4k", sc)), inc="dve")
            p.use(("seltmp",), e6)
            p.use(("bank", bidx), e4, e5)
            sel_ev.append((e3, e6))
    for sc in range(8):
        t1 = trans(pb_bf[7][:, 0:128], selq_tm[sc],
                   waits=[sel_ev[sc][0], d_ident] + p.takewaits(("bank", 7)))
        c1 = p.em("act", lambda s, sc=sc: s.copy(selq_fm[:, 128 * sc:128 * (sc + 1)],
                                                 pb_bf[7][:, 0:128]),
                  waits=[t1], inc="act")
        p.use(("bank", 7), c1)
        t2 = trans(pb_bf[7][:, 0:128], selk_tm[sc],
                   waits=[sel_ev[sc][1]] + p.takewaits(("bank", 7)))
        c2 = p.em("act", lambda s, sc=sc: s.copy(selk_fm[:, 128 * sc:128 * (sc + 1)],
                                                 pb_bf[7][:, 0:128]),
                  waits=[t2], inc="act")
        p.use(("bank", 7), c2)
        p.use(("selq_fm",), c1)
        p.use(("selk_fm",), c2)

    low_last_pe = p.now("pe")

    wkt = p.takewaits(("selk_fm",))
    ktail = []
    for h in range(H):
        d_t = dma("sync", kf_in.ap()[128 * h + 120:128 * (h + 1), :],
                  selk_fm[8 * h:8 * (h + 1), :],
                  waits=wkt if h == 0 else [], inc="dskt")
        wkt = []
        ktail.append(d_t)

    dws7 = {}

    def v_dma(h):
        dws7[h] = wa_dma(17 + h, h % 3)

    v_dma(0)
    v_dma(1)
    v_dma(2)
    coll("AllGather", ALU.bypass, [kf_in.ap()], [kf_out.ap()],
         waits=ktail + [p.now("dkf")], inc="cc2")

    PHASE_MARKS.append(("PH6v", sum(1 for o in p.ops if o[0] == "pe")))
    vstg = {}
    for h in range(16):
        par = h % 3
        for wave in range(2):
            mss = []
            for i in range(4):
                sc = 4 * wave + i
                bank = pb[i]
                bw = [dws7[h][0]] + p.takewaits(("bank", i))
                ms = None
                for fc in range(NFC):
                    if fc in (4, 8, 12):
                        bw = bw + [dws7[h][fc // 4]]
                    ms = mm(bank[:, :128], xbf(fc, sc * 128, 128),
                            wpan[par][:, 128 * fc:128 * (fc + 1)],
                            start=(fc == 0), stop=(fc == NFC - 1),
                            waits=bw,
                            inc="pe" if fc == NFC - 1 else None)
                    bw = []
                mss.append(ms)
            for i in range(4):
                sc = 4 * wave + i
                bank = pb[i]
                dstv = cdv[:, sc * (H * 129) + 129 * h: sc * (H * 129) + 129 * h + 128]
                ev = p.em("act", lambda s, d=dstv, b=bank: s.copy(d, b[:, :128]),
                          waits=[mss[i]], inc="act")
                p.use(("bank", i), ev)
                p.use(("cdv2", h // 8), ev)
        p.use(wkey(par), p.now("pe"))
        if h + 3 < 16:
            v_dma(h + 3)
        if h == 9:
            coll("AllGather", ALU.bypass, [vv_ins[0].ap()], [vv_outs[0].ap()],
                 waits=[vstg[0]], inc="cc3")
        if h % 8 == 7:
            k = h // 8
            vstg[k] = dma("sync",
                          vv_ins[k].ap().rearrange("(c p) s -> p c s", p=128),
                          cdv[:, :].rearrange("p (c hs) -> p c hs", c=8)
                          [:, :, 129 * 8 * k:129 * 8 * (k + 1)],
                          waits=p.takewaits(("cdv2", k)), inc="dvv")

    dwq7 = {}

    def qstd_dma(h):
        dwq7[h] = wa_dma(33 + h, h % 2)

    qstd_dma(0)
    qstd_dma(1)
    coll("AllGather", ALU.bypass, [vv_ins[1].ap()], [vv_outs[1].ap()],
         waits=[vstg[1]], inc="cc4")

    d_mask = dma("sync", mask_sb[:, 0:4096], masks_i,
                 waits=[low_last_pe], inc="dsm")

    if "kf" in dbg:
        dma("sync", dbg["kf"], kf_out.ap(), waits=[("cc2", 1)], inc="ds")

    if stop_phase <= 6:
        return _finish(nc, p, sem, stck)

    PHASE_MARKS.append(("PH7", sum(1 for o in p.ops if o[0] == "pe")))
    class QstdFeed:

        def __init__(self, h):
            self.h = h
            self.par = h % 2
            self.dw = dwq7[h]
            self.seq = [(slot, fc) for slot in range(2) for fc in range(NFC)]
            self.i = 0
            self.evq = {}

        def feed(self, n):
            while n > 0 and self.i < len(self.seq):
                slot, fc = self.seq[self.i]
                self.i += 1
                n -= 1
                bidx = 5 + slot
                bank = pb[bidx]
                bw = []
                if fc == 0:
                    bw = [self.dw[0]] + p.takewaits(("bank", bidx))
                if fc in (4, 8, 12):
                    bw = bw + [self.dw[fc // 4]]
                ms = mm(bank[0:120, :512],
                        wbuf[self.par, :, 120 * fc:120 * (fc + 1)],
                        xbf(fc, slot * 512, 512),
                        start=(fc == 0), stop=(fc == NFC - 1),
                        waits=bw,
                        inc="pe" if fc == NFC - 1 else None)
                if fc == NFC - 1:
                    dstq = qbuf[self.par, 0:120, 512 * slot:512 * (slot + 1)]
                    ev = p.em("act", lambda s, d=dstq, b=bank, hh=self.h:
                              s.activation(d, b[0:120, :512], AF.Identity,
                                           bias=0.0, scale=wstd_c[0:120, hh:hh + 1]),
                              waits=[ms] + p.takewaits(("qbuf", self.par, slot)),
                              inc="act")
                    p.use(("bank", bidx), ev)
                    self.evq[slot] = ev

        def finish(self):
            self.feed(len(self.seq))
            h, par = self.h, self.par
            p.use(("wbuf", par), p.now("pe"))
            d_t = dma("sync", qbuf[par, 120:128, 0:1024],
                      selq_fm[8 * h:8 * (h + 1), :],
                      waits=p.takewaits(("qbuf_t", par)), inc="dst%d" % par)
            wkv = p.takewaits(("xkv", par)) + ([("cc2", 1)] if h < 2 else [])
            l1 = dma("sync", xkv[par, :, 0:1024],
                     kf_out.ap()[128 * h:128 * (h + 1), :], waits=wkv, inc="dsk%d" % par)
            l2 = dma("sync", xkv[par, :, 1024:2048],
                     kf_out.ap()[H * 128 + 128 * h: H * 128 + 128 * (h + 1), :],
                     inc="dsk%d" % par)
            l3 = dma("sync", xkv[par, :, 2048:4112].rearrange("p (g c) -> p g c", g=16),
                     vv_outs[h // 8].ap().rearrange("(g p) c -> p g c", p=128)
                     [:, :, 129 * (h % 8):129 * (h % 8 + 1)],
                     waits=[("cc%d" % (3 + h // 8), 1)], inc="dsk%d" % par)
            return [self.evq[0], self.evq[1], d_t, l1, l2, l3]

    SMIN = [0, 1, 2, 3, 0, 1, 2, 3]
    PAIR_LIVE = [[g for g in range(8) if SMIN[g] <= 1],
                 list(range(8))]

    tail_q = []
    OB7 = [2, 3, 4, 7]

    def flush_tail():
        while tail_q:
            th, tpairi, tcl = tail_q.pop(0)
            for i in range(4):
                msA = tcl[i]
                sqg = 4 * tpairi + i
                sw = p.takewaits(("seltmp",))
                r1 = p.em("dve", lambda v, b=pb[OB7[i]], q=i: v.reciprocal(
                    rec4[q], b[:, 128:129]),
                    waits=[msA] + p.takewaits(("rec", i)), inc="dve")
                a1 = p.em("dve", lambda v, b=pb[OB7[i]], q=i: v.tensor_scalar_mul(
                    atm4[q], b[:, 0:128], rec4[q]),
                    waits=[r1] + p.takewaits(("att_tm", i)) + sw, inc="dve")
                p.use(("rec", i), a1)
                tb = OB7[i]
                t1 = trans(pb_bf[tb][:, 0:128], atm4[i],
                           waits=[a1, d_ident])
                p.use(("att_tm", i), t1)
                dsta = cdk[:, SC * th + 128 * sqg: SC * th + 128 * sqg + 128]
                cm = p.em("act", lambda s, d=dsta, tb=tb: s.copy(d, pb_bf[tb][:, 0:128]),
                          waits=[t1], inc="act")
                p.use(("bank", tb), cm)
                p.use(("attn2",), cm)

    def emit_attn2(h, first_w, feeder=None):
        par = h % 2
        if "qf" in dbg and h == 13:
            dma("sync", dbg["qf"], qbuf[par, :, 0:1024], waits=list(first_w), inc="ds")
        kf = xkv[par, :, 0:2048]
        vv = xkv[par, :, 2048:4112]
        first_w = list(first_w)
        OB = [2, 3, 4, 7]
        prot = 0
        for pairi in range(2):
            base = 512 * pairi
            live = PAIR_LIVE[pairi]
            first_av = [True] * 4
            seqlist = [(g, k) for g in live for k in range(2)]
            last_touch = {}
            for i in range(4):
                s_i = (4 * pairi + i) // 2
                last_touch[i] = [gk for gk in seqlist if SMIN[gk[0]] <= s_i][-1]
            LEAD = 6
            pmeta = {}
            nseq = len(seqlist)

            def emit_S(t):
                nonlocal first_w, prot
                g, k = seqlist[t]
                w0 = max(256 * SMIN[g], base)
                width = base + 512 - w0
                sb_i = t % 2
                sbank = pb[sb_i]
                msS = mm(sbank[:, :width],
                         kf[:, 256 * g + 128 * k: 256 * g + 128 * (k + 1)],
                         qbuf[par, :, w0:base + 512], True, True,
                         waits=first_w + p.takewaits(("bank", sb_i)),
                         inc="pe")
                first_w = []
                rot = prot
                prot = (prot + 1) % 8
                pdst = pbufW[rot][:, 0:width]
                aE = p.em("act", lambda s, d=pdst, b=sbank, w=width: s.activation(
                    d, b[:, :w], AF.Exp, bias=nshift[:, 0:1], scale=SCALE2),
                    waits=[msS, m_nsh] + p.takewaits(("pbufP", rot)), inc="act")
                p.use(("bank", sb_i), aE)
                lastP = aE
                if SMIN[g] >= base // 256:
                    jm = g // 4
                    moff = ((SMIN[g] * 2 + jm) * 2 + k) * 256
                    lastP = p.em("dve", lambda v, d=pdst, mo=moff: v.tensor_mul(
                        d[:, 0:256], d[:, 0:256], mask_sb[:, mo:mo + 256]),
                        waits=[aE, d_mask], inc="dve")
                pmeta[t] = (rot, w0, lastP)

            def emit_AV(t):
                g, k = seqlist[t]
                skc = 2 * g + k
                rot, w0, lastP = pmeta.pop(t)
                n_sq = [i for i in range(4) if (4 * pairi + i) // 2 >= SMIN[g]]
                for ii, i in enumerate(n_sq):
                    col = 128 * i + base - w0
                    last_ch = (last_touch[i] == (g, k))
                    msA = mm(pb[OB[i]][:, :129],
                             pbufW[rot][:, col:col + 128],
                             vv[:, 129 * skc:129 * (skc + 1)],
                             start=first_av[i], stop=last_ch,
                             waits=[lastP] + oww[i],
                             inc="pe" if (ii == len(n_sq) - 1 or last_ch) else None)
                    oww[i] = []
                    first_av[i] = False
                    if ii == len(n_sq) - 1:
                        p.use(("pbufP", rot), msA)
                    if last_ch:
                        chunk_last[i] = msA

            chunk_last = {}
            for t in range(min(LEAD, nseq)):
                emit_S(t)
            oww = [p.takewaits(("bank", OB[i])) for i in range(4)]
            for t in range(nseq):
                if feeder:
                    feeder.feed(1)
                if t + LEAD < nseq:
                    emit_S(t + LEAD)
                if feeder:
                    feeder.feed(1)
                emit_AV(t)
            tail_q.append((h, pairi, chunk_last))
            flush_tail()
        endm = p.now("pe")
        p.use(("xkv", par), endm)
        for slot in range(2):
            p.use(("qbuf", par, slot), endm)
        p.use(("qbuf_t", par), endm)
        if PH7_SERIAL:
            p.barrier()

    fw_prev = QstdFeed(0).finish()
    for h in range(16):
        if h + 2 < 16:
            qstd_dma(h + 2)
        feeder = QstdFeed(h + 1) if h + 1 < 16 else None
        emit_attn2(h, fw_prev, feeder)
        fw_prev = feeder.finish() if feeder else None
    flush_tail()

    p.barrier()

    if "attn2" in dbg:
        dma("sync", dbg["attn2"], cdk[:, :], inc="ds")

    if stop_phase <= 7:
        return _finish(nc, p, sem, stck)

    PHASE_MARKS.append(("PH8", sum(1 for o in p.ops if o[0] == "pe")))
    bcm8 = []
    for slot in range(2):
        dA8 = dma("gp", abc[slot, 0, :, :],
                  rowbuf.ap()[2 * slot:2 * slot + 1, :].partition_broadcast(128),
                  waits=[lr1[slot][0]] + p.takewaits(("abcA", slot)),
                  inc="dgr%d" % (2 * slot))
        dB8 = dma("gp", abc[slot, 1, :, :],
                  rowbuf.ap()[2 * slot + 1:2 * slot + 2, :].partition_broadcast(128),
                  waits=[lr1[slot][1]] + p.takewaits(("abcB", slot)),
                  inc="dgr%d" % (2 * slot + 1))
        bcm8.append((dA8, dB8))
    sm2 = proj_resid_stats(
        wp_i, lambda ec, slot: cdk[:, SC * ec + 512 * slot: SC * ec + 512 * (slot + 1)],
        None,
        lambda jc, slot: zbuf.ap()[128 * jc:128 * (jc + 1),
                                   512 * slot:512 * (slot + 1)],
        x1buf.ap(),
        bft3=[zbft[0], zbft[1], kvs[1]],
        sqt3=[zsqt[0], zsqt[1], kvs[2]],
        xfix=(abc, bcm8))
    lr2, _, _ = ln_rows(4, sm2)
    out_ms = ln_normalize(4, lr2, out_t, x1buf.ap())

    return _finish(nc, p, sem, stck)


def _finish(nc, p, sem, stck):
    p.barrier(exclude=())
    p.em("sync", lambda e: e.nop())
    p.em("gp", lambda e: e.nop())
    p.em("pe", lambda e: e.nop())
    p.em("dve", lambda v: v.nop())
    p.em("act", lambda s: s.nop())

    with nc.Block() as block:
        def runner(engname):
            def _(eng):
                seen = {}
                for (e, fn, waits, inc, amt) in p.ops:
                    if e != engname:
                        continue
                    best = {}
                    for (s, v) in waits:
                        if v > 0 and v > seen.get(s, 0) and v > best.get(s, 0):
                            best[s] = v
                    for (s, v) in best.items():
                        eng.wait_ge(sem[s], v)
                        seen[s] = v
                    instr = fn(eng)
                    if inc is not None:
                        if inc.startswith("cc"):
                            instr.then_inc(sem[inc])
                        elif inc in ("pe", "dve", "act", "gpc"):
                            instr.then_inc(sem[inc], 1)
                        else:
                            instr.then_inc(sem[inc], 16)
            return _
        block.tensor(runner("pe"))
        block.vector(runner("dve"))
        block.scalar(runner("act"))
        block.gpsimd(runner("gp"))
        block.sync(runner("sync"))
    stck.close()
    return nc


def _host_inputs(inputs):
    f = np.float32
    wq = np.asarray(inputs["L0_attn_wq"], f)
    wk = np.asarray(inputs["L0_attn_wk"], f)
    wv = np.asarray(inputs["L0_attn_wv"], f)
    wo = np.asarray(inputs["L0_attn_wo"], f)
    Wa = np.asarray(inputs["L0_ffn_Wa"], f)
    Wp = np.asarray(inputs["L0_ffn_Wp"], f)
    def _img_bias(M, brow):
        pieces = []
        for stt, w in W16:
            blk = np.ascontiguousarray(M[:, stt:stt + w])
            im = blk.reshape(16, 128, w).transpose(1, 0, 2).reshape(128, 16 * w)
            bb = np.zeros((128, 128), np.float32)
            bb[0, :] = brow[stt:stt + w]
            pieces.append(np.concatenate([im, bb], axis=1))
        return np.ascontiguousarray(np.concatenate(pieces, axis=1))

    shared = dict(
        wq_i=_img_panels(np.ascontiguousarray(wq.T), W16),
        wk_i=_img_bias(np.ascontiguousarray(wk.T), np.asarray(inputs["L0_attn_bk"], f)),
        wv_i=_img_bias(np.ascontiguousarray(wv.T), np.asarray(inputs["L0_attn_bv"], f)),
        wo_i=_img_panels(np.ascontiguousarray(wo.T), W16),
        wp_i=_img_panels(Wp, W16),
        wa_i=_img_panels(Wa, WA_RANGES),
        bq_c=np.ascontiguousarray(np.asarray(inputs["L0_attn_bq"], f).reshape(16, 128).T),
        bo_c=np.ascontiguousarray(np.asarray(inputs["L0_attn_bo"], f).reshape(16, 128).T),
        bk_r=np.asarray(inputs["L0_attn_bk"], f).reshape(1, D),
        bv_r=np.asarray(inputs["L0_attn_bv"], f).reshape(1, D),
        wsr_i=np.concatenate([np.asarray(inputs["L0_ffn_w_std"], f),
                              np.asarray(inputs["L0_ffn_w_rec"], f)]).reshape(1, 32),
        r01_i=(np.arange(128)[None, :] // 8 == np.arange(16)[:, None]).astype(f),
        ident_i=np.eye(128, dtype=BF),
    )
    x = np.asarray(inputs["L0_attn_x"], f)
    per_core = []
    for c in range(8):
        b, parity = c // 2, c % 2
        xc = np.ascontiguousarray(x[b][_own_idx(parity)])
        xi = _img_panels(np.ascontiguousarray(xc.T), [(0, SC)])
        masks = _masks_for(parity).transpose(3, 0, 1, 2, 4).reshape(128, 16 * 256)
        per_core.append(dict(shared, x_img=xi, masks_i=np.ascontiguousarray(masks)))
    return per_core


def kernel(**inputs):
    per_core = _host_inputs(inputs)
    nc = build_program()
    res = run_bass_kernel_spmd(nc, per_core, core_ids=list(range(8)))
    x = np.asarray(inputs["L0_attn_x"])
    out = np.zeros((4, 2048, D), dtype=np.float32)
    for c in range(8):
        b, parity = c // 2, c % 2
        o = np.asarray(res.results[c]["out_fm"])
        out[b, _own_idx(parity)] = np.ascontiguousarray(o.T)
    return out
